# version 06
# baseline (speedup 1.0000x reference)
import numpy as np
import concourse.bass as bass
import concourse.mybir as mybir
from concourse.bass_utils import run_bass_kernel_spmd

F32 = mybir.dt.float32
BF16 = mybir.dt.bfloat16
F8 = mybir.dt.float8e4
I32 = mybir.dt.int32
ALU = mybir.AluOpType
AX = mybir.AxisListType
AF = mybir.ActivationFunctionType

D = 2048
II = 8192
TC = 2048
NT = 16
ND = 16
NI512 = 16
NIT = 64
H, HKV, HD = 16, 4, 128
NBLK = 512
THETA = 10000.0
EPS = 1e-5
ISC = float(1.0 / np.sqrt(128.0))

C250 = 250 << 23
CGA_X = 4 << 23
CGA_I = -(2 << 23)
CLIP_A = 0.119140625
C255 = 255 << 23
CW_BACK = 5 << 23
WCLIP = 6.9921875 / 2.0

PH = {"ffn1": True, "attn": True, "ffn2": True}
DBG = {"dump": False}
_LAST_RES = [None]

SEMS = ["sWL", "sWA", "sXL", "sPL", "sV", "sS", "sT", "sG", "sO", "sC", "sB", "sQ"]


class Env:
    def __init__(self):
        self.handles = None
        self.tags = {}
        self.mode = "dry"
        self.engine = None
        self.eng = None
        self.counts = {}
        self.barn = 0

    def start(self, mode, engine=None, eng=None):
        self.mode = mode
        self.engine = engine
        self.eng = eng
        self.counts = {s: 0 for s in SEMS}
        self.barn = 0

    def op(self, engine, fn, inc=None, tag=None):
        inst = None
        if self.mode == "emit" and engine == self.engine:
            inst = fn(self.eng)
        if inc is not None:
            sem, k = inc
            self.counts[sem] += k
            if tag is not None:
                self.tags[tag] = (sem, self.counts[sem])
            if inst is not None:
                inst.then_inc(self.handles[sem], k)
        return inst

    def wait(self, engine, tag):
        if tag not in self.tags:
            raise KeyError(f"wait on unset tag {tag!r}")
        sem, v = self.tags[tag]
        if v <= 0:
            return
        if self.mode == "emit" and engine == self.engine:
            self.eng.wait_ge(self.handles[sem], v)

    def bar(self, engine, fn):
        self.barn += 1
        t = ("bar", self.barn)
        self.op(engine, fn, inc=("sB", 1), tag=t)
        self.wait(engine, t)

    def settag(self, tag, other):
        self.tags[tag] = self.tags[other]

    def zerotag(self, tag):
        self.tags[tag] = ("sB", 0)


def build_program():
    nc = bass.Bass("TRN2", target_bir_lowering=False, debug=False,
                   detect_race_conditions=False)

    x_in = nc.dram_tensor("x", [TC, D], F32, kind="ExternalInput")
    wq_h = nc.dram_tensor("wq", [D, D], F32, kind="ExternalInput")
    wk_h = nc.dram_tensor("wk", [D, 512], F32, kind="ExternalInput")
    wv_h = nc.dram_tensor("wv", [D, 512], F32, kind="ExternalInput")
    wo_h = nc.dram_tensor("wo", [D, D], F32, kind="ExternalInput")
    wiq_h = nc.dram_tensor("wiq", [D, HD], F32, kind="ExternalInput")
    wik_h = nc.dram_tensor("wik", [D, HD], F32, kind="ExternalInput")
    cw_h = nc.dram_tensor("cw", [2, HD], F32, kind="ExternalInput")
    wg_h = nc.dram_tensor("wg_shard", [II // 8, D], F32, kind="ExternalInput")
    wu_h = nc.dram_tensor("wu_shard", [II // 8, D], F32, kind="ExternalInput")
    wd_h = nc.dram_tensor("wd_shard", [D, II // 8], F32, kind="ExternalInput")
    slot_h = nc.dram_tensor("slot", [1, 4], F32, kind="ExternalInput")
    out_h = nc.dram_tensor("out", [TC, D], F32, kind="ExternalOutput")

    _ko = {"kind": "ExternalOutput"} if DBG["dump"] else {}
    x_mid = nc.dram_tensor("x_mid", [TC, D], F32, **_ko)
    x_att = nc.dram_tensor("x_att", [TC, D], F32, **_ko)
    interq_s = nc.dram_tensor("interq_s", [NT, II, 128], BF16, **_ko)
    qg_in = nc.dram_tensor("qg_in", [2, ND, 128, 512], F8)
    qu_in = nc.dram_tensor("qu_in", [2, ND, 128, 512], F8)
    qd_in = nc.dram_tensor("qd_in", [8, 128, D], F8)
    qg_all = nc.dram_tensor("qg_all", [NI512, ND, 128, 512], F8, addr_space="Shared")
    qu_all = nc.dram_tensor("qu_all", [NI512, ND, 128, 512], F8, addr_space="Shared")
    qd_all = nc.dram_tensor("qd_all", [NIT, 128, D], F8, addr_space="Shared")
    ATO_CK = 0
    ATO_CV = 128 * 512
    ATO_QI = 2 * 128 * 512
    ATO_CS = 3 * 128 * 512
    AT_N = ATO_CS + D
    at_in = nc.dram_tensor("at_in", [AT_N], BF16)
    qdbg = (nc.dram_tensor("qdbg", [48 * 128, 1024], BF16, kind="ExternalOutput")
            if DBG.get("dump_w") else None)
    bounce_f = nc.dram_tensor("bounce_f", [1024], F32)
    bounce_b = nc.dram_tensor("bounce_b", [1024], BF16)
    at_all = nc.dram_tensor("at_all", [4, AT_N], BF16)

    env = Env()

    from contextlib import ExitStack
    with ExitStack() as st:
        ec = st.enter_context
        env.handles = {s: ec(nc.semaphore(s)) for s in SEMS}

        arenaB = ec(nc.sbuf_tensor("arenaB", [128, 16384], BF16))
        arenaF = ec(nc.sbuf_tensor("arenaF", [128, 10240], F32))
        stat = ec(nc.sbuf_tensor("stat", [128, 256], F32))
        statI = ec(nc.sbuf_tensor("statI", [128, 128], I32))
        small = ec(nc.sbuf_tensor("small", [128, 512], F32))
        ident = ec(nc.sbuf_tensor("ident", [128, 128], BF16))
        ones1 = ec(nc.sbuf_tensor("ones1", [128, 1024], BF16))
        indt = ec(nc.sbuf_tensor("indt", [128, 8], BF16))
        with nc.sbuf_tensor("qxT", [128, 32768], F8) as qxT, \
             nc.sbuf_tensor("arena8", [128, 40960], F8) as arena8:
            pass
        with nc.sbuf_tensor("qwd_res", [128, 131072], F8) as qwd_res:
            pass
        with nc.sbuf_tensor("ckv", [128, 4096], BF16) as ckv, \
             nc.sbuf_tensor("tabc", [128, 2048], F32) as tabc, \
             nc.sbuf_tensor("tabs", [128, 2048], F32) as tabs, \
             nc.sbuf_tensor("kvbuf", [128, 16384], BF16) as kvbuf, \
             nc.sbuf_tensor("wstream", [128, 16384], BF16) as wstream:
            pass
        pb = [ec(nc.psum_tensor(f"pb{i}", [128, 512], F32)) for i in range(6)]
        pt0 = ec(nc.psum_tensor("pt0", [128, 1024], BF16))
        pt1 = ec(nc.psum_tensor("pt1", [128, 1024], BF16))
        DBG["h"] = dict(arenaB=arenaB, arenaF=arenaF, stat=stat, statI=statI,
                        small=small, ident=ident, ones1=ones1, indt=indt,
                        qxT=qxT, arena8=arena8, qwd_res=qwd_res, ckv=ckv,
                        tabc=tabc, tabs=tabs, kvbuf=kvbuf, wstream=wstream)

        def ap(h, off, pattern):
            return bass.AP(h, off, pattern)

        def a8(off, *dims):
            return ap(arena8, off, [[40960, 128]] + list(dims))

        def aB(off, *dims):
            return ap(arenaB, off, [[16384, 128]] + list(dims))

        def aF(off, *dims):
            return ap(arenaF, off, [[10240, 128]] + list(dims))

        def kv(off, *dims):
            return ap(kvbuf, off, [[16384, 128]] + list(dims))

        def ws(off, *dims):
            return ap(wstream, off, [[16384, 128]] + list(dims))

        def program(E):
            op, wait, bar = E.op, E.wait, E.bar

            op("gpsimd", lambda e: e.memset(ident[:, :], 1.0))
            op("gpsimd", lambda e: e.affine_select(
                ident[:, :], ident[:, :], [[1, 128]], ALU.is_equal, 0.0,
                base=0, channel_multiplier=-1), inc=("sG", 1), tag="ident")
            op("gpsimd", lambda e: e.memset(ones1[:, :], 1.0), inc=("sG", 1),
               tag="ones")
            op("gpsimd", lambda e: e.iota(
                statI[:, 0:8], [[-16, 8]], channel_multiplier=1,
                allow_small_or_imprecise_dtypes=True),
               inc=("sG", 1), tag="indi")
            wait("vector", "indi")
            bar("vector", lambda e: e.tensor_copy(stat[:, 192:200], statI[:, 0:8]))
            bar("vector", lambda e: e.tensor_scalar(
                stat[:, 208:216], stat[:, 192:200], 0.0, 0.5, ALU.is_ge, ALU.mult))
            bar("vector", lambda e: e.scalar_tensor_tensor(
                stat[:, 192:200], stat[:, 192:200], 16.0, stat[:, 208:216],
                ALU.is_lt, ALU.mult))
            op("vector", lambda e: e.tensor_copy(indt[:, :], stat[:, 192:200]),
               inc=("sV", 1), tag="indt")

            phase_w(E)

            if PH["ffn1"]:
                gphase(E, "f1", x_in, None)
                dphase(E, "f1", x_in, x_mid, None, "ag_d", "f1")
            else:
                copy_dram(E, "cp1", x_in, x_mid, None)
            if PH["attn"]:
                aphase(E, "at")
                if not PH.get("stop_after_a"):
                    bphase(E, "at")
            else:
                copy_dram(E, "cp2", x_mid, x_att,
                          ("f1_dst{t}" if PH["ffn1"] else "cp1_s{t}"),
                          entry=("f1_dev15_3" if PH["ffn1"] else None))
            if PH.get("stop_after_a"):
                v = E.counts["sO"]
                if E.mode == "emit" and E.engine == "sync":
                    E.eng.wait_ge(E.handles["sO"], v)
                return
            if PH["ffn2"]:
                rdy = "at_xst{t}" if PH["attn"] else "cp2_s{t}"
                gphase(E, "f2", x_att, rdy)
                dphase(E, "f2", x_att, out_h, rdy, None, "f2")
            else:
                rdy = "at_xst{t}" if PH["attn"] else "cp2_s{t}"
                copy_dram(E, "cp3", x_att, out_h, rdy,
                          entry=("at_woev3_1_3" if PH["attn"] else "cp2_s15"))
            if DBG.get("dump_w"):
                dbg_readback(E)

            v = E.counts["sO"]
            if E.mode == "emit" and E.engine == "sync":
                E.eng.wait_ge(E.handles["sO"], v)

        def wq_chain(E, key, ncols, prev_tr=None):
            op, wait, bar = E.op, E.wait, E.bar
            nb = ncols // 32
            amax = stat[:, 0:nb]
            mt = statI[:, 0:nb]
            fasc = stat[:, 64:64 + nb]
            gasc = stat[:, 128:128 + nb]
            xs3 = aF(0, [32, nb], [1, 32])
            bar("vector", lambda e: e.tensor_reduce(
                ap(stat, 0, [[256, 128], [1, nb], [0, 1]]), xs3, AX.X,
                ALU.max, apply_absolute_value=True))
            bar("vector", lambda e: e.tensor_scalar(
                mt, amax.bitcast(I32), 0x7F800000, None, ALU.bitwise_and))
            bar("vector", lambda e: e.tensor_scalar(
                fasc.bitcast(I32), mt, -1, C255, ALU.mult, ALU.add))
            bar("vector", lambda e: e.tensor_scalar(
                gasc.bitcast(I32), mt, CW_BACK, None, ALU.add))
            fab = ap(stat, 64, [[256, 128], [1, nb], [0, 32]])
            gab = ap(stat, 128, [[256, 128], [1, nb], [0, 32]])
            t0f = aF(2048, [1, ncols])
            t03 = aF(2048, [32, nb], [1, 32])
            bar("vector", lambda e: e.tensor_tensor(t03, xs3, fab, ALU.mult))
            bar("vector", lambda e: e.tensor_scalar(
                t0f, t0f, WCLIP, -WCLIP, ALU.min, ALU.max))
            op("vector", lambda e: e.engine_nop(), inc=("sV", 1), tag=f"{key}_t0")
            wait("scalar", f"{key}_t0")
            aw = aF(4096, [1, ncols])
            op("scalar", lambda e: e.activation(aw, t0f, AF.Abs),
               inc=("sS", 1), tag=f"{key}_aw")
            wait("vector", f"{key}_aw")
            ge1 = aF(6144, [1, ncols])
            ge2 = aF(8192, [1, ncols])
            s2e = aF(4096, [1, ncols])
            bar("vector", lambda e: e.tensor_scalar(ge1, aw, 1.0, None, ALU.is_ge))
            bar("vector", lambda e: e.tensor_scalar(ge2, aw, 2.0, None, ALU.is_ge))
            bar("vector", lambda e: e.tensor_scalar(s2e, ge1, -0.5, 1.0, ALU.mult, ALU.add))
            bar("vector", lambda e: e.scalar_tensor_tensor(s2e, ge2, -0.25, s2e, ALU.mult, ALU.add))
            bar("vector", lambda e: e.tensor_scalar(s2e, s2e, 2.0 ** -7, None, ALU.mult))
            t8 = a8(0, [1, ncols])
            bar("vector", lambda e: e.tensor_tensor(t8, t0f, s2e, ALU.mult))
            bar("vector", lambda e: e.tensor_scalar(ge1, ge1, 1.0, 1.0, ALU.mult, ALU.add))
            bar("vector", lambda e: e.scalar_tensor_tensor(ge1, ge2, 2.0, ge1, ALU.mult, ALU.add))
            bar("vector", lambda e: e.tensor_scalar(ge1, ge1, 2.0 ** 7, None, ALU.mult))
            bar("vector", lambda e: e.tensor_tensor(t0f, t8, ge1, ALU.mult))
            qwb3 = aB(0, [32, nb], [1, 32])
            if prev_tr is not None:
                wait("vector", prev_tr)
            op("vector", lambda e: e.tensor_tensor(qwb3, t03, gab, ALU.mult),
               inc=("sV", 1), tag=f"{key}_qwb")

        def wq_transposes(E, key, ntiles, prev_store):
            op, wait = E.op, E.wait
            wait("tensor", f"{key}_qwb")
            for ft in range(ntiles):
                pdst = pt0 if ft % 2 == 0 else pt1
                if ft >= 2:
                    wait("tensor", f"{key}_ev{ft - 2}")
                op("tensor", lambda e, ft=ft, pdst=pdst: e.transpose(
                    pdst[:, 0:128], aB(ft * 128, [1, 128]), ident[:, :]),
                   inc=("sT", 1), tag=f"{key}_tr{ft}")
                wait("vector", f"{key}_tr{ft}")
                if ft == 0 and prev_store is not None:
                    wait("vector", prev_store)
                op("vector", lambda e, ft=ft, pdst=pdst: e.tensor_copy(
                    a8(4096 + ft * 128, [1, 128]), pdst[:, 0:128]),
                   inc=("sV", 1), tag=f"{key}_ev{ft}")

        def phase_w(E):
            op, wait = E.op, E.wait
            wait("tensor", "ident")
            E.zerotag("wfree_0")
            prev_free = "wfree_0"
            prev_store = None
            prev_tr = None
            for kind, src, dst in (("g", wg_h, qg_in), ("u", wu_h, qu_in)):
                for j in range(8):
                    key = f"w{kind}{j}"
                    wait("scalar", prev_free)
                    op("scalar", lambda e, src=src, j=j: e.dma_start(
                        out=aF(0, [1, D]),
                        in_=ap(src, j * 128 * D, [[D, 128], [1, D]])),
                       inc=("sXL", 16), tag=f"{key}_ld")
                    wait("vector", f"{key}_ld")
                    wq_chain(E, key, D, prev_tr)
                    prev_free = f"{key}_t0"
                    wq_transposes(E, key, ND, prev_store)
                    wait("gpsimd", f"{key}_ev{ND - 1}")
                    op("gpsimd", lambda e, j=j, dst=dst: e.dma_start(
                        out=ap(dst, (j // 4) * (ND * 128 * 512) + (j % 4) * 128,
                               [[512, 128], [128 * 512, ND], [1, 128]]),
                        in_=a8(4096, [128, ND], [1, 128])),
                       inc=("sO", 16), tag=f"{key}_st")
                    prev_store = f"{key}_st"
                    prev_tr = f"{key}_tr{ND - 1}"
            for j in range(16):
                key = f"wd{j}"
                wait("scalar", prev_free)
                op("scalar", lambda e, j=j: e.dma_start(
                    out=aF(0, [1, 1024]),
                    in_=ap(wd_h, j * 128 * 1024, [[1024, 128], [1, 1024]])),
                   inc=("sXL", 16), tag=f"{key}_ld")
                wait("vector", f"{key}_ld")
                wq_chain(E, key, 1024, prev_tr)
                prev_free = f"{key}_t0"
                wq_transposes(E, key, 8, prev_store)
                wait("gpsimd", f"{key}_ev7")
                op("gpsimd", lambda e, j=j: e.dma_start(
                    out=ap(qd_in, j * 128, [[D, 128], [128 * D, 8], [1, 128]]),
                    in_=a8(4096, [128, 8], [1, 128])),
                   inc=("sO", 16), tag=f"{key}_st")
                prev_store = f"{key}_st"
                prev_tr = f"{key}_tr7"
            for t in ("wg7_st", "wu7_st", "wd15_st"):
                wait("gpsimd", t)
            if PH.get("dly_pre_ag"):
                for _ in range(PH["dly_pre_ag"]):
                    op("gpsimd", lambda e: e.engine_nop())
            op("gpsimd", lambda e: e.collective_compute(
                "AllGather", ALU.bypass, replica_groups=[list(range(8))],
                ins=[qg_in.ap().opt()], outs=[qg_all.ap().opt()]),
               inc=("sC", 1), tag="ag_g")
            op("gpsimd", lambda e: e.collective_compute(
                "AllGather", ALU.bypass, replica_groups=[list(range(8))],
                ins=[qu_in.ap().opt()], outs=[qu_all.ap().opt()]),
               inc=("sC", 1), tag="ag_u")
            op("gpsimd", lambda e: e.collective_compute(
                "AllGather", ALU.bypass, replica_groups=[list(range(8))],
                ins=[qd_in.ap().opt()], outs=[qd_all.ap().opt()]),
               inc=("sC", 1), tag="ag_d")

        def ln_stats(E, key, xoff, save_col=None, prev_user=None):
            op, wait, bar = E.op, E.wait, E.bar
            if prev_user is not None:
                E.wait("vector", prev_user)
            xf = aF(xoff, [1, D])
            scr = aF(8192, [1, D])
            op("scalar", lambda e: e.activation(
                scr, xf, AF.Identity, accum_out=small[:, 2:3]), inc=("sS", 1))
            op("scalar", lambda e: e.activation(
                scr, xf, AF.Square, accum_out=small[:, 3:4]),
               inc=("sS", 1), tag=f"{key}_sums")
            wait("vector", f"{key}_sums")
            bar("vector", lambda e: e.tensor_scalar(
                small[:, 4:5], small[:, 2:3], 1.0 / D, None, ALU.mult))
            bar("vector", lambda e: e.tensor_tensor(
                small[:, 5:6], small[:, 4:5], small[:, 4:5], ALU.mult))
            bar("vector", lambda e: e.scalar_tensor_tensor(
                small[:, 6:7], small[:, 3:4], 1.0 / D, small[:, 5:6],
                ALU.mult, ALU.subtract))
            bar("vector", lambda e: e.tensor_scalar(
                small[:, 6:7], small[:, 6:7], EPS, None, ALU.add))
            op("vector", lambda e: e.engine_nop(), inc=("sV", 1), tag=f"{key}_var")
            wait("scalar", f"{key}_var")
            op("scalar", lambda e: e.activation(
                small[:, 7:8], small[:, 6:7], AF.Sqrt), inc=("sS", 1), tag=f"{key}_sd")
            wait("vector", f"{key}_sd")
            bar("vector", lambda e: e.reciprocal(small[:, 8:9], small[:, 7:8]))
            bar("vector", lambda e: e.tensor_tensor(
                small[:, 9:10], small[:, 8:9], small[:, 8:9], ALU.mult))
            bar("vector", lambda e: e.tensor_tensor(
                small[:, 9:10], small[:, 9:10], small[:, 6:7], ALU.mult))
            bar("vector", lambda e: e.tensor_scalar(
                small[:, 9:10], small[:, 9:10], -0.5, 1.5, ALU.mult, ALU.add))
            bar("vector", lambda e: e.tensor_tensor(
                small[:, 0:1], small[:, 8:9], small[:, 9:10], ALU.mult))
            bar("vector", lambda e: e.scalar_tensor_tensor(
                small[:, 1:2], small[:, 4:5], -1.0, small[:, 0:1],
                ALU.mult, ALU.mult))
            if save_col is not None:
                bar("vector", lambda e, c=save_col: e.tensor_copy(
                    small[:, 16 + c:17 + c], small[:, 0:1]))
                bar("vector", lambda e, c=save_col: e.tensor_copy(
                    small[:, 32 + c:33 + c], small[:, 1:2]))
            op("vector", lambda e: e.engine_nop(), inc=("sV", 1), tag=f"{key}_ab")

        def act_quant(E, tag, srcoff, n, dst_fn, cga, tqoff=7168):
            op, wait, bar = E.op, E.wait, E.bar
            nb = n // 32
            amax = stat[:, 192:192 + nb]
            mt = statI[:, 64:64 + nb]
            fasc = stat[:, 0:nb]
            gasc = stat[:, 64:64 + nb]
            src3 = aF(srcoff, [32, nb], [1, 32])
            bar("vector", lambda e: e.tensor_reduce(
                ap(stat, 192, [[256, 128], [1, nb], [0, 1]]), src3, AX.X,
                ALU.max, apply_absolute_value=True))
            bar("vector", lambda e: e.tensor_scalar(
                mt, amax.bitcast(I32), 0x7F800000, None, ALU.bitwise_and))
            bar("vector", lambda e: e.tensor_scalar(
                fasc.bitcast(I32), mt, -1, C250, ALU.mult, ALU.add))
            bar("vector", lambda e: e.tensor_scalar(
                gasc.bitcast(I32), mt, cga, None, ALU.add))
            fab = ap(stat, 0, [[256, 128], [1, nb], [0, 32]])
            gab = ap(stat, 64, [[256, 128], [1, nb], [0, 32]])
            tq3 = aF(tqoff, [32, nb], [1, 32])
            tqf = aF(tqoff, [1, n])
            bar("vector", lambda e: e.tensor_tensor(tq3, src3, fab, ALU.mult))
            t8 = a8(32768, [1, n])
            t83 = a8(32768, [32, nb], [1, 32])
            bar("vector", lambda e: e.tensor_scalar(
                t8, tqf, CLIP_A, -CLIP_A, ALU.min, ALU.max))
            op("vector", lambda e: e.tensor_tensor(dst_fn(), t83, gab, ALU.mult),
               inc=("sV", 1), tag=tag)

        def gphase(E, tag, src_h, rdy):
            op, wait, bar = E.op, E.wait, E.bar
            if tag == "f2":
                for eng in ("scalar", "vector", "tensor", "sync", "gpsimd"):
                    wait(eng, "at_done" if PH["attn"] else "cp2_s15")
            for t in range(NT):
                xoff = 0 if t % 2 == 0 else 2048
                if t >= 2:
                    wait("scalar", f"{tag}_pfree{t - 2}")
                if rdy is not None:
                    wait("scalar", rdy.format(t=t))
                if t == 0 and tag == "f1":
                    wait("scalar", "wd15_qwb")
                op("scalar", lambda e, t=t, xoff=xoff: e.dma_start(
                    out=aF(xoff, [1, D]),
                    in_=ap(src_h, t * 128 * D, [[D, 128], [1, D]])),
                   inc=("sXL", 16), tag=f"{tag}_xl{t}")
                wait("scalar", f"{tag}_xl{t}")
                wait("vector", f"{tag}_xl{t}")
                ln_stats(E, f"{tag}_ln{t}", xoff,
                         prev_user=(f"{tag}_xln{t - 1}" if t >= 1 else None))
                wait("scalar", f"{tag}_ln{t}_ab")
                if t >= 1:
                    wait("scalar", f"{tag}_qx{t - 1}")
                op("scalar", lambda e, xoff=xoff: e.activation(
                    aF(4096, [1, D]), aF(xoff, [1, D]),
                    AF.Identity, bias=small[:, 1:2], scale=small[:, 0:1]),
                   inc=("sS", 1), tag=f"{tag}_xln{t}")
                E.settag(f"{tag}_pfree{t}", f"{tag}_xln{t}")
                wait("vector", f"{tag}_xln{t}")
                if t >= 1:
                    wait("vector", f"{tag}_qxt{t - 1}_{ND - 1}")
                act_quant(E, f"{tag}_qx{t}", 4096, D,
                          lambda: aB(0, [32, 64], [1, 32]), CGA_X, tqoff=6144)
                wait("tensor", f"{tag}_qx{t}")
                for ft in range(ND):
                    pdst = pt0 if ft % 2 == 0 else pt1
                    if ft >= 2:
                        wait("tensor", f"{tag}_qxe{t}_{ft - 2}")
                    elif t >= 1:
                        wait("tensor", f"{tag}_qxe{t - 1}_{ND - 2 + ft}")
                    op("tensor", lambda e, ft=ft, pdst=pdst: e.transpose(
                        pdst[:, 0:128], aB(ft * 128, [1, 128]),
                        ident[:, :]), inc=("sT", 1), tag=f"{tag}_qxt{t}_{ft}")
                    wait("vector", f"{tag}_qxt{t}_{ft}")
                    op("vector", lambda e, t=t, ft=ft, pdst=pdst:
                       e.tensor_copy(
                           ap(qxT, ft * TC + t * 128, [[32768, 128], [1, 128]]),
                           pdst[:, 0:128]),
                       inc=("sV", 1), tag=f"{tag}_qxe{t}_{ft}")
            def iq_stage(gi):
                isub, t = gi // NT, gi % NT
                par = gi % 2
                ptr = pt0 if par == 0 else pt1
                wait("tensor", f"{tag}_qi{isub}_{t}")
                if gi >= 2:
                    pi = gi - 2
                    wait("tensor", f"{tag}_iqe{pi // NT}_{pi % NT}")
                else:
                    wait("tensor", f"{tag}_qxe{NT - 1}_{ND - 2 + par}")
                for it in range(4):
                    op("tensor", lambda e, it=it, ptr=ptr, par=par: e.transpose(
                        ptr[:, it * 128:(it + 1) * 128],
                        aB(14336 + par * 512 + it * 128, [1, 128]), ident[:, :]),
                       inc=(("sT", 1) if it == 3 else None),
                       tag=(f"{tag}_iqt{isub}_{t}" if it == 3 else None))
                wait("vector", f"{tag}_iqt{isub}_{t}")
                if gi >= 2:
                    pi = gi - 2
                    wait("vector", f"{tag}_iqs{pi // NT}_{pi % NT}")
                op("vector", lambda e, ptr=ptr, par=par: e.tensor_copy(
                    aB(15360 + par * 512, [1, 512]), ptr[:, 0:512]),
                   inc=("sV", 1), tag=f"{tag}_iqe{isub}_{t}")
                wait("gpsimd", f"{tag}_iqe{isub}_{t}")
                op("gpsimd", lambda e, isub=isub, t=t, par=par: e.dma_start(
                    out=ap(interq_s, t * II * 128 + isub * 512 * 128,
                           [[128, 128], [128 * 128, 4], [1, 128]]),
                    in_=aB(15360 + par * 512, [128, 4], [1, 128])),
                   inc=("sO", 16), tag=f"{tag}_iqs{isub}_{t}")

            for isub in range(NI512):
                parw = isub % 2
                goff = parw * 8192
                uoff = 16384 + parw * 8192
                if isub >= 2:
                    wait("sync", f"{tag}_wfree{isub - 2}")
                if isub == 0:
                    wait("sync", "ag_g")
                    wait("sync", "ag_u")
                    if PH.get("dly_post_ag"):
                        for _ in range(PH["dly_post_ag"]):
                            op("sync", lambda e: e.nop())
                    wait("sync", f"{tag}_qxe0_{ND - 1}")
                op("sync", lambda e, isub=isub, goff=goff: e.dma_start(
                    out=a8(goff, [1, 8192]),
                    in_=ap(qg_all, isub * ND * 128 * 512,
                           [[512, 128], [128 * 512, ND], [1, 512]])),
                   inc=("sWL", 16), tag=f"{tag}_wg{isub}")
                op("sync", lambda e, isub=isub, uoff=uoff: e.dma_start(
                    out=a8(uoff, [1, 8192]),
                    in_=ap(qu_all, isub * ND * 128 * 512,
                           [[512, 128], [128 * 512, ND], [1, 512]])),
                   inc=("sWL", 16), tag=f"{tag}_wu{isub}")
                for t in range(NT):
                    gi = isub * NT + t
                    par = gi % 2
                    b = (gi % 3) * 2
                    pg, pu = pb[b], pb[b + 1]
                    wait("tensor", f"{tag}_wg{isub}")
                    wait("tensor", f"{tag}_wu{isub}")
                    if isub == 0:
                        wait("tensor", f"{tag}_qxe{t}_{ND - 1}")
                    if gi >= 3:
                        pi = gi - 3
                        wait("tensor", f"{tag}_int{pi // NT}_{pi % NT}")
                    for ft in range(ND):
                        op("tensor", lambda e, t=t, ft=ft, pg=pg, goff=goff: e.matmul(
                            pg[:, :],
                            ap(qxT, ft * TC + t * 128, [[32768, 128], [1, 128]]),
                            a8(goff + ft * 512, [1, 512]),
                            start=(ft == 0), stop=(ft == ND - 1)),
                           inc=(("sT", 1) if ft == ND - 1 else None),
                           tag=(f"{tag}_mmg{isub}_{t}" if ft == ND - 1 else None))
                        op("tensor", lambda e, t=t, ft=ft, pu=pu, uoff=uoff: e.matmul(
                            pu[:, :],
                            ap(qxT, ft * TC + t * 128, [[32768, 128], [1, 128]]),
                            a8(uoff + ft * 512, [1, 512]),
                            start=(ft == 0), stop=(ft == ND - 1)),
                           inc=(("sT", 1) if ft == ND - 1 else None),
                           tag=(f"{tag}_mmu{isub}_{t}" if ft == ND - 1 else None))
                    if t == NT - 1:
                        E.settag(f"{tag}_wfree{isub}", f"{tag}_mmu{isub}_{t}")
                    wait("scalar", f"{tag}_mmg{isub}_{t}")
                    if gi >= 2:
                        pi = gi - 2
                        wait("scalar", f"{tag}_int{pi // NT}_{pi % NT}")
                    op("scalar", lambda e, pg=pg, par=par: e.activation(
                        aF(6144 + par * 512, [1, 512]), pg[:, :], AF.Silu,
                        scale=2.0 ** -6),
                       inc=("sS", 1), tag=f"{tag}_silu{isub}_{t}")
                    wait("vector", f"{tag}_silu{isub}_{t}")
                    wait("vector", f"{tag}_mmu{isub}_{t}")
                    bar("vector", lambda e, pu=pu, par=par: e.tensor_tensor(
                        aF(7168 + par * 512, [1, 512]), aF(6144 + par * 512, [1, 512]),
                        pu[:, :], ALU.mult))
                    op("vector", lambda e: e.engine_nop(), inc=("sV", 1),
                       tag=f"{tag}_int{isub}_{t}")
                    if gi >= 2:
                        pi = gi - 2
                        wait("vector", f"{tag}_iqt{pi // NT}_{pi % NT}")
                    act_quant(E, f"{tag}_qi{isub}_{t}", 7168 + par * 512, 512,
                              lambda par=par: aB(14336 + par * 512, [32, 16], [1, 32]),
                              CGA_I, tqoff=8192)
                    if gi >= 1:
                        iq_stage(gi - 1)
            iq_stage(NI512 * NT - 1)

        def dphase(E, tag, src_h, dst_h, rdy, agtag, gtag):
            op, wait, bar = E.op, E.wait, E.bar
            wait("sync", f"{gtag}_mmu{NI512 - 1}_{NT - 1}")
            wait("sync", f"{gtag}_qi{NI512 - 1}_{NT - 1}")
            if agtag:
                wait("sync", agtag)
            for r in range(8):
                op("sync", lambda e, r=r: e.dma_start(
                    out=ap(qwd_res, r * 8 * D, [[131072, 128], [D, 8], [1, D]]),
                    in_=ap(qd_all, r * 8 * 128 * D, [[D, 128], [128 * D, 8], [1, D]])),
                   inc=("sWL", 16), tag=f"{tag}_dw{r}")
            for t in range(NT):
                poff = (t % 2) * 8192
                if t >= 2:
                    wait("scalar", f"{tag}_plfree{t - 2}")
                if t < 2:
                    wait("scalar", f"{gtag}_iqs{NI512 - 1}_{NT - 1}")
                else:
                    wait("scalar", f"{gtag}_iqs{NI512 - 1}_{t}")
                op("scalar", lambda e, t=t, poff=poff: e.dma_start(
                    out=aB(poff, [1, 8192]),
                    in_=ap(interq_s, t * II * 128,
                           [[128, 128], [128 * 128, NIT], [1, 128]])),
                   inc=("sPL", 16), tag=f"{tag}_pl{t}")
                xoff = 0 if t % 2 == 0 else 2048
                if t >= 2:
                    wait("scalar", f"{tag}_xrfree{t - 2}")
                if rdy is not None:
                    wait("scalar", rdy.format(t=t))
                op("scalar", lambda e, t=t, xoff=xoff: e.dma_start(
                    out=aF(xoff, [1, D]),
                    in_=ap(src_h, t * 128 * D, [[D, 128], [1, D]])),
                   inc=("sXL", 16), tag=f"{tag}_xr{t}")
                wait("tensor", f"{tag}_pl{t}")
                if t == 0:
                    for r in range(8):
                        wait("tensor", f"{tag}_dw{r}")
                banks = [(4 * t + k) % 6 for k in range(4)]
                for k in range(4):
                    n = 4 * t + k - 6
                    if n >= 0:
                        wait("tensor", f"{tag}_dev{n // 4}_{n % 4}")
                for it in range(NIT):
                    for k in range(4):
                        bk = pb[banks[k]]
                        op("tensor", lambda e, it=it, k=k, bk=bk, poff=poff: e.matmul(
                            bk[:, :], aB(poff + it * 128, [1, 128]),
                            ap(qwd_res, it * D + k * 512, [[131072, 128], [1, 512]]),
                            start=(it == 0), stop=(it == NIT - 1)),
                           inc=(("sT", 1) if it == NIT - 1 else None),
                           tag=(f"{tag}_dmm{t}_{k}" if it == NIT - 1 else None))
                E.settag(f"{tag}_plfree{t}", f"{tag}_dmm{t}_3")
                ooff = 4096 if t % 2 == 0 else 6144
                for k in range(4):
                    bk = pb[banks[k]]
                    wait("vector", f"{tag}_dmm{t}_{k}")
                    if k == 0:
                        wait("vector", f"{tag}_xr{t}")
                        if t >= 2:
                            wait("vector", f"{tag}_dst{t - 2}")
                    op("vector", lambda e, k=k, bk=bk, xoff=xoff, ooff=ooff:
                       e.scalar_tensor_tensor(
                           aF(ooff + k * 512, [1, 512]), bk[:, :],
                           0.5 * 2.0 ** -6, aF(xoff + k * 512, [1, 512]),
                           ALU.mult, ALU.add),
                       inc=("sV", 1), tag=f"{tag}_dev{t}_{k}")
                E.settag(f"{tag}_xrfree{t}", f"{tag}_dev{t}_3")
                wait("gpsimd", f"{tag}_dev{t}_3")
                op("gpsimd", lambda e, t=t, ooff=ooff: e.dma_start(
                    out=ap(dst_h, t * 128 * D, [[D, 128], [1, D]]),
                    in_=aF(ooff, [1, D])),
                   inc=("sO", 16), tag=f"{tag}_dst{t}")

        def copy_dram(E, tag, src, dst, rdy, entry=None):
            op, wait = E.op, E.wait
            for t in range(NT):
                if t == 0 and tag == "cp1":
                    wait("scalar", "wd15_qwb")
                if t < 2 and entry is not None:
                    wait("scalar", entry)
                if t >= 2:
                    wait("scalar", f"{tag}_s{t - 2}")
                if rdy is not None:
                    wait("scalar", rdy.format(t=t))
                xoff = 0 if t % 2 == 0 else 2048
                op("scalar", lambda e, t=t, xoff=xoff: e.dma_start(
                    out=aF(xoff, [1, D]),
                    in_=ap(src, t * 128 * D, [[D, 128], [1, D]])),
                   inc=("sXL", 16), tag=f"{tag}_l{t}")
                wait("gpsimd", f"{tag}_l{t}")
                op("gpsimd", lambda e, t=t, xoff=xoff: e.dma_start(
                    out=ap(dst, t * 128 * D, [[D, 128], [1, D]]),
                    in_=aF(xoff, [1, D])),
                   inc=("sO", 16), tag=f"{tag}_s{t}")

        def aphase(E, tag):
            op, wait, bar = E.op, E.wait, E.bar
            fdone = "f1_dmm15_3" if PH["ffn1"] else "cp1_s15"
            for eng in ("gpsimd", "scalar", "vector", "tensor", "sync"):
                wait(eng, fdone)
            xrdy = "f1_dst{t}" if PH["ffn1"] else "cp1_s{t}"
            op("gpsimd", lambda e: e.iota(
                statI[:, 0:128], [[1, 64], [0, 2]], channel_multiplier=0,
                allow_small_or_imprecise_dtypes=True),
               inc=("sG", 1), tag=f"{tag}_iotap")
            wait("vector", f"{tag}_iotap")
            bar("vector", lambda e: e.tensor_copy(aF(5760, [1, 128]), statI[:, 0:128]))
            op("vector", lambda e: e.engine_nop(), inc=("sV", 1), tag=f"{tag}_j2")
            wait("scalar", f"{tag}_j2")
            op("scalar", lambda e: e.activation(
                aF(5504, [1, 128]), aF(5760, [1, 128]),
                AF.Exp, scale=-float(np.log(THETA)) / 64.0),
               inc=("sS", 1), tag=f"{tag}_freqs")
            wait("gpsimd", f"{tag}_j2")
            op("gpsimd", lambda e: e.iota(
                statI[:, 0:1], [[0, 1]], channel_multiplier=1,
                allow_small_or_imprecise_dtypes=True),
               inc=("sG", 1), tag=f"{tag}_iotac")
            op("scalar", lambda e: e.dma_start(
                out=ap(small, 40, [[512, 1], [1, 4]]), in_=slot_h.ap()),
               inc=("sXL", 16), tag=f"{tag}_slot")
            wait("vector", f"{tag}_slot")
            bar("vector", lambda e: e.tensor_copy(
                ap(arenaB, 4300, [[16384, 1], [1, 4]]),
                ap(small, 40, [[512, 1], [1, 4]])))
            op("vector", lambda e: e.engine_nop(), inc=("sV", 1),
               tag=f"{tag}_slotbf")
            wait("tensor", f"{tag}_slotbf")
            wait("tensor", "ones")
            op("tensor", lambda e: e.matmul(
                pb[3][:, 0:4], ap(ones1, 0, [[1024, 1], [1, 128]]),
                ap(arenaB, 4300, [[16384, 1], [1, 4]]),
                start=True, stop=True), inc=("sT", 1), tag=f"{tag}_slotmm")
            wait("vector", f"{tag}_slotmm")
            op("vector", lambda e: e.tensor_copy(
                ap(small, 44, [[512, 128], [1, 4]]), pb[3][:, 0:4]),
               inc=("sV", 1), tag=f"{tag}_slotb")
            wait("vector", f"{tag}_slotb")
            wait("vector", f"{tag}_iotac")
            wait("vector", f"{tag}_freqs")
            bar("vector", lambda e: e.tensor_copy(small[:, 48:49], statI[:, 0:1]))
            op("vector", lambda e: e.scalar_tensor_tensor(
                small[:, 49:50], small[:, 45:46], 2048.0, small[:, 48:49],
                ALU.mult, ALU.add), inc=("sV", 1), tag=f"{tag}_pos")
            wait("gpsimd", f"{tag}_j2")
            wait("gpsimd", f"{tag}_pos")
            op("gpsimd", lambda e: e.iota(
                statI[:, 0:128], [[0, 64], [1, 2]], channel_multiplier=0,
                allow_small_or_imprecise_dtypes=True),
               inc=("sG", 1), tag=f"{tag}_sgni")
            wait("vector", f"{tag}_sgni")
            bar("vector", lambda e: e.tensor_copy(aF(5632, [1, 128]), statI[:, 0:128]))
            op("vector", lambda e: e.tensor_scalar(
                aF(5632, [1, 128]), aF(5632, [1, 128]), 2.0, -1.0,
                ALU.mult, ALU.add), inc=("sV", 1), tag=f"{tag}_sgnr")
            c1 = float(np.float32(6.28125))
            c2 = float(np.float32(0.001934051513671875))
            c3 = float(2 * np.pi) - c1 - c2

            def rope_tab(t):
                AA, BB, CC, DD = 5760, 5888, 6016, 6144
                if t == 0:
                    wait("vector", f"{tag}_sgnr")
                    wait("vector", f"{tag}_freqs")
                bar("vector", lambda e, t=t: e.tensor_scalar(
                    small[:, 50:51], small[:, 49:50], float(128 * t), None, ALU.add))
                bar("vector", lambda e: e.tensor_scalar(
                    aF(AA, [1, 128]), aF(5504, [1, 128]), small[:, 50:51],
                    None, ALU.mult))
                bar("vector", lambda e: e.tensor_scalar(
                    aF(BB, [1, 128]), aF(AA, [1, 128]),
                    float(1.0 / (2 * np.pi)), None, ALU.mult))
                bar("vector", lambda e: e.tensor_copy(
                    statI[:, 0:128], aF(BB, [1, 128])))
                bar("vector", lambda e: e.tensor_copy(
                    aF(BB, [1, 128]), statI[:, 0:128]))
                bar("vector", lambda e: e.scalar_tensor_tensor(
                    aF(CC, [1, 128]), aF(BB, [1, 128]), -c1,
                    aF(AA, [1, 128]), ALU.mult, ALU.add))
                bar("vector", lambda e: e.scalar_tensor_tensor(
                    aF(CC, [1, 128]), aF(BB, [1, 128]), -c2,
                    aF(CC, [1, 128]), ALU.mult, ALU.add))
                bar("vector", lambda e: e.scalar_tensor_tensor(
                    aF(CC, [1, 128]), aF(BB, [1, 128]), -c3,
                    aF(CC, [1, 128]), ALU.mult, ALU.add))
                bar("vector", lambda e: e.tensor_scalar(
                    aF(DD, [1, 128]), aF(CC, [1, 128]), 0.0, None, ALU.add))
                bar("vector", lambda e: e.tensor_scalar(
                    aF(AA, [1, 128]), aF(DD, [1, 128]), -float(np.pi), None, ALU.is_lt))
                bar("vector", lambda e: e.tensor_scalar(
                    aF(BB, [1, 128]), aF(DD, [1, 128]), float(np.pi), None, ALU.is_gt))
                bar("vector", lambda e: e.tensor_tensor(
                    aF(AA, [1, 128]), aF(AA, [1, 128]), aF(BB, [1, 128]), ALU.subtract))
                op("vector", lambda e: e.scalar_tensor_tensor(
                    aF(DD, [1, 128]), aF(AA, [1, 128]), float(2 * np.pi),
                    aF(DD, [1, 128]), ALU.mult, ALU.add),
                   inc=("sV", 1), tag=f"{tag}_wr0{t}")
                wait("scalar", f"{tag}_wr0{t}")
                op("scalar", lambda e: e.activation(
                    aF(AA, [1, 128]), aF(DD, [1, 128]), AF.Sin),
                   inc=("sS", 1), tag=f"{tag}_sin0{t}")
                bar("vector", lambda e: e.tensor_scalar(
                    aF(BB, [1, 128]), aF(CC, [1, 128]), float(np.pi / 2), None, ALU.add))
                wait("vector", f"{tag}_sin0{t}")
                bar("vector", lambda e: e.tensor_scalar(
                    aF(CC, [1, 128]), aF(BB, [1, 128]), -float(np.pi), None, ALU.is_lt))
                bar("vector", lambda e: e.tensor_scalar(
                    aF(DD, [1, 128]), aF(BB, [1, 128]), float(np.pi), None, ALU.is_gt))
                bar("vector", lambda e: e.tensor_tensor(
                    aF(CC, [1, 128]), aF(CC, [1, 128]), aF(DD, [1, 128]), ALU.subtract))
                op("vector", lambda e: e.scalar_tensor_tensor(
                    aF(BB, [1, 128]), aF(CC, [1, 128]), float(2 * np.pi),
                    aF(BB, [1, 128]), ALU.mult, ALU.add),
                   inc=("sV", 1), tag=f"{tag}_wr1{t}")
                wait("scalar", f"{tag}_wr1{t}")
                op("scalar", lambda e, t=t: e.activation(
                    tabc[:, t * 128:(t + 1) * 128], aF(BB, [1, 128]), AF.Sin),
                   inc=("sS", 1), tag=f"{tag}_tab{t}")
                op("vector", lambda e, t=t: e.tensor_tensor(
                    tabs[:, t * 128:(t + 1) * 128], aF(AA, [1, 128]),
                    aF(5632, [1, 128]), ALU.mult),
                   inc=("sV", 1), tag=f"{tag}_tabs{t}")
            op("gpsimd", lambda e: e.dma_start(
                out=ws(0, [512, ND], [1, 512]),
                in_=ap(wk_h, 0, [[512, 128], [512 * 128, ND], [1, 512]])),
               inc=("sWA", 16), tag=f"{tag}_wk")
            op("gpsimd", lambda e: e.dma_start(
                out=ws(8192, [512, ND], [1, 512]),
                in_=ap(wv_h, 0, [[512, 128], [512 * 128, ND], [1, 512]])),
               inc=("sWA", 16), tag=f"{tag}_wv")
            op("gpsimd", lambda e: e.dma_start(
                out=ap(ckv, 0, [[4096, 128], [128, ND], [1, 128]]),
                in_=ap(wiq_h, 0, [[128, 128], [128 * 128, ND], [1, 128]])),
               inc=("sWA", 16), tag=f"{tag}_wiq")
            op("scalar", lambda e: e.dma_start(
                out=ap(small, 56, [[512, 1], [1, 256]]),
                in_=ap(cw_h, 0, [[256, 1], [1, 256]])),
               inc=("sXL", 16), tag=f"{tag}_cw")
            wait("vector", f"{tag}_cw")
            bar("vector", lambda e: e.tensor_copy(
                ap(arenaB, 4096, [[16384, 1], [1, 256]]),
                ap(small, 56, [[512, 1], [1, 256]])))
            op("vector", lambda e: e.engine_nop(), inc=("sV", 1), tag=f"{tag}_cwb")
            wait("tensor", f"{tag}_cwb")
            wait("tensor", "ones")
            op("tensor", lambda e: e.matmul(
                pb[2][:, 0:128], ap(ones1, 0, [[1024, 1], [1, 128]]),
                ap(arenaB, 4096, [[16384, 1], [1, 128]]),
                start=True, stop=True), inc=("sT", 1))
            op("tensor", lambda e: e.matmul(
                pb[2][:, 128:256], ap(ones1, 0, [[1024, 1], [1, 128]]),
                ap(arenaB, 4096 + 128, [[16384, 1], [1, 128]]),
                start=True, stop=True), inc=("sT", 1), tag=f"{tag}_cwt")
            wait("vector", f"{tag}_cwt")
            op("vector", lambda e: e.tensor_copy(
                aF(5120, [1, 256]), pb[2][:, 0:256]),
               inc=("sV", 1), tag=f"{tag}_cwtile")
            for t in range(NT):
                xoff = 0 if t % 2 == 0 else 2048
                if t >= 2:
                    wait("scalar", f"{tag}_afree{t - 2}")
                if t < 2 and PH["ffn1"]:
                    wait("scalar", "f1_dev15_3")
                wait("scalar", xrdy.format(t=t))
                op("scalar", lambda e, t=t, xoff=xoff: e.dma_start(
                    out=aF(xoff, [1, D]),
                    in_=ap(x_mid, t * 128 * D, [[D, 128], [1, D]])),
                   inc=("sXL", 16), tag=f"{tag}_xl{t}")
                rope_tab(t)
                wait("scalar", f"{tag}_xl{t}")
                wait("vector", f"{tag}_xl{t}")
                ln_stats(E, f"{tag}_ln{t}", xoff, save_col=t,
                         prev_user=(f"{tag}_xln{t - 1}" if t >= 1 else None))
                wait("scalar", f"{tag}_ln{t}_ab")
                if t >= 1:
                    wait("scalar", f"{tag}_xtt{t - 1}_{ND - 1}")
                op("scalar", lambda e, xoff=xoff: e.activation(
                    aB(0, [1, D]), aF(xoff, [1, D]),
                    AF.Identity, bias=small[:, 1:2], scale=small[:, 0:1]),
                   inc=("sS", 1), tag=f"{tag}_xln{t}")
                E.settag(f"{tag}_afree{t}", f"{tag}_xln{t}")
                wait("tensor", f"{tag}_xln{t}")
                for ft in range(ND):
                    op("tensor", lambda e, ft=ft, t=t: e.matmul(
                        pb[5][:, ft * 16 + t:ft * 16 + t + 1], aB(ft * 128, [1, 128]),
                        ones1[:, 0:1], start=True, stop=True),
                       inc=(("sT", 1) if (t == NT - 1 and ft == ND - 1) else None),
                       tag=(f"{tag}_csmm" if (t == NT - 1 and ft == ND - 1) else None))
                for ft in range(ND):
                    pdst = pt0 if ft % 2 == 0 else pt1
                    if ft >= 2:
                        wait("tensor", f"{tag}_xte{t}_{ft - 2}")
                    elif t >= 1:
                        wait("tensor", f"{tag}_xte{t - 1}_{ND - 2 + ft}")
                        if ft == 0:
                            wait("tensor", f"{tag}_kde{t - 1}")
                    op("tensor", lambda e, ft=ft, pdst=pdst: e.transpose(
                        pdst[:, 0:128], aB(ft * 128, [1, 128]),
                        ident[:, :]), inc=("sT", 1), tag=f"{tag}_xtt{t}_{ft}")
                    wait("vector", f"{tag}_xtt{t}_{ft}")
                    if ft == 0 and t >= 1:
                        wait("vector", f"{tag}_kmm{t - 1}")
                        wait("vector", f"{tag}_vmm{t - 1}")
                        if t - 1 < 4:
                            wait("vector", f"{tag}_qimm{t - 1}")
                    op("vector", lambda e, ft=ft, pdst=pdst: e.tensor_copy(
                        aB(2048 + ft * 128, [1, 128]), pdst[:, 0:128]),
                       inc=("sV", 1), tag=f"{tag}_xte{t}_{ft}")
                wait("tensor", f"{tag}_xte{t}_{ND - 1}")
                if t == 0:
                    wait("tensor", f"{tag}_wk")
                    wait("tensor", f"{tag}_wv")
                    wait("tensor", f"{tag}_wiq")
                else:
                    wait("tensor", f"{tag}_krope{t - 1}")
                    wait("tensor", f"{tag}_vev{t - 1}")
                for ft in range(ND):
                    op("tensor", lambda e, ft=ft: e.matmul(
                        pb[0][:, :], aB(2048 + ft * 128, [1, 128]),
                        ws(ft * 512, [1, 512]),
                        start=(ft == 0), stop=(ft == ND - 1)),
                       inc=(("sT", 1) if ft == ND - 1 else None),
                       tag=(f"{tag}_kmm{t}" if ft == ND - 1 else None))
                    op("tensor", lambda e, ft=ft: e.matmul(
                        pb[1][:, :], aB(2048 + ft * 128, [1, 128]),
                        ws(8192 + ft * 512, [1, 512]),
                        start=(ft == 0), stop=(ft == ND - 1)),
                       inc=(("sT", 1) if ft == ND - 1 else None),
                       tag=(f"{tag}_vmm{t}" if ft == ND - 1 else None))
                if t < 4:
                    if t >= 1:
                        wait("tensor", f"{tag}_qie{t - 1}")
                    for ft in range(ND):
                        op("tensor", lambda e, ft=ft: e.matmul(
                            pb[4][:, 0:128],
                            ap(ckv, ft * 128, [[4096, 128], [1, 128]]),
                            aB(2048 + ft * 128, [1, 128]),
                            start=(ft == 0), stop=(ft == ND - 1)),
                           inc=(("sT", 1) if ft == ND - 1 else None),
                           tag=(f"{tag}_qimm{t}" if ft == ND - 1 else None))
                    wait("vector", f"{tag}_qimm{t}")
                    op("vector", lambda e, t=t: e.tensor_copy(
                        ap(ckv, 3584 + t * 128, [[4096, 128], [1, 128]]),
                        pb[4][:, 0:128]),
                       inc=("sV", 1), tag=f"{tag}_qie{t}")
                wait("vector", f"{tag}_kmm{t}")
                wait("vector", f"{tag}_tabs{t}")
                wait("vector", f"{tag}_tab{t}")
                cosrep = ap(tabc, t * 128, [[2048, 128], [0, 4], [1, 128]])
                sinrep4 = ap(tabs, t * 128, [[2048, 128], [0, 4], [2, 64], [1, 2]])
                bar("vector", lambda e, cosrep=cosrep: e.tensor_tensor(
                    aF(4096, [128, 4], [1, 128]),
                    ap(pb[0], 0, [[512, 128], [128, 4], [1, 128]]),
                    cosrep, ALU.mult))
                ksw = ap(pb[0], 1, [[512, 128], [128, 4], [2, 64], [-1, 2]])
                bar("vector", lambda e, ksw=ksw, sinrep4=sinrep4: e.tensor_tensor(
                    aF(4608, [128, 4], [2, 64], [1, 2]), ksw, sinrep4, ALU.mult))
                op("vector", lambda e, t=t: e.tensor_tensor(
                    kv(t * 512, [1, 512]), aF(4096, [1, 512]),
                    aF(4608, [1, 512]), ALU.add),
                   inc=("sV", 1), tag=f"{tag}_krope{t}")
                wait("vector", f"{tag}_vmm{t}")
                op("vector", lambda e, t=t: e.tensor_copy(
                    kv(8192 + t * 512, [1, 512]), pb[1][:, :]),
                   inc=("sV", 1), tag=f"{tag}_vev{t}")
                wait("vector", f"{tag}_cwtile")
                cwa_rep = ap(arenaF, 5120, [[10240, 128], [0, 4], [1, 128]])
                cwb_rep = ap(arenaF, 5248, [[10240, 128], [0, 4], [1, 128]])
                kch3 = kv(t * 512, [128, 4], [1, 128])
                bar("vector", lambda e, kch3=kch3, cwa_rep=cwa_rep: e.tensor_tensor(
                    aF(4096, [128, 4], [1, 128]), kch3, cwa_rep, ALU.mult))
                bar("vector", lambda e: e.tensor_reduce(
                    aF(5376, [1, 4], [0, 1]), aF(4096, [128, 4], [1, 128]),
                    AX.X, ALU.add))
                bar("vector", lambda e, kch3=kch3, cwb_rep=cwb_rep: e.tensor_tensor(
                    aF(4096, [128, 4], [1, 128]), kch3, cwb_rep, ALU.mult))
                bar("vector", lambda e: e.tensor_reduce(
                    aF(5380, [1, 4], [0, 1]), aF(4096, [128, 4], [1, 128]),
                    AX.X, ALU.add))
                op("vector", lambda e: e.tensor_copy(
                    aB(4224, [1, 8]), aF(5376, [1, 8])), inc=("sV", 1),
                   tag=f"{tag}_kd{t}")
                wait("tensor", f"{tag}_kd{t}")
                if t >= 2:
                    wait("tensor", f"{tag}_kde{t - 2}")
                ptr = pt0 if t % 2 == 0 else pt1
                op("tensor", lambda e, ptr=ptr: e.transpose(
                    ptr[:, 0:128], aB(4224, [1, 128]), ident[:, :]),
                   inc=("sT", 1), tag=f"{tag}_kdt{t}")
                wait("vector", f"{tag}_kdt{t}")
                op("vector", lambda e, t=t, ptr=ptr: e.tensor_copy(
                    ap(arenaB, 4352 + t * 128, [[16384, 8], [1, 128]]),
                    ap(ptr, 0, [[1024, 8], [1, 128]])),
                   inc=("sV", 1), tag=f"{tag}_kde{t}")
            kdallB = ap(arenaB, 4352, [[16384, 8], [1, 2048]])
            kdF = ap(arenaF, 4096, [[10240, 8], [1, 2048]])
            kd3 = ap(arenaF, 4096, [[10240, 8], [16, 128], [1, 16]])
            mx3 = ap(arenaF, 6400, [[10240, 8], [1, 128], [0, 1]])
            mxb = ap(arenaF, 6400, [[10240, 8], [1, 128], [0, 16]])
            bar("vector", lambda e: e.tensor_copy(kdF, kdallB))
            bar("vector", lambda e: e.tensor_reduce(mx3, kd3, AX.X, ALU.max))
            bar("vector", lambda e: e.tensor_tensor(kd3, kd3, mxb, ALU.subtract))
            op("vector", lambda e: e.engine_nop(), inc=("sV", 1), tag=f"{tag}_kdsub")
            wait("scalar", f"{tag}_kdsub")
            op("scalar", lambda e: e.activation(kdF, kdF, AF.Exp),
               inc=("sS", 1), tag=f"{tag}_kdexp")
            wait("vector", f"{tag}_kdexp")
            bar("vector", lambda e: e.tensor_reduce(mx3, kd3, AX.X, ALU.add))
            bar("vector", lambda e: e.reciprocal(
                ap(arenaF, 6400, [[10240, 8], [1, 128]]),
                ap(arenaF, 6400, [[10240, 8], [1, 128]])))
            bar("vector", lambda e: e.tensor_tensor(kd3, kd3, mxb, ALU.mult))
            shm = [(i + 4) % 32 for i in range(32)]
            bar("vector", lambda e: e.stream_shuffle(
                ap(arenaF, 6656, [[10240, 32], [1, 2048]]),
                ap(arenaF, 4096, [[10240, 32], [1, 2048]]), shm))
            op("vector", lambda e: e.tensor_tensor(
                ap(arenaF, 4096, [[10240, 4], [1, 2048]]),
                ap(arenaF, 4096, [[10240, 4], [1, 2048]]),
                ap(arenaF, 6656, [[10240, 4], [1, 2048]]), ALU.add),
               inc=("sV", 1), tag=f"{tag}_wrow")
            for c in range(NT):
                bar("vector", lambda e, c=c: e.tensor_copy(
                    ap(arenaB, 6400, [[16384, 4], [1, 128]]),
                    ap(arenaF, 4096 + c * 128, [[10240, 4], [1, 128]])))
                op("vector", lambda e: e.engine_nop(), inc=("sV", 1),
                   tag=f"{tag}_wcb{c}")
                wait("tensor", f"{tag}_wcb{c}")
                if c >= 2:
                    wait("tensor", f"{tag}_wte{c - 2}")
                ptr = pt0 if c % 2 == 0 else pt1
                op("tensor", lambda e, ptr=ptr: e.transpose(
                    ptr[:, 0:128], ap(arenaB, 6400, [[16384, 128], [1, 128]]),
                    ident[:, :]), inc=("sT", 1), tag=f"{tag}_wtt{c}")
                wait("vector", f"{tag}_wtt{c}")
                op("vector", lambda e, c=c, ptr=ptr: e.tensor_copy(
                    ap(ckv, 3072 + c * 4, [[4096, 128], [1, 4]]),
                    ap(ptr, 0, [[1024, 128], [1, 4]])),
                   inc=("sV", 1), tag=f"{tag}_wte{c}")
            wait("vector", "indt")
            for c in range(NT):
                wrep = ap(ckv, 3072 + c * 4, [[4096, 128], [1, 4], [0, 128]])
                bar("vector", lambda e, c=c, wrep=wrep: e.tensor_tensor(
                    aB(6528, [128, 4], [1, 128]), kv(c * 512, [128, 4], [1, 128]),
                    wrep, ALU.mult))
                op("vector", lambda e, c=c, wrep=wrep: e.tensor_tensor(
                    aB(7040, [128, 4], [1, 128]),
                    kv(8192 + c * 512, [128, 4], [1, 128]), wrep, ALU.mult),
                   inc=("sV", 1), tag=f"{tag}_kvw{c}")
                wait("tensor", f"{tag}_kvw{c}")
                if c >= 1:
                    wait("tensor", f"{tag}_cke{c - 1}")
                op("tensor", lambda e: e.matmul(
                    pb[2][0:8, 0:512], indt[:, :], aB(6528, [1, 512]),
                    start=True, stop=True), inc=("sT", 1))
                op("tensor", lambda e: e.matmul(
                    pb[3][0:8, 0:512], indt[:, :], aB(7040, [1, 512]),
                    start=True, stop=True), inc=("sT", 1), tag=f"{tag}_ckmm{c}")
                wait("vector", f"{tag}_ckmm{c}")
                if c >= 2:
                    wait("vector", f"{tag}_ckdma{c - 2}")
                par = (c % 2) * 1024
                op("vector", lambda e, par=par: e.tensor_copy(
                    ap(arenaB, 8192 + par, [[16384, 8], [1, 512]]),
                    ap(pb[2], 0, [[512, 8], [1, 512]])), inc=("sV", 1))
                op("vector", lambda e, par=par: e.tensor_copy(
                    ap(arenaB, 8192 + par + 512, [[16384, 8], [1, 512]]),
                    ap(pb[3], 0, [[512, 8], [1, 512]])),
                   inc=("sV", 1), tag=f"{tag}_cke{c}")
                wait("gpsimd", f"{tag}_cke{c}")
                op("gpsimd", lambda e, c=c, par=par: e.dma_start(
                    out=ap(at_in, ATO_CK + c * 8 * 512, [[512, 8], [1, 512]]),
                    in_=ap(arenaB, 8192 + par, [[16384, 8], [1, 512]])),
                   inc=("sO", 16))
                op("gpsimd", lambda e, c=c, par=par: e.dma_start(
                    out=ap(at_in, ATO_CV + c * 8 * 512, [[512, 8], [1, 512]]),
                    in_=ap(arenaB, 8192 + par + 512, [[16384, 8], [1, 512]])),
                   inc=("sO", 16), tag=f"{tag}_ckdma{c}")
            E.settag(f"{tag}_ckst", f"{tag}_ckdma{NT - 1}")
            E.settag(f"{tag}_cvst", f"{tag}_ckdma{NT - 1}")
            wait("vector", f"{tag}_csmm")
            bar("vector", lambda e: e.tensor_reduce(
                aF(4096, [1, 16]),
                ap(pb[5], 0, [[512, 128], [16, 16], [1, 16]]), AX.X, ALU.add))
            op("vector", lambda e: e.tensor_copy(
                ap(ckv, 3136 + 0, [[4096, 128], [1, 16]]), aF(4096, [1, 16])),
               inc=("sV", 1), tag=f"{tag}_csev")
            wait("gpsimd", f"{tag}_csev")
            wait("gpsimd", f"{tag}_qie3")
            op("gpsimd", lambda e: e.dma_start(
                out=ap(at_in, ATO_QI, [[512, 128], [1, 512]]),
                in_=ap(ckv, 3584, [[4096, 128], [1, 512]])),
               inc=("sO", 16), tag=f"{tag}_qist")
            op("gpsimd", lambda e: e.dma_start(
                out=ap(at_in, ATO_CS, [[16, 128], [1, 16]]),
                in_=ap(ckv, 3136, [[4096, 128], [1, 16]])),
               inc=("sO", 16), tag=f"{tag}_csst")
            for tg in (f"{tag}_ckst", f"{tag}_cvst", f"{tag}_qist", f"{tag}_csst"):
                wait("gpsimd", tg)
            op("gpsimd", lambda e: e.collective_compute(
                "AllGather", ALU.bypass,
                replica_groups=[[0, 1, 2, 3], [4, 5, 6, 7]],
                ins=[at_in.ap().opt()], outs=[at_all.ap().opt()]),
               inc=("sC", 1), tag="ag_at")
            wait("gpsimd", "ag_at")
            for r in range(4):
                for h in range(4):
                    op("gpsimd", lambda e, r=r, h=h: e.dma_start(
                        out=ap(ckv, 2048 + h * 512 + r * 128,
                               [[4096, 128], [1, 128]]),
                        in_=ap(at_all, r * AT_N + ATO_CV + h * 128,
                               [[512, 128], [1, 128]])),
                       inc=("sO", 16), tag=f"{tag}_ucv{r}_{h}")
            for r in range(4):
                for h in range(4):
                    op("gpsimd", lambda e, r=r, h=h: e.dma_start(
                        out=kv((4 * r + h) * 128, [1, 128]),
                        in_=ap(at_all, r * AT_N + ATO_CK + h * 128,
                               [[512, 128], [1, 128]])),
                       inc=("sO", 16), tag=f"{tag}_uck{r}_{h}")
            op("gpsimd", lambda e: e.dma_start(
                out=kv(2048, [1, 512]),
                in_=ap(at_all, 0 * AT_N + ATO_QI, [[512, 128], [1, 512]])),
               inc=("sO", 16), tag=f"{tag}_uqi")
            for r in range(4):
                op("gpsimd", lambda e, r=r: e.dma_start(
                    out=kv(2560 + r * 16, [1, 16]),
                    in_=ap(at_all, r * AT_N + ATO_CS, [[16, 128], [1, 16]])),
                   inc=("sO", 16), tag=f"{tag}_ucs{r}")
            for r in range(4):
                wait("vector", f"{tag}_ucs{r}")
            bar("vector", lambda e: e.tensor_copy(
                aF(4096, [16, 4], [1, 16]), kv(2560, [16, 4], [1, 16])))
            bar("vector", lambda e: e.tensor_tensor(
                aF(4096, [1, 16]), aF(4096, [1, 16]), aF(4096 + 16, [1, 16]), ALU.add))
            bar("vector", lambda e: e.tensor_tensor(
                aF(4096 + 32, [1, 16]), aF(4096 + 32, [1, 16]),
                aF(4096 + 48, [1, 16]), ALU.add))
            bar("vector", lambda e: e.tensor_tensor(
                aF(4096, [1, 16]), aF(4096, [1, 16]), aF(4096 + 32, [1, 16]), ALU.add))
            op("vector", lambda e: e.tensor_copy(
                aB(6400, [1, 16]), aF(4096, [1, 16])), inc=("sV", 1),
               tag=f"{tag}_cst")
            op("gpsimd", lambda e: e.dma_start(
                out=kv(2816, [128, ND], [1, 128]),
                in_=ap(wik_h, 0, [[128, 128], [128 * 128, ND], [1, 128]])),
               inc=("sWA", 16), tag=f"{tag}_wik")
            wait("tensor", f"{tag}_cst")
            wait("tensor", f"{tag}_wik")
            wait("tensor", f"{tag}_uqi")
            for ft in range(ND):
                op("tensor", lambda e, ft=ft: e.matmul(
                    pb[4][0:1, 0:128], aB(6400 + ft, [1, 1]),
                    kv(2816 + ft * 128, [1, 128]),
                    start=(ft == 0), stop=(ft == ND - 1)),
                   inc=(("sT", 1) if ft == ND - 1 else None),
                   tag=(f"{tag}_kimm" if ft == ND - 1 else None))
            wait("vector", f"{tag}_kimm")
            op("vector", lambda e: e.tensor_copy(
                ap(arenaB, 6528, [[16384, 1], [1, 128]]),
                ap(pb[4], 0, [[512, 1], [1, 128]])), inc=("sV", 1),
               tag=f"{tag}_kime")
            wait("tensor", f"{tag}_kime")
            wait("tensor", f"{tag}_kde{NT - 1}")
            op("tensor", lambda e: e.transpose(
                pt0[:, 0:1], ap(arenaB, 6528, [[16384, 1], [1, 128]]),
                ident[0:1, 0:1]), inc=("sT", 1), tag=f"{tag}_kimt")
            wait("vector", f"{tag}_kimt")
            op("vector", lambda e: e.tensor_copy(
                ap(arenaB, 6656, [[16384, 128], [1, 1]]),
                ap(pt0, 0, [[1024, 128], [1, 1]])), inc=("sV", 1),
               tag=f"{tag}_kimc")
            wait("tensor", f"{tag}_kimc")
            op("tensor", lambda e: e.matmul(
                pb[4][0:1, 0:512], ap(arenaB, 6656, [[16384, 128], [1, 1]]),
                kv(2048, [1, 512]), start=True, stop=True),
               inc=("sT", 1), tag=f"{tag}_smm")
            wait("vector", f"{tag}_smm")
            op("vector", lambda e: e.tensor_copy(
                ap(arenaB, 6784, [[16384, 1], [1, 512]]),
                ap(pb[4], 0, [[512, 1], [1, 512]])), inc=("sV", 1),
               tag=f"{tag}_sev")
            wait("tensor", f"{tag}_sev")
            op("tensor", lambda e: e.matmul(
                pb[4][:, 0:512], ap(ones1, 0, [[1024, 1], [1, 128]]),
                ap(arenaB, 6784, [[16384, 1], [1, 512]]),
                start=True, stop=True), inc=("sT", 1), tag=f"{tag}_bmm")
            for c in range(4):
                op("tensor", lambda e, c=c: e.transpose(
                    pt1[:, 2 * c:2 * c + 1],
                    ap(arenaB, 6784 + c * 128, [[16384, 1], [1, 128]]),
                    ident[0:1, 0:1]), inc=(("sT", 1) if c == 3 else None),
                   tag=(f"{tag}_sct" if c == 3 else None))
            wait("vector", f"{tag}_bmm")
            op("vector", lambda e: e.tensor_copy(
                aF(4096, [1, 512]), pb[4][:, 0:512]), inc=("sV", 1))
            wait("vector", f"{tag}_sct")
            op("vector", lambda e: e.tensor_copy(
                aF(4608, [1, 4]), ap(pt1, 0, [[1024, 128], [2, 4]])),
               inc=("sV", 1), tag=f"{tag}_scev")
            wait("vector", f"{tag}_scev")
            for c in range(4):
                bar("vector", lambda e, c=c: e.tensor_scalar(
                    aF(5120, [1, 512]), aF(4096, [1, 512]),
                    ap(arenaF, 4608 + c, [[10240, 128], [1, 1]]), None, ALU.is_gt))
                bar("vector", lambda e, c=c: e.tensor_reduce(
                    ap(arenaF, 5632 + c, [[10240, 128], [1, 1]]),
                    aF(5120, [1, 512]), AX.X, ALU.add))
            bar("vector", lambda e: e.tensor_scalar(
                aF(5120, [1, 4]), aF(5632, [1, 4]), 64.0, 1.0,
                ALU.is_lt, ALU.subtract))
            op("vector", lambda e: e.tensor_scalar(
                aB(6912, [1, 4]), aF(5120, [1, 4]), 30000.0, None, ALU.mult),
               inc=("sV", 1), tag=f"{tag}_maskc")
            wait("tensor", f"{tag}_maskc")
            op("tensor", lambda e: e.transpose(
                pt0[0:4, 0:128], ap(arenaB, 6912, [[16384, 128], [1, 4]]),
                ident[:, :]), inc=("sT", 1), tag=f"{tag}_maskt")
            wait("vector", f"{tag}_maskt")
            op("vector", lambda e: e.tensor_copy(
                ap(arenaB, 7424, [[16384, 4], [1, 128]]),
                ap(pt0, 0, [[1024, 4], [1, 128]])), inc=("sV", 1),
               tag=f"{tag}_maske")
            wait("gpsimd", f"{tag}_maske")
            op("gpsimd", lambda e: e.dma_start(
                out=ap(bounce_b, 0, [[1, 512]]),
                in_=ap(arenaB, 7424, [[16384, 4], [1, 128]])),
               inc=("sO", 16), tag=f"{tag}_maskbo")
            wait("gpsimd", f"{tag}_maskbo")
            op("gpsimd", lambda e: e.dma_start(
                out=ap(ones1, 512, [[1024, 1], [1, 512]]),
                in_=ap(bounce_b, 0, [[1, 512]])),
               inc=("sO", 16), tag=f"{tag}_maskrow")
            for i in range(16):
                r, h = i // 4, i % 4
                wait("tensor", f"{tag}_uck{r}_{h}")
                ptr = pt0 if i % 2 == 0 else pt1
                if i >= 2:
                    wait("tensor", f"{tag}_ckfe{i - 2}")
                elif i == 0:
                    wait("tensor", f"{tag}_maske")
                else:
                    wait("tensor", f"{tag}_scev")
                op("tensor", lambda e, i=i, ptr=ptr: e.transpose(
                    ptr[:, 0:128], kv(i * 128, [1, 128]), ident[:, :]),
                   inc=("sT", 1), tag=f"{tag}_ckft{i}")
                wait("vector", f"{tag}_ckft{i}")
                op("vector", lambda e, i=i, r=r, h=h, ptr=ptr: e.tensor_copy(
                    ap(ckv, h * 512 + r * 128, [[4096, 128], [1, 128]]),
                    ptr[:, 0:128]), inc=("sV", 1), tag=f"{tag}_ckfe{i}")
            op("vector", lambda e: e.engine_nop(), inc=("sV", 1), tag=f"{tag}_adone")

        def bphase(E, tag):
            op, wait, bar = E.op, E.wait, E.bar
            xrdy = "f1_dst{t}" if PH["ffn1"] else "cp1_s{t}"
            for eng in ("scalar", "vector", "tensor", "sync", "gpsimd"):
                wait(eng, f"{tag}_adone")
                wait(eng, f"{tag}_maskrow")
                for r in range(4):
                    wait(eng, f"{tag}_ucv{r}_3")
            for g in range(4):
                for tc in range(4):
                    t = 4 * g + tc
                    xoff = tc * 2048
                    wait("scalar", xrdy.format(t=t))
                    if g >= 1:
                        wait("scalar", f"{tag}_xst{t - 4}")
                    op("scalar", lambda e, t=t, xoff=xoff: e.dma_start(
                        out=aF(xoff, [1, D]),
                        in_=ap(x_mid, t * 128 * D, [[D, 128], [1, D]])),
                       inc=("sXL", 16), tag=f"{tag}_bxl{t}")
                    wait("scalar", f"{tag}_bxl{t}")
                    if t >= 1:
                        wait("scalar", f"{tag}_bxtt{t - 1}")
                    op("scalar", lambda e, t=t, xoff=xoff: e.activation(
                        aB(8192, [1, D]), aF(xoff, [1, D]), AF.Identity,
                        bias=small[:, 32 + t:33 + t], scale=small[:, 16 + t:17 + t]),
                       inc=("sS", 1), tag=f"{tag}_bxln{t}")
                    wait("tensor", f"{tag}_bxln{t}")
                    for ft in range(ND):
                        pdst = pt0 if ft % 2 == 0 else pt1
                        if ft >= 2:
                            wait("tensor", f"{tag}_bxe{t}_{ft - 2}")
                        elif t >= 1:
                            wait("tensor", f"{tag}_bxe{t - 1}_{ND - 2 + ft}")
                        op("tensor", lambda e, ft=ft, pdst=pdst:
                           e.transpose(
                               pdst[:, 0:128], aB(8192 + ft * 128, [1, 128]),
                               ident[:, :]),
                           inc=("sT", 1),
                           tag=(f"{tag}_bxtt{t}" if ft == ND - 1 else f"{tag}_bxt{t}_{ft}"))
                        wait("vector", f"{tag}_bxtt{t}" if ft == ND - 1 else f"{tag}_bxt{t}_{ft}")
                        if ft == 0 and g >= 1:
                            wait("vector", f"{tag}_qmm{g - 1}_1_{tc}")
                        op("vector", lambda e, tc=tc, ft=ft, pdst=pdst:
                           e.tensor_copy(
                               aB(ft * 512 + tc * 128, [1, 128]),
                               pdst[:, 0:128]),
                           inc=("sV", 1), tag=f"{tag}_bxe{t}_{ft}")
                    E.settag(f"{tag}_bxe{t}", f"{tag}_bxe{t}_{ND - 1}")
                for qp in range(2):
                    for half in range(2):
                        qc = 2 * qp + half
                        woff = half * 8192
                        if qp == 1:
                            wait("gpsimd", f"{tag}_qmm{g}_0_3")
                            wait("gpsimd", f"{tag}_qmm{g}_0_3a")
                        elif g >= 1:
                            wait("gpsimd", f"{tag}_smm{g - 1}_{H - 1}_3")
                        op("gpsimd", lambda e, qc=qc, woff=woff: e.dma_start(
                            out=ws(woff, [512, ND], [1, 512]),
                            in_=ap(wq_h, qc * 512,
                                   [[D, 128], [D * 128, ND], [1, 512]])),
                           inc=("sWA", 16), tag=f"{tag}_wq{g}_{qc}")
                    for tc in range(4):
                        t = 4 * g + tc
                        pq0 = pb[0] if tc % 2 == 0 else pb[2]
                        pq1 = pb[1] if tc % 2 == 0 else pb[3]
                        if qp == 0 and tc == 0 and g >= 1:
                            wait("tensor", f"{tag}_att{g - 1}_{H - 1}_3")
                        wait("tensor", f"{tag}_wq{g}_{2 * qp}")
                        wait("tensor", f"{tag}_wq{g}_{2 * qp + 1}")
                        wait("tensor", f"{tag}_bxe{t}")
                        if not (g == 0 and qp == 0 and tc < 2):
                            pass
                        idx = (g * 2 + qp) * 4 + tc
                        if idx >= 2:
                            pidx = idx - 2
                            wait("tensor", f"{tag}_rope{pidx}")
                        for ft in range(ND):
                            op("tensor", lambda e, tc=tc, ft=ft, pq0=pq0: e.matmul(
                                pq0[:, :], aB(ft * 512 + tc * 128, [1, 128]),
                                ws(0 * 8192 + ft * 512, [1, 512]),
                                start=(ft == 0), stop=(ft == ND - 1)),
                               inc=(("sT", 1) if ft == ND - 1 else None),
                               tag=(f"{tag}_qmm{g}_{qp}_{tc}a" if ft == ND - 1 else None))
                            op("tensor", lambda e, tc=tc, ft=ft, pq1=pq1: e.matmul(
                                pq1[:, :], aB(ft * 512 + tc * 128, [1, 128]),
                                ws(8192 + ft * 512, [1, 512]),
                                start=(ft == 0), stop=(ft == ND - 1)),
                               inc=(("sT", 1) if ft == ND - 1 else None),
                               tag=(f"{tag}_qmm{g}_{qp}_{tc}" if ft == ND - 1 else None))
                        wait("vector", f"{tag}_qmm{g}_{qp}_{tc}a")
                        wait("vector", f"{tag}_qmm{g}_{qp}_{tc}")
                        for half, pq in ((0, pq0), (1, pq1)):
                            cosrep = ap(tabc, t * 128, [[2048, 128], [0, 4], [1, 128]])
                            sinrep4 = ap(tabs, t * 128,
                                         [[2048, 128], [0, 4], [2, 64], [1, 2]])
                            bar("vector", lambda e, pq=pq, cosrep=cosrep: e.tensor_tensor(
                                aF(8192, [128, 4], [1, 128]),
                                ap(pq, 0, [[512, 128], [128, 4], [1, 128]]),
                                cosrep, ALU.mult))
                            qsw = ap(pq, 1, [[512, 128], [128, 4], [2, 64], [-1, 2]])
                            bar("vector", lambda e, qsw=qsw, sinrep4=sinrep4: e.tensor_tensor(
                                aF(8704, [128, 4], [2, 64], [1, 2]), qsw, sinrep4,
                                ALU.mult))
                            tg = f"{tag}_rope{idx}" if half == 1 else None
                            op("vector", lambda e, tc=tc, qp=qp, half=half: e.tensor_tensor(
                                aB(8192 + tc * 2048 + qp * 1024 + half * 512, [1, 512]),
                                aF(8192, [1, 512]), aF(8704, [1, 512]), ALU.add),
                               inc=("sV", 1), tag=(tg if tg else f"{tag}_ropeh{idx}_{half}"))
                for tc in range(4):
                    t = 4 * g + tc
                    idx3 = (g * 2 + 1) * 4 + tc
                    wait("tensor", f"{tag}_rope{idx3}")
                    for ft in range(ND):
                        pdst = pt0 if ft % 2 == 0 else pt1
                        if ft >= 2:
                            wait("tensor", f"{tag}_qte{t}_{ft - 2}")
                        elif tc >= 1:
                            wait("tensor", f"{tag}_qte{t - 1}_{ND - 2 + ft}")
                        op("tensor", lambda e, tc=tc, ft=ft, pdst=pdst:
                           e.transpose(
                               pdst[:, 0:128],
                               aB(8192 + tc * 2048 + ft * 128, [1, 128]),
                               ident[:, :]), inc=("sT", 1),
                           tag=f"{tag}_qtt{t}_{ft}")
                        wait("vector", f"{tag}_qtt{t}_{ft}")
                        if ft == 0 and tc == 0 and g >= 1:
                            wait("vector", f"{tag}_wo_done{g - 1}")
                        op("vector", lambda e, tc=tc, ft=ft, pdst=pdst:
                           e.tensor_copy(
                               ws(ft * 512 + tc * 128, [1, 128]),
                               pdst[:, 0:128]),
                           inc=("sV", 1), tag=f"{tag}_qte{t}_{ft}")
                for tc in range(4):
                    wait("tensor", f"{tag}_qte{4 * g + tc}_{ND - 1}")
                P5 = [pb[0], pb[1], pb[2], pb[3], pb[5]]
                for h in range(H):
                    hk = h // 4
                    for tc in range(4):
                        idxp = (g * H + h) * 4 + tc
                        psc = P5[idxp % 5]
                        if h == 0:
                            wait("tensor", f"{tag}_rope{(g * 2 + 1) * 4 + 3}")
                        if idxp >= 5:
                            pv = idxp - 5
                            wait("tensor", "%s_att%d_%d_%d" % (
                                tag, pv // (4 * H), (pv // 4) % H, pv % 4))
                        op("tensor", lambda e, h=h, tc=tc, psc=psc, hk=hk: e.matmul(
                            psc[:, :], ws(h * 512 + tc * 128, [1, 128]),
                            ap(ckv, hk * 512, [[4096, 128], [1, 512]]),
                            start=True, stop=False))
                        op("tensor", lambda e, psc=psc: e.matmul(
                            psc[:, :], ap(ones1, 0, [[1024, 1], [1, 128]]),
                            ap(ones1, 512, [[1024, 1], [1, 512]]),
                            start=False, stop=True),
                           inc=("sT", 1), tag=f"{tag}_smm{g}_{h}_{tc}")
                        wait("vector", f"{tag}_smm{g}_{h}_{tc}")
                        idx2 = (g * H + h) * 4 + tc
                        p2 = idx2 % 2
                        sm0 = 10 + p2 * 2
                        if idx2 >= 2:
                            pp = idx2 - 2
                            wait("vector", "%s_exp%d_%d_%d" % (
                                tag, pp // (4 * H), (pp // 4) % H, pp % 4))
                        bar("vector", lambda e, psc=psc, sm0=sm0: e.tensor_reduce(
                            small[:, sm0:sm0 + 1], ap(psc, 0, [[512, 128], [1, 512]]),
                            AX.X, ALU.max))
                        bar("vector", lambda e, sm0=sm0: e.tensor_scalar(
                            small[:, sm0:sm0 + 1], small[:, sm0:sm0 + 1],
                            -ISC, None, ALU.mult))
                        op("vector", lambda e: e.engine_nop(), inc=("sV", 1),
                           tag=f"{tag}_negm{g}_{h}_{tc}")
                        wait("scalar", f"{tag}_negm{g}_{h}_{tc}")
                        if idx2 >= 2:
                            pp = idx2 - 2
                            wait("scalar", "%s_att%d_%d_%d" % (
                                tag, pp // (4 * H), (pp // 4) % H, pp % 4))
                        op("scalar", lambda e, psc=psc, p2=p2, sm0=sm0: e.activation(
                            aF(8192 + p2 * 512, [1, 512]), psc[:, :], AF.Exp,
                            bias=small[:, sm0:sm0 + 1], scale=ISC),
                           inc=("sS", 1), tag=f"{tag}_exp{g}_{h}_{tc}")
                        wait("vector", f"{tag}_exp{g}_{h}_{tc}")
                        bar("vector", lambda e, p2=p2, sm0=sm0: e.tensor_reduce(
                            small[:, sm0 + 1:sm0 + 2], aF(8192 + p2 * 512, [1, 512]),
                            AX.X, ALU.add))
                        bar("vector", lambda e, sm0=sm0: e.reciprocal(
                            small[:, sm0 + 1:sm0 + 2], small[:, sm0 + 1:sm0 + 2]))
                        op("vector", lambda e, tc=tc, p2=p2, sm0=sm0: e.tensor_scalar(
                            aB(8192 + tc * 512, [1, 512]), aF(8192 + p2 * 512, [1, 512]),
                            small[:, sm0 + 1:sm0 + 2], None, ALU.mult),
                           inc=("sV", 1), tag=f"{tag}_att{g}_{h}_{tc}")
                    for tc in range(4):
                        wait("tensor", f"{tag}_att{g}_{h}_{tc}")
                        ptr = pt0 if tc % 2 == 0 else pt1
                        if tc >= 2:
                            wait("tensor", f"{tag}_atte{g}_{h}_{tc - 2}")
                        elif h >= 1:
                            wait("tensor", f"{tag}_atte{g}_{h - 1}_{tc + 2}")
                        for blt in range(4):
                            op("tensor", lambda e, tc=tc, blt=blt, ptr=ptr: e.transpose(
                                ptr[:, blt * 128:(blt + 1) * 128],
                                aB(8192 + tc * 512 + blt * 128, [1, 128]),
                                ident[:, :]),
                               inc=(("sT", 1) if blt == 3 else None),
                               tag=(f"{tag}_attt{g}_{h}_{tc}" if blt == 3 else None))
                        wait("vector", f"{tag}_attt{g}_{h}_{tc}")
                        op("vector", lambda e, tc=tc, ptr=ptr: e.tensor_copy(
                            ap(arenaB, 10240 + tc * 128,
                               [[16384, 128], [512, 4], [1, 128]]),
                            ap(ptr, 0, [[1024, 128], [128, 4], [1, 128]])),
                           inc=("sV", 1), tag=f"{tag}_atte{g}_{h}_{tc}")
                    for tc in range(4):
                        wait("tensor", f"{tag}_atte{g}_{h}_{tc}")
                    if h >= 1:
                        wait("tensor", f"{tag}_oev{g}_{h - 1}")
                    for blt in range(4):
                        op("tensor", lambda e, blt=blt, hk=hk: e.matmul(
                            pb[4][:, :],
                            ap(ckv, 2048 + hk * 512 + blt * 128,
                               [[4096, 128], [1, 128]]),
                            ap(arenaB, 10240 + blt * 512, [[16384, 128], [1, 512]]),
                            start=(blt == 0), stop=(blt == 3)),
                           inc=(("sT", 1) if blt == 3 else None),
                           tag=(f"{tag}_avmm{g}_{h}" if blt == 3 else None))
                    wait("vector", f"{tag}_avmm{g}_{h}")
                    op("vector", lambda e, h=h: e.tensor_copy(
                        kv(8192 + h * 512, [1, 512]), pb[4][:, :]),
                       inc=("sV", 1), tag=f"{tag}_oev{g}_{h}")
                for op2 in range(2):
                    for half in range(2):
                        oc = 2 * op2 + half
                        woff = half * 8192
                        wait("gpsimd", f"{tag}_smm{g}_{H - 1}_3")
                        if op2 == 1:
                            wait("gpsimd", f"{tag}_womm{g}_0_3")
                            wait("gpsimd", f"{tag}_womm{g}_0_3a")
                        op("gpsimd", lambda e, oc=oc, woff=woff: e.dma_start(
                            out=ws(woff, [512, ND], [1, 512]),
                            in_=ap(wo_h, oc * 512,
                                   [[D, 128], [D * 128, ND], [1, 512]])),
                           inc=("sWA", 16), tag=f"{tag}_wo{g}_{oc}")
                    for tc in range(4):
                        t = 4 * g + tc
                        po0 = pb[0] if tc % 2 == 0 else pb[2]
                        po1 = pb[1] if tc % 2 == 0 else pb[3]
                        wait("tensor", f"{tag}_wo{g}_{2 * op2}")
                        wait("tensor", f"{tag}_wo{g}_{2 * op2 + 1}")
                        wait("tensor", f"{tag}_oev{g}_{H - 1}")
                        idx = op2 * 4 + tc
                        if idx >= 2:
                            pidx = idx - 2
                            wait("tensor", f"{tag}_woev{g}_{pidx // 4}_{pidx % 4}")
                        for ft in range(ND):
                            op("tensor", lambda e, tc=tc, ft=ft, po0=po0: e.matmul(
                                po0[:, :], kv(8192 + ft * 512 + tc * 128, [1, 128]),
                                ws(0 + ft * 512, [1, 512]),
                                start=(ft == 0), stop=(ft == ND - 1)),
                               inc=(("sT", 1) if ft == ND - 1 else None),
                               tag=(f"{tag}_womm{g}_{op2}_{tc}a" if ft == ND - 1 else None))
                            op("tensor", lambda e, tc=tc, ft=ft, po1=po1: e.matmul(
                                po1[:, :], kv(8192 + ft * 512 + tc * 128, [1, 128]),
                                ws(8192 + ft * 512, [1, 512]),
                                start=(ft == 0), stop=(ft == ND - 1)),
                               inc=(("sT", 1) if ft == ND - 1 else None),
                               tag=(f"{tag}_womm{g}_{op2}_{tc}" if ft == ND - 1 else None))
                        wait("vector", f"{tag}_womm{g}_{op2}_{tc}a")
                        wait("vector", f"{tag}_womm{g}_{op2}_{tc}")
                        xoff = tc * 2048
                        for half, po in ((0, po0), (1, po1)):
                            oc = 2 * op2 + half
                            op("vector", lambda e, po=po, xoff=xoff, oc=oc:
                               e.scalar_tensor_tensor(
                                   aF(xoff + oc * 512, [1, 512]), po[:, :], 1.0,
                                   aF(xoff + oc * 512, [1, 512]), ALU.mult, ALU.add),
                               inc=("sV", 1),
                               tag=f"{tag}_woev{g}_{op2}_{tc}" if half == 1 else None)
                for tc in range(4):
                    t = 4 * g + tc
                    wait("gpsimd", f"{tag}_woev{g}_1_{tc}")
                    op("gpsimd", lambda e, t=t, tc=tc: e.dma_start(
                        out=ap(x_att, t * 128 * D, [[D, 128], [1, D]]),
                        in_=aF(tc * 2048, [1, D])),
                       inc=("sO", 16), tag=f"{tag}_xst{t}")
                op("vector", lambda e: e.engine_nop(), inc=("sV", 1),
                   tag=f"{tag}_wo_done{g}")
            op("vector", lambda e: e.engine_nop(), inc=("sV", 1), tag="at_done")

        def dbg_readback(E):
            op, wait = E.op, E.wait
            end_tag = "f2_dst15" if PH["ffn2"] else "cp3_s15"
            wait("sync", end_tag)
            for ci, (srch, cbase) in enumerate(
                    ((qg_in, 0), (qu_in, 16), (qd_in, 32))):
                for c in range(16):
                    gc = cbase + c
                    par = gc % 2
                    if gc >= 2:
                        wait("sync", f"dbgv{gc - 2}")
                    op("sync", lambda e, c=c, par=par, srch=srch: e.dma_start(
                        out=a8(par * 1024, [1, 1024]),
                        in_=ap(srch, c * 131072, [[1024, 128], [1, 1024]])),
                       inc=("sWL", 16), tag=f"dbgl{gc}")
                    wait("vector", f"dbgl{gc}")
                    if gc >= 2:
                        wait("vector", f"dbgs{gc - 2}")
                    op("vector", lambda e, par=par: e.tensor_copy(
                        aB(par * 1024, [1, 1024]), a8(par * 1024, [1, 1024])),
                       inc=("sV", 1), tag=f"dbgv{gc}")
                    wait("gpsimd", f"dbgv{gc}")
                    op("gpsimd", lambda e, gc=gc, par=par: e.dma_start(
                        out=ap(qdbg, gc * 131072, [[1024, 128], [1, 1024]]),
                        in_=aB(par * 1024, [1, 1024])),
                       inc=("sO", 16), tag=f"dbgs{gc}")
            wait("vector", "dbgs47")
            op("vector", lambda e: e.engine_nop(), inc=("sV", 1), tag="dbg_done")

        env.start("dry")
        program(env)
        with nc.Block() as blk:
            for engname, dec in (("sync", blk.sync), ("vector", blk.vector),
                                 ("scalar", blk.scalar), ("tensor", blk.tensor),
                                 ("gpsimd", blk.gpsimd)):
                def mk(name):
                    def f(e):
                        env.start("emit", name, e)
                        program(env)
                    return f
                dec(mk(engname))

    return nc


_NC_CACHE = {}


def kernel(L0_attn_x, L0_attn_wq, L0_attn_wk, L0_attn_wv, L0_attn_wo,
           L0_attn_wiq, L0_attn_wik, L0_attn_cwa, L0_attn_cwb,
           L0_ffn_w_gate, L0_ffn_w_up, L0_ffn_w_down):
    x = np.ascontiguousarray(np.asarray(L0_attn_x, dtype=np.float32)
                             ).reshape(2 * 8192, D)
    wq = np.ascontiguousarray(np.asarray(L0_attn_wq, dtype=np.float32))
    wk = np.ascontiguousarray(np.asarray(L0_attn_wk, dtype=np.float32))
    wv = np.ascontiguousarray(np.asarray(L0_attn_wv, dtype=np.float32))
    wo = np.ascontiguousarray(np.asarray(L0_attn_wo, dtype=np.float32))
    wiq = np.ascontiguousarray(np.asarray(L0_attn_wiq, dtype=np.float32))
    wik = np.ascontiguousarray(np.asarray(L0_attn_wik, dtype=np.float32))
    cw = np.ascontiguousarray(
        np.stack([np.asarray(L0_attn_cwa, dtype=np.float32),
                  np.asarray(L0_attn_cwb, dtype=np.float32)], axis=0))
    wg = np.ascontiguousarray(np.asarray(L0_ffn_w_gate, dtype=np.float32))
    wu = np.ascontiguousarray(np.asarray(L0_ffn_w_up, dtype=np.float32))
    wd = np.ascontiguousarray(np.asarray(L0_ffn_w_down, dtype=np.float32))

    if "nc" not in _NC_CACHE:
        _NC_CACHE["nc"] = build_program()
    nc = _NC_CACHE["nc"]

    in_maps = []
    for c in range(8):
        slot = c % 4
        in_maps.append({
            "x": np.ascontiguousarray(x[c * TC:(c + 1) * TC]),
            "wq": wq, "wk": wk, "wv": wv, "wo": wo,
            "wiq": wiq, "wik": wik, "cw": cw,
            "wg_shard": np.ascontiguousarray(wg[c * (II // 8):(c + 1) * (II // 8)]),
            "wu_shard": np.ascontiguousarray(wu[c * (II // 8):(c + 1) * (II // 8)]),
            "wd_shard": np.ascontiguousarray(wd[:, c * (II // 8):(c + 1) * (II // 8)]),
            "slot": np.array([[1.0 if slot == 0 else 0.0, float(slot), 0.0, 0.0]],
                             dtype=np.float32),
        })
    res = run_bass_kernel_spmd(nc, in_maps, core_ids=list(range(8)))
    _LAST_RES[0] = res
    out = np.concatenate([np.asarray(r["out"]) for r in res.results], axis=0)
    return out.reshape(2, 8192, D)


# version 07
# speedup vs baseline: 1.0070x; 1.0070x over previous
import numpy as np
import concourse.bass as bass
import concourse.mybir as mybir
from concourse.bass_utils import run_bass_kernel_spmd

F32 = mybir.dt.float32
BF16 = mybir.dt.bfloat16
F8 = mybir.dt.float8e4
I32 = mybir.dt.int32
ALU = mybir.AluOpType
AX = mybir.AxisListType
AF = mybir.ActivationFunctionType

D = 2048
II = 8192
TC = 2048
NT = 16
ND = 16
NI512 = 16
NIT = 64
H, HKV, HD = 16, 4, 128
NBLK = 512
THETA = 10000.0
EPS = 1e-5
ISC = float(1.0 / np.sqrt(128.0))

C250 = 250 << 23
CGA_X = 4 << 23
CGA_I = -(2 << 23)
CLIP_A = 0.119140625
C255 = 255 << 23
CW_BACK = 5 << 23
WCLIP = 6.9921875 / 2.0

PH = {"ffn1": True, "attn": True, "ffn2": True}
DBG = {"dump": False}
_LAST_RES = [None]

SEMS = ["sWL", "sWA", "sXL", "sPL", "sV", "sS", "sT", "sG", "sO", "sC", "sB", "sQ"]


class Env:
    def __init__(self):
        self.handles = None
        self.tags = {}
        self.mode = "dry"
        self.engine = None
        self.eng = None
        self.counts = {}
        self.barn = 0

    def start(self, mode, engine=None, eng=None):
        self.mode = mode
        self.engine = engine
        self.eng = eng
        self.counts = {s: 0 for s in SEMS}
        self.barn = 0

    def op(self, engine, fn, inc=None, tag=None):
        inst = None
        if self.mode == "emit" and engine == self.engine:
            inst = fn(self.eng)
        if inc is not None:
            sem, k = inc
            self.counts[sem] += k
            if tag is not None:
                self.tags[tag] = (sem, self.counts[sem])
            if inst is not None:
                inst.then_inc(self.handles[sem], k)
        return inst

    def wait(self, engine, tag):
        if tag not in self.tags:
            raise KeyError(f"wait on unset tag {tag!r}")
        sem, v = self.tags[tag]
        if v <= 0:
            return
        if self.mode == "emit" and engine == self.engine:
            self.eng.wait_ge(self.handles[sem], v)

    def bar(self, engine, fn):
        self.barn += 1
        t = ("bar", self.barn)
        self.op(engine, fn, inc=("sB", 1), tag=t)
        self.wait(engine, t)

    def settag(self, tag, other):
        self.tags[tag] = self.tags[other]

    def zerotag(self, tag):
        self.tags[tag] = ("sB", 0)


def build_program():
    nc = bass.Bass("TRN2", target_bir_lowering=False, debug=False,
                   detect_race_conditions=False)

    x_in = nc.dram_tensor("x", [TC, D], F32, kind="ExternalInput")
    wq_h = nc.dram_tensor("wq", [D, D], F32, kind="ExternalInput")
    wk_h = nc.dram_tensor("wk", [D, 512], F32, kind="ExternalInput")
    wv_h = nc.dram_tensor("wv", [D, 512], F32, kind="ExternalInput")
    wo_h = nc.dram_tensor("wo", [D, D], F32, kind="ExternalInput")
    wiq_h = nc.dram_tensor("wiq", [D, HD], F32, kind="ExternalInput")
    wik_h = nc.dram_tensor("wik", [D, HD], F32, kind="ExternalInput")
    cw_h = nc.dram_tensor("cw", [2, HD], F32, kind="ExternalInput")
    wg_h = nc.dram_tensor("wg_shard", [II // 8, D], F32, kind="ExternalInput")
    wu_h = nc.dram_tensor("wu_shard", [II // 8, D], F32, kind="ExternalInput")
    wd_h = nc.dram_tensor("wd_shard", [D, II // 8], F32, kind="ExternalInput")
    slot_h = nc.dram_tensor("slot", [1, 4], F32, kind="ExternalInput")
    out_h = nc.dram_tensor("out", [TC, D], F32, kind="ExternalOutput")

    _ko = {"kind": "ExternalOutput"} if DBG["dump"] else {}
    x_mid = nc.dram_tensor("x_mid", [TC, D], F32, **_ko)
    x_att = nc.dram_tensor("x_att", [TC, D], F32, **_ko)
    interq_s = nc.dram_tensor("interq_s", [NT, II, 128], BF16, **_ko)
    qg_in = nc.dram_tensor("qg_in", [2, ND, 128, 512], F8)
    qu_in = nc.dram_tensor("qu_in", [2, ND, 128, 512], F8)
    qd_in = nc.dram_tensor("qd_in", [8, 128, D], F8)
    qg_all = nc.dram_tensor("qg_all", [NI512, ND, 128, 512], F8, addr_space="Shared")
    qu_all = nc.dram_tensor("qu_all", [NI512, ND, 128, 512], F8, addr_space="Shared")
    qd_all = nc.dram_tensor("qd_all", [NIT, 128, D], F8, addr_space="Shared")
    ATO_CK = 0
    ATO_CV = 128 * 512
    ATO_QI = 2 * 128 * 512
    ATO_CS = 3 * 128 * 512
    AT_N = ATO_CS + D
    at_in = nc.dram_tensor("at_in", [AT_N], BF16)
    qdbg = (nc.dram_tensor("qdbg", [48 * 128, 1024], BF16, kind="ExternalOutput")
            if DBG.get("dump_w") else None)
    bounce_f = nc.dram_tensor("bounce_f", [1024], F32)
    bounce_b = nc.dram_tensor("bounce_b", [1024], BF16)
    at_all = nc.dram_tensor("at_all", [4, AT_N], BF16)

    env = Env()

    from contextlib import ExitStack
    with ExitStack() as st:
        ec = st.enter_context
        env.handles = {s: ec(nc.semaphore(s)) for s in SEMS}

        arenaB = ec(nc.sbuf_tensor("arenaB", [128, 16384], BF16))
        arenaF = ec(nc.sbuf_tensor("arenaF", [128, 10240], F32))
        stat = ec(nc.sbuf_tensor("stat", [128, 256], F32))
        statI = ec(nc.sbuf_tensor("statI", [128, 128], I32))
        small = ec(nc.sbuf_tensor("small", [128, 512], F32))
        ident = ec(nc.sbuf_tensor("ident", [128, 128], BF16))
        ones1 = ec(nc.sbuf_tensor("ones1", [128, 1024], BF16))
        indt = ec(nc.sbuf_tensor("indt", [128, 8], BF16))
        with nc.sbuf_tensor("qxT", [128, 32768], F8) as qxT, \
             nc.sbuf_tensor("arena8", [128, 40960], F8) as arena8:
            pass
        with nc.sbuf_tensor("qwd_res", [128, 131072], F8) as qwd_res:
            pass
        with nc.sbuf_tensor("ckv", [128, 4096], BF16) as ckv, \
             nc.sbuf_tensor("tabc", [128, 2048], F32) as tabc, \
             nc.sbuf_tensor("tabs", [128, 2048], F32) as tabs, \
             nc.sbuf_tensor("kvbuf", [128, 16384], BF16) as kvbuf, \
             nc.sbuf_tensor("wstream", [128, 16384], BF16) as wstream:
            pass
        pb = [ec(nc.psum_tensor(f"pb{i}", [128, 512], F32)) for i in range(6)]
        pt0 = ec(nc.psum_tensor("pt0", [128, 1024], BF16))
        pt1 = ec(nc.psum_tensor("pt1", [128, 1024], BF16))
        DBG["h"] = dict(arenaB=arenaB, arenaF=arenaF, stat=stat, statI=statI,
                        small=small, ident=ident, ones1=ones1, indt=indt,
                        qxT=qxT, arena8=arena8, qwd_res=qwd_res, ckv=ckv,
                        tabc=tabc, tabs=tabs, kvbuf=kvbuf, wstream=wstream)

        def ap(h, off, pattern):
            return bass.AP(h, off, pattern)

        def a8(off, *dims):
            return ap(arena8, off, [[40960, 128]] + list(dims))

        def aB(off, *dims):
            return ap(arenaB, off, [[16384, 128]] + list(dims))

        def aF(off, *dims):
            return ap(arenaF, off, [[10240, 128]] + list(dims))

        def kv(off, *dims):
            return ap(kvbuf, off, [[16384, 128]] + list(dims))

        def ws(off, *dims):
            return ap(wstream, off, [[16384, 128]] + list(dims))

        def program(E):
            op, wait, bar = E.op, E.wait, E.bar

            op("gpsimd", lambda e: e.memset(ident[:, :], 1.0))
            op("gpsimd", lambda e: e.affine_select(
                ident[:, :], ident[:, :], [[1, 128]], ALU.is_equal, 0.0,
                base=0, channel_multiplier=-1), inc=("sG", 1), tag="ident")
            op("gpsimd", lambda e: e.memset(ones1[:, :], 1.0), inc=("sG", 1),
               tag="ones")
            op("gpsimd", lambda e: e.iota(
                statI[:, 0:8], [[-16, 8]], channel_multiplier=1,
                allow_small_or_imprecise_dtypes=True),
               inc=("sG", 1), tag="indi")
            wait("vector", "indi")
            bar("vector", lambda e: e.tensor_copy(stat[:, 192:200], statI[:, 0:8]))
            bar("vector", lambda e: e.tensor_scalar(
                stat[:, 208:216], stat[:, 192:200], 0.0, 0.5, ALU.is_ge, ALU.mult))
            bar("vector", lambda e: e.scalar_tensor_tensor(
                stat[:, 192:200], stat[:, 192:200], 16.0, stat[:, 208:216],
                ALU.is_lt, ALU.mult))
            op("vector", lambda e: e.tensor_copy(indt[:, :], stat[:, 192:200]),
               inc=("sV", 1), tag="indt")

            phase_w(E)

            if PH["ffn1"]:
                gphase(E, "f1", x_in, None)
                dphase(E, "f1", x_in, x_mid, None, "ag_d", "f1")
            else:
                copy_dram(E, "cp1", x_in, x_mid, None)
            if PH["attn"]:
                aphase(E, "at")
                if not PH.get("stop_after_a"):
                    bphase(E, "at")
            else:
                copy_dram(E, "cp2", x_mid, x_att,
                          ("f1_dst{t}" if PH["ffn1"] else "cp1_s{t}"),
                          entry=("f1_dev15_3" if PH["ffn1"] else None))
            if PH.get("stop_after_a"):
                v = E.counts["sO"]
                if E.mode == "emit" and E.engine == "sync":
                    E.eng.wait_ge(E.handles["sO"], v)
                return
            if PH["ffn2"]:
                rdy = "at_xst{t}" if PH["attn"] else "cp2_s{t}"
                gphase(E, "f2", x_att, rdy)
                dphase(E, "f2", x_att, out_h, rdy, None, "f2")
            else:
                rdy = "at_xst{t}" if PH["attn"] else "cp2_s{t}"
                copy_dram(E, "cp3", x_att, out_h, rdy,
                          entry=("at_woev3_1_3" if PH["attn"] else "cp2_s15"))
            if DBG.get("dump_w"):
                dbg_readback(E)

            v = E.counts["sO"]
            if E.mode == "emit" and E.engine == "sync":
                E.eng.wait_ge(E.handles["sO"], v)

        def wq_chain(E, key, ncols, prev_tr=None):
            op, wait, bar = E.op, E.wait, E.bar
            nb = ncols // 32
            amax = stat[:, 0:nb]
            mt = statI[:, 0:nb]
            fasc = stat[:, 64:64 + nb]
            gasc = stat[:, 128:128 + nb]
            xs3 = aF(0, [32, nb], [1, 32])
            bar("vector", lambda e: e.tensor_reduce(
                ap(stat, 0, [[256, 128], [1, nb], [0, 1]]), xs3, AX.X,
                ALU.max, apply_absolute_value=True))
            bar("vector", lambda e: e.tensor_scalar(
                mt, amax.bitcast(I32), 0x7F800000, None, ALU.bitwise_and))
            bar("vector", lambda e: e.tensor_scalar(
                fasc.bitcast(I32), mt, -1, C255, ALU.mult, ALU.add))
            bar("vector", lambda e: e.tensor_scalar(
                gasc.bitcast(I32), mt, CW_BACK, None, ALU.add))
            fab = ap(stat, 64, [[256, 128], [1, nb], [0, 32]])
            gab = ap(stat, 128, [[256, 128], [1, nb], [0, 32]])
            t0f = aF(2048, [1, ncols])
            t03 = aF(2048, [32, nb], [1, 32])
            bar("vector", lambda e: e.tensor_tensor(t03, xs3, fab, ALU.mult))
            bar("vector", lambda e: e.tensor_scalar(
                t0f, t0f, WCLIP, -WCLIP, ALU.min, ALU.max))
            op("vector", lambda e: e.engine_nop(), inc=("sV", 1), tag=f"{key}_t0")
            wait("scalar", f"{key}_t0")
            aw = aF(4096, [1, ncols])
            op("scalar", lambda e: e.activation(aw, t0f, AF.Abs),
               inc=("sS", 1), tag=f"{key}_aw")
            wait("vector", f"{key}_aw")
            ge1 = aF(6144, [1, ncols])
            ge2 = aF(8192, [1, ncols])
            s2e = aF(4096, [1, ncols])
            bar("vector", lambda e: e.tensor_scalar(ge1, aw, 1.0, None, ALU.is_ge))
            bar("vector", lambda e: e.tensor_scalar(ge2, aw, 2.0, None, ALU.is_ge))
            bar("vector", lambda e: e.tensor_scalar(s2e, ge1, -0.5, 1.0, ALU.mult, ALU.add))
            bar("vector", lambda e: e.scalar_tensor_tensor(s2e, ge2, -0.25, s2e, ALU.mult, ALU.add))
            bar("vector", lambda e: e.tensor_scalar(s2e, s2e, 2.0 ** -7, None, ALU.mult))
            t8 = a8(0, [1, ncols])
            bar("vector", lambda e: e.tensor_tensor(t8, t0f, s2e, ALU.mult))
            bar("vector", lambda e: e.tensor_scalar(ge1, ge1, 1.0, 1.0, ALU.mult, ALU.add))
            bar("vector", lambda e: e.scalar_tensor_tensor(ge1, ge2, 2.0, ge1, ALU.mult, ALU.add))
            bar("vector", lambda e: e.tensor_scalar(ge1, ge1, 2.0 ** 7, None, ALU.mult))
            bar("vector", lambda e: e.tensor_tensor(t0f, t8, ge1, ALU.mult))
            qwb3 = aB(0, [32, nb], [1, 32])
            if prev_tr is not None:
                wait("vector", prev_tr)
            op("vector", lambda e: e.tensor_tensor(qwb3, t03, gab, ALU.mult),
               inc=("sV", 1), tag=f"{key}_qwb")

        def wq_transposes(E, key, ntiles, prev_store):
            op, wait = E.op, E.wait
            wait("tensor", f"{key}_qwb")
            for ft in range(ntiles):
                pdst = pt0 if ft % 2 == 0 else pt1
                if ft >= 2:
                    wait("tensor", f"{key}_ev{ft - 2}")
                op("tensor", lambda e, ft=ft, pdst=pdst: e.transpose(
                    pdst[:, 0:128], aB(ft * 128, [1, 128]), ident[:, :]),
                   inc=("sT", 1), tag=f"{key}_tr{ft}")
                wait("vector", f"{key}_tr{ft}")
                if ft == 0 and prev_store is not None:
                    wait("vector", prev_store)
                op("vector", lambda e, ft=ft, pdst=pdst: e.tensor_copy(
                    a8(4096 + ft * 128, [1, 128]), pdst[:, 0:128]),
                   inc=("sV", 1), tag=f"{key}_ev{ft}")

        def phase_w(E):
            op, wait = E.op, E.wait
            wait("tensor", "ident")
            E.zerotag("wfree_0")
            prev_free = "wfree_0"
            prev_store = None
            prev_tr = None
            for kind, src, dst in (("g", wg_h, qg_in), ("u", wu_h, qu_in)):
                for j in range(8):
                    key = f"w{kind}{j}"
                    wait("scalar", prev_free)
                    op("scalar", lambda e, src=src, j=j: e.dma_start(
                        out=aF(0, [1, D]),
                        in_=ap(src, j * 128 * D, [[D, 128], [1, D]])),
                       inc=("sXL", 16), tag=f"{key}_ld")
                    wait("vector", f"{key}_ld")
                    wq_chain(E, key, D, prev_tr)
                    prev_free = f"{key}_t0"
                    wq_transposes(E, key, ND, prev_store)
                    wait("gpsimd", f"{key}_ev{ND - 1}")
                    op("gpsimd", lambda e, j=j, dst=dst: e.dma_start(
                        out=ap(dst, (j // 4) * (ND * 128 * 512) + (j % 4) * 128,
                               [[512, 128], [128 * 512, ND], [1, 128]]),
                        in_=a8(4096, [128, ND], [1, 128])),
                       inc=("sO", 16), tag=f"{key}_st")
                    prev_store = f"{key}_st"
                    prev_tr = f"{key}_tr{ND - 1}"
            for j in range(16):
                key = f"wd{j}"
                wait("scalar", prev_free)
                op("scalar", lambda e, j=j: e.dma_start(
                    out=aF(0, [1, 1024]),
                    in_=ap(wd_h, j * 128 * 1024, [[1024, 128], [1, 1024]])),
                   inc=("sXL", 16), tag=f"{key}_ld")
                wait("vector", f"{key}_ld")
                wq_chain(E, key, 1024, prev_tr)
                prev_free = f"{key}_t0"
                wq_transposes(E, key, 8, prev_store)
                wait("gpsimd", f"{key}_ev7")
                op("gpsimd", lambda e, j=j: e.dma_start(
                    out=ap(qd_in, j * 128, [[D, 128], [128 * D, 8], [1, 128]]),
                    in_=a8(4096, [128, 8], [1, 128])),
                   inc=("sO", 16), tag=f"{key}_st")
                prev_store = f"{key}_st"
                prev_tr = f"{key}_tr7"
            for t in ("wg7_st", "wu7_st", "wd15_st"):
                wait("gpsimd", t)
            if PH.get("dly_pre_ag"):
                for _ in range(PH["dly_pre_ag"]):
                    op("gpsimd", lambda e: e.engine_nop())
            op("gpsimd", lambda e: e.collective_compute(
                "AllGather", ALU.bypass, replica_groups=[list(range(8))],
                ins=[qg_in.ap().opt()], outs=[qg_all.ap().opt()]),
               inc=("sC", 1), tag="ag_g")
            op("gpsimd", lambda e: e.collective_compute(
                "AllGather", ALU.bypass, replica_groups=[list(range(8))],
                ins=[qu_in.ap().opt()], outs=[qu_all.ap().opt()]),
               inc=("sC", 1), tag="ag_u")
            op("gpsimd", lambda e: e.collective_compute(
                "AllGather", ALU.bypass, replica_groups=[list(range(8))],
                ins=[qd_in.ap().opt()], outs=[qd_all.ap().opt()]),
               inc=("sC", 1), tag="ag_d")

        def ln_stats(E, key, xoff, save_col=None, prev_user=None):
            op, wait, bar = E.op, E.wait, E.bar
            if prev_user is not None:
                E.wait("vector", prev_user)
            xf = aF(xoff, [1, D])
            scr = aF(8192, [1, D])
            op("scalar", lambda e: e.activation(
                scr, xf, AF.Identity, accum_out=small[:, 2:3]), inc=("sS", 1))
            op("scalar", lambda e: e.activation(
                scr, xf, AF.Square, accum_out=small[:, 3:4]),
               inc=("sS", 1), tag=f"{key}_sums")
            wait("vector", f"{key}_sums")
            bar("vector", lambda e: e.tensor_scalar(
                small[:, 4:5], small[:, 2:3], 1.0 / D, None, ALU.mult))
            bar("vector", lambda e: e.tensor_tensor(
                small[:, 5:6], small[:, 4:5], small[:, 4:5], ALU.mult))
            bar("vector", lambda e: e.scalar_tensor_tensor(
                small[:, 6:7], small[:, 3:4], 1.0 / D, small[:, 5:6],
                ALU.mult, ALU.subtract))
            bar("vector", lambda e: e.tensor_scalar(
                small[:, 6:7], small[:, 6:7], EPS, None, ALU.add))
            op("vector", lambda e: e.engine_nop(), inc=("sV", 1), tag=f"{key}_var")
            wait("scalar", f"{key}_var")
            op("scalar", lambda e: e.activation(
                small[:, 7:8], small[:, 6:7], AF.Sqrt), inc=("sS", 1), tag=f"{key}_sd")
            wait("vector", f"{key}_sd")
            bar("vector", lambda e: e.reciprocal(small[:, 8:9], small[:, 7:8]))
            bar("vector", lambda e: e.tensor_tensor(
                small[:, 9:10], small[:, 8:9], small[:, 8:9], ALU.mult))
            bar("vector", lambda e: e.tensor_tensor(
                small[:, 9:10], small[:, 9:10], small[:, 6:7], ALU.mult))
            bar("vector", lambda e: e.tensor_scalar(
                small[:, 9:10], small[:, 9:10], -0.5, 1.5, ALU.mult, ALU.add))
            bar("vector", lambda e: e.tensor_tensor(
                small[:, 0:1], small[:, 8:9], small[:, 9:10], ALU.mult))
            bar("vector", lambda e: e.scalar_tensor_tensor(
                small[:, 1:2], small[:, 4:5], -1.0, small[:, 0:1],
                ALU.mult, ALU.mult))
            if save_col is not None:
                bar("vector", lambda e, c=save_col: e.tensor_copy(
                    small[:, 16 + c:17 + c], small[:, 0:1]))
                bar("vector", lambda e, c=save_col: e.tensor_copy(
                    small[:, 32 + c:33 + c], small[:, 1:2]))
            op("vector", lambda e: e.engine_nop(), inc=("sV", 1), tag=f"{key}_ab")

        def act_quant(E, tag, srcoff, n, dst_fn, cga, tqoff=7168):
            op, wait, bar = E.op, E.wait, E.bar
            nb = n // 32
            amax = stat[:, 192:192 + nb]
            mt = statI[:, 64:64 + nb]
            fasc = stat[:, 0:nb]
            gasc = stat[:, 64:64 + nb]
            src3 = aF(srcoff, [32, nb], [1, 32])
            bar("vector", lambda e: e.tensor_reduce(
                ap(stat, 192, [[256, 128], [1, nb], [0, 1]]), src3, AX.X,
                ALU.max, apply_absolute_value=True))
            bar("vector", lambda e: e.tensor_scalar(
                mt, amax.bitcast(I32), 0x7F800000, None, ALU.bitwise_and))
            bar("vector", lambda e: e.tensor_scalar(
                fasc.bitcast(I32), mt, -1, C250, ALU.mult, ALU.add))
            bar("vector", lambda e: e.tensor_scalar(
                gasc.bitcast(I32), mt, cga, None, ALU.add))
            fab = ap(stat, 0, [[256, 128], [1, nb], [0, 32]])
            gab = ap(stat, 64, [[256, 128], [1, nb], [0, 32]])
            tq3 = aF(tqoff, [32, nb], [1, 32])
            tqf = aF(tqoff, [1, n])
            bar("vector", lambda e: e.tensor_tensor(tq3, src3, fab, ALU.mult))
            t8 = a8(32768, [1, n])
            t83 = a8(32768, [32, nb], [1, 32])
            bar("vector", lambda e: e.tensor_scalar(
                t8, tqf, CLIP_A, -CLIP_A, ALU.min, ALU.max))
            op("vector", lambda e: e.tensor_tensor(dst_fn(), t83, gab, ALU.mult),
               inc=("sV", 1), tag=tag)

        def gphase(E, tag, src_h, rdy):
            op, wait, bar = E.op, E.wait, E.bar
            if tag == "f2":
                for eng in ("scalar", "vector", "tensor", "sync", "gpsimd"):
                    wait(eng, "at_done" if PH["attn"] else "cp2_s15")
            for t in range(NT):
                xoff = 0 if t % 2 == 0 else 2048
                if t >= 2:
                    wait("scalar", f"{tag}_pfree{t - 2}")
                if rdy is not None:
                    wait("scalar", rdy.format(t=t))
                if t == 0 and tag == "f1":
                    wait("scalar", "wd15_qwb")
                op("scalar", lambda e, t=t, xoff=xoff: e.dma_start(
                    out=aF(xoff, [1, D]),
                    in_=ap(src_h, t * 128 * D, [[D, 128], [1, D]])),
                   inc=("sXL", 16), tag=f"{tag}_xl{t}")
                wait("scalar", f"{tag}_xl{t}")
                wait("vector", f"{tag}_xl{t}")
                ln_stats(E, f"{tag}_ln{t}", xoff,
                         prev_user=(f"{tag}_xln{t - 1}" if t >= 1 else None))
                wait("scalar", f"{tag}_ln{t}_ab")
                if t >= 1:
                    wait("scalar", f"{tag}_qx{t - 1}")
                op("scalar", lambda e, xoff=xoff: e.activation(
                    aF(4096, [1, D]), aF(xoff, [1, D]),
                    AF.Identity, bias=small[:, 1:2], scale=small[:, 0:1]),
                   inc=("sS", 1), tag=f"{tag}_xln{t}")
                E.settag(f"{tag}_pfree{t}", f"{tag}_xln{t}")
                wait("vector", f"{tag}_xln{t}")
                if t >= 1:
                    wait("vector", f"{tag}_qxt{t - 1}_{ND - 1}")
                act_quant(E, f"{tag}_qx{t}", 4096, D,
                          lambda: aB(0, [32, 64], [1, 32]), CGA_X, tqoff=6144)
                wait("tensor", f"{tag}_qx{t}")
                for ft in range(ND):
                    pdst = pt0 if ft % 2 == 0 else pt1
                    if ft >= 2:
                        wait("tensor", f"{tag}_qxe{t}_{ft - 2}")
                    elif t >= 1:
                        wait("tensor", f"{tag}_qxe{t - 1}_{ND - 2 + ft}")
                    op("tensor", lambda e, ft=ft, pdst=pdst: e.transpose(
                        pdst[:, 0:128], aB(ft * 128, [1, 128]),
                        ident[:, :]), inc=("sT", 1), tag=f"{tag}_qxt{t}_{ft}")
                    wait("vector", f"{tag}_qxt{t}_{ft}")
                    op("vector", lambda e, t=t, ft=ft, pdst=pdst:
                       e.tensor_copy(
                           ap(qxT, ft * TC + t * 128, [[32768, 128], [1, 128]]),
                           pdst[:, 0:128]),
                       inc=("sV", 1), tag=f"{tag}_qxe{t}_{ft}")
            def iq_stage(gi):
                isub, t = gi // NT, gi % NT
                par = gi % 2
                ptr = pt0 if par == 0 else pt1
                wait("tensor", f"{tag}_qi{isub}_{t}")
                if gi >= 2:
                    pi = gi - 2
                    wait("tensor", f"{tag}_iqe{pi // NT}_{pi % NT}")
                else:
                    wait("tensor", f"{tag}_qxe{NT - 1}_{ND - 2 + par}")
                for it in range(4):
                    op("tensor", lambda e, it=it, ptr=ptr, par=par: e.transpose(
                        ptr[:, it * 128:(it + 1) * 128],
                        aB(14336 + par * 512 + it * 128, [1, 128]), ident[:, :]),
                       inc=(("sT", 1) if it == 3 else None),
                       tag=(f"{tag}_iqt{isub}_{t}" if it == 3 else None))
                wait("vector", f"{tag}_iqt{isub}_{t}")
                if gi >= 2:
                    pi = gi - 2
                    wait("vector", f"{tag}_iqs{pi // NT}_{pi % NT}")
                op("vector", lambda e, ptr=ptr, par=par: e.tensor_copy(
                    aB(15360 + par * 512, [1, 512]), ptr[:, 0:512]),
                   inc=("sV", 1), tag=f"{tag}_iqe{isub}_{t}")
                wait("gpsimd", f"{tag}_iqe{isub}_{t}")
                op("gpsimd", lambda e, isub=isub, t=t, par=par: e.dma_start(
                    out=ap(interq_s, t * II * 128 + isub * 512 * 128,
                           [[128, 128], [128 * 128, 4], [1, 128]]),
                    in_=aB(15360 + par * 512, [128, 4], [1, 128])),
                   inc=("sO", 16), tag=f"{tag}_iqs{isub}_{t}")

            for isub in range(NI512):
                parw = isub % 2
                goff = parw * 8192
                uoff = 16384 + parw * 8192
                if isub >= 2:
                    wait("sync", f"{tag}_wfree{isub - 2}")
                if isub == 0:
                    wait("sync", "ag_g")
                    wait("sync", "ag_u")
                    if PH.get("dly_post_ag"):
                        for _ in range(PH["dly_post_ag"]):
                            op("sync", lambda e: e.nop())
                    wait("sync", f"{tag}_qxe0_{ND - 1}")
                op("sync", lambda e, isub=isub, goff=goff: e.dma_start(
                    out=a8(goff, [1, 8192]),
                    in_=ap(qg_all, isub * ND * 128 * 512,
                           [[512, 128], [128 * 512, ND], [1, 512]])),
                   inc=("sWL", 16), tag=f"{tag}_wg{isub}")
                op("sync", lambda e, isub=isub, uoff=uoff: e.dma_start(
                    out=a8(uoff, [1, 8192]),
                    in_=ap(qu_all, isub * ND * 128 * 512,
                           [[512, 128], [128 * 512, ND], [1, 512]])),
                   inc=("sWL", 16), tag=f"{tag}_wu{isub}")
                for t in range(NT):
                    gi = isub * NT + t
                    par = gi % 2
                    b = (gi % 3) * 2
                    pg, pu = pb[b], pb[b + 1]
                    wait("tensor", f"{tag}_wg{isub}")
                    wait("tensor", f"{tag}_wu{isub}")
                    if isub == 0:
                        wait("tensor", f"{tag}_qxe{t}_{ND - 1}")
                    if gi >= 3:
                        pi = gi - 3
                        wait("tensor", f"{tag}_int{pi // NT}_{pi % NT}")
                    for ft in range(ND):
                        op("tensor", lambda e, t=t, ft=ft, pg=pg, goff=goff: e.matmul(
                            pg[:, :],
                            ap(qxT, ft * TC + t * 128, [[32768, 128], [1, 128]]),
                            a8(goff + ft * 512, [1, 512]),
                            start=(ft == 0), stop=(ft == ND - 1)),
                           inc=(("sT", 1) if ft == ND - 1 else None),
                           tag=(f"{tag}_mmg{isub}_{t}" if ft == ND - 1 else None))
                        op("tensor", lambda e, t=t, ft=ft, pu=pu, uoff=uoff: e.matmul(
                            pu[:, :],
                            ap(qxT, ft * TC + t * 128, [[32768, 128], [1, 128]]),
                            a8(uoff + ft * 512, [1, 512]),
                            start=(ft == 0), stop=(ft == ND - 1)),
                           inc=(("sT", 1) if ft == ND - 1 else None),
                           tag=(f"{tag}_mmu{isub}_{t}" if ft == ND - 1 else None))
                    if t == NT - 1:
                        E.settag(f"{tag}_wfree{isub}", f"{tag}_mmu{isub}_{t}")
                    wait("scalar", f"{tag}_mmg{isub}_{t}")
                    if gi >= 2:
                        pi = gi - 2
                        wait("scalar", f"{tag}_int{pi // NT}_{pi % NT}")
                    op("scalar", lambda e, pg=pg, par=par: e.activation(
                        aF(6144 + par * 512, [1, 512]), pg[:, :], AF.Silu,
                        scale=2.0 ** -6),
                       inc=("sS", 1), tag=f"{tag}_silu{isub}_{t}")
                    wait("vector", f"{tag}_silu{isub}_{t}")
                    wait("vector", f"{tag}_mmu{isub}_{t}")
                    bar("vector", lambda e, pu=pu, par=par: e.tensor_tensor(
                        aF(7168 + par * 512, [1, 512]), aF(6144 + par * 512, [1, 512]),
                        pu[:, :], ALU.mult))
                    op("vector", lambda e: e.engine_nop(), inc=("sV", 1),
                       tag=f"{tag}_int{isub}_{t}")
                    if gi >= 2:
                        pi = gi - 2
                        wait("vector", f"{tag}_iqt{pi // NT}_{pi % NT}")
                    act_quant(E, f"{tag}_qi{isub}_{t}", 7168 + par * 512, 512,
                              lambda par=par: aB(14336 + par * 512, [32, 16], [1, 32]),
                              CGA_I, tqoff=8192)
                    if gi >= 1:
                        iq_stage(gi - 1)
            iq_stage(NI512 * NT - 1)

        def dphase(E, tag, src_h, dst_h, rdy, agtag, gtag):
            op, wait, bar = E.op, E.wait, E.bar
            wait("sync", f"{gtag}_mmu{NI512 - 1}_{NT - 1}")
            wait("sync", f"{gtag}_qi{NI512 - 1}_{NT - 1}")
            if agtag:
                wait("sync", agtag)
            for r in range(8):
                op("sync", lambda e, r=r: e.dma_start(
                    out=ap(qwd_res, r * 8 * D, [[131072, 128], [D, 8], [1, D]]),
                    in_=ap(qd_all, r * 8 * 128 * D, [[D, 128], [128 * D, 8], [1, D]])),
                   inc=("sWL", 16), tag=f"{tag}_dw{r}")
            for t in range(NT):
                poff = (t % 2) * 8192
                if t >= 2:
                    wait("scalar", f"{tag}_plfree{t - 2}")
                if t < 2:
                    wait("scalar", f"{gtag}_iqs{NI512 - 1}_{NT - 1}")
                else:
                    wait("scalar", f"{gtag}_iqs{NI512 - 1}_{t}")
                op("scalar", lambda e, t=t, poff=poff: e.dma_start(
                    out=aB(poff, [1, 8192]),
                    in_=ap(interq_s, t * II * 128,
                           [[128, 128], [128 * 128, NIT], [1, 128]])),
                   inc=("sPL", 16), tag=f"{tag}_pl{t}")
                xoff = 0 if t % 2 == 0 else 2048
                if t >= 2:
                    wait("scalar", f"{tag}_xrfree{t - 2}")
                if rdy is not None:
                    wait("scalar", rdy.format(t=t))
                op("scalar", lambda e, t=t, xoff=xoff: e.dma_start(
                    out=aF(xoff, [1, D]),
                    in_=ap(src_h, t * 128 * D, [[D, 128], [1, D]])),
                   inc=("sXL", 16), tag=f"{tag}_xr{t}")
                wait("tensor", f"{tag}_pl{t}")
                if t == 0:
                    for r in range(8):
                        wait("tensor", f"{tag}_dw{r}")
                banks = [(4 * t + k) % 6 for k in range(4)]
                for k in range(4):
                    n = 4 * t + k - 6
                    if n >= 0:
                        wait("tensor", f"{tag}_dev{n // 4}_{n % 4}")
                for it in range(NIT):
                    for k in range(4):
                        bk = pb[banks[k]]
                        op("tensor", lambda e, it=it, k=k, bk=bk, poff=poff: e.matmul(
                            bk[:, :], aB(poff + it * 128, [1, 128]),
                            ap(qwd_res, it * D + k * 512, [[131072, 128], [1, 512]]),
                            start=(it == 0), stop=(it == NIT - 1)),
                           inc=(("sT", 1) if it == NIT - 1 else None),
                           tag=(f"{tag}_dmm{t}_{k}" if it == NIT - 1 else None))
                E.settag(f"{tag}_plfree{t}", f"{tag}_dmm{t}_3")
                ooff = 4096 if t % 2 == 0 else 6144
                for k in range(4):
                    bk = pb[banks[k]]
                    wait("vector", f"{tag}_dmm{t}_{k}")
                    if k == 0:
                        wait("vector", f"{tag}_xr{t}")
                        if t >= 2:
                            wait("vector", f"{tag}_dst{t - 2}")
                    op("vector", lambda e, k=k, bk=bk, xoff=xoff, ooff=ooff:
                       e.scalar_tensor_tensor(
                           aF(ooff + k * 512, [1, 512]), bk[:, :],
                           0.5 * 2.0 ** -6, aF(xoff + k * 512, [1, 512]),
                           ALU.mult, ALU.add),
                       inc=("sV", 1), tag=f"{tag}_dev{t}_{k}")
                E.settag(f"{tag}_xrfree{t}", f"{tag}_dev{t}_3")
                wait("gpsimd", f"{tag}_dev{t}_3")
                op("gpsimd", lambda e, t=t, ooff=ooff: e.dma_start(
                    out=ap(dst_h, t * 128 * D, [[D, 128], [1, D]]),
                    in_=aF(ooff, [1, D])),
                   inc=("sO", 16), tag=f"{tag}_dst{t}")

        def copy_dram(E, tag, src, dst, rdy, entry=None):
            op, wait = E.op, E.wait
            for t in range(NT):
                if t == 0 and tag == "cp1":
                    wait("scalar", "wd15_qwb")
                if t < 2 and entry is not None:
                    wait("scalar", entry)
                if t >= 2:
                    wait("scalar", f"{tag}_s{t - 2}")
                if rdy is not None:
                    wait("scalar", rdy.format(t=t))
                xoff = 0 if t % 2 == 0 else 2048
                op("scalar", lambda e, t=t, xoff=xoff: e.dma_start(
                    out=aF(xoff, [1, D]),
                    in_=ap(src, t * 128 * D, [[D, 128], [1, D]])),
                   inc=("sXL", 16), tag=f"{tag}_l{t}")
                wait("gpsimd", f"{tag}_l{t}")
                op("gpsimd", lambda e, t=t, xoff=xoff: e.dma_start(
                    out=ap(dst, t * 128 * D, [[D, 128], [1, D]]),
                    in_=aF(xoff, [1, D])),
                   inc=("sO", 16), tag=f"{tag}_s{t}")

        def aphase(E, tag):
            op, wait, bar = E.op, E.wait, E.bar
            fdone = "f1_dmm15_3" if PH["ffn1"] else "cp1_s15"
            for eng in ("gpsimd", "scalar", "vector", "tensor", "sync"):
                wait(eng, fdone)
            xrdy = "f1_dst{t}" if PH["ffn1"] else "cp1_s{t}"
            op("gpsimd", lambda e: e.iota(
                statI[:, 0:128], [[1, 64], [0, 2]], channel_multiplier=0,
                allow_small_or_imprecise_dtypes=True),
               inc=("sG", 1), tag=f"{tag}_iotap")
            wait("vector", f"{tag}_iotap")
            bar("vector", lambda e: e.tensor_copy(aF(5760, [1, 128]), statI[:, 0:128]))
            op("vector", lambda e: e.engine_nop(), inc=("sV", 1), tag=f"{tag}_j2")
            wait("scalar", f"{tag}_j2")
            op("scalar", lambda e: e.activation(
                aF(5504, [1, 128]), aF(5760, [1, 128]),
                AF.Exp, scale=-float(np.log(THETA)) / 64.0),
               inc=("sS", 1), tag=f"{tag}_freqs")
            wait("gpsimd", f"{tag}_j2")
            op("gpsimd", lambda e: e.iota(
                statI[:, 0:1], [[0, 1]], channel_multiplier=1,
                allow_small_or_imprecise_dtypes=True),
               inc=("sG", 1), tag=f"{tag}_iotac")
            op("scalar", lambda e: e.dma_start(
                out=ap(small, 40, [[512, 1], [1, 4]]), in_=slot_h.ap()),
               inc=("sXL", 16), tag=f"{tag}_slot")
            wait("vector", f"{tag}_slot")
            bar("vector", lambda e: e.tensor_copy(
                ap(arenaB, 4300, [[16384, 1], [1, 4]]),
                ap(small, 40, [[512, 1], [1, 4]])))
            op("vector", lambda e: e.engine_nop(), inc=("sV", 1),
               tag=f"{tag}_slotbf")
            wait("tensor", f"{tag}_slotbf")
            wait("tensor", "ones")
            op("tensor", lambda e: e.matmul(
                pb[3][:, 0:4], ap(ones1, 0, [[1024, 1], [1, 128]]),
                ap(arenaB, 4300, [[16384, 1], [1, 4]]),
                start=True, stop=True), inc=("sT", 1), tag=f"{tag}_slotmm")
            wait("vector", f"{tag}_slotmm")
            op("vector", lambda e: e.tensor_copy(
                ap(small, 44, [[512, 128], [1, 4]]), pb[3][:, 0:4]),
               inc=("sV", 1), tag=f"{tag}_slotb")
            wait("vector", f"{tag}_slotb")
            wait("vector", f"{tag}_iotac")
            wait("vector", f"{tag}_freqs")
            bar("vector", lambda e: e.tensor_copy(small[:, 48:49], statI[:, 0:1]))
            op("vector", lambda e: e.scalar_tensor_tensor(
                small[:, 49:50], small[:, 45:46], 2048.0, small[:, 48:49],
                ALU.mult, ALU.add), inc=("sV", 1), tag=f"{tag}_pos")
            wait("gpsimd", f"{tag}_j2")
            wait("gpsimd", f"{tag}_pos")
            op("gpsimd", lambda e: e.iota(
                statI[:, 0:128], [[0, 64], [1, 2]], channel_multiplier=0,
                allow_small_or_imprecise_dtypes=True),
               inc=("sG", 1), tag=f"{tag}_sgni")
            wait("vector", f"{tag}_sgni")
            bar("vector", lambda e: e.tensor_copy(aF(5632, [1, 128]), statI[:, 0:128]))
            op("vector", lambda e: e.tensor_scalar(
                aF(5632, [1, 128]), aF(5632, [1, 128]), 2.0, -1.0,
                ALU.mult, ALU.add), inc=("sV", 1), tag=f"{tag}_sgnr")
            c1 = float(np.float32(6.28125))
            c2 = float(np.float32(0.001934051513671875))
            c3 = float(2 * np.pi) - c1 - c2

            def rope_tab(t):
                AA, BB, CC, DD = 5760, 5888, 6016, 6144
                if t == 0:
                    wait("vector", f"{tag}_sgnr")
                    wait("vector", f"{tag}_freqs")
                bar("vector", lambda e, t=t: e.tensor_scalar(
                    small[:, 50:51], small[:, 49:50], float(128 * t), None, ALU.add))
                bar("vector", lambda e: e.tensor_scalar(
                    aF(AA, [1, 128]), aF(5504, [1, 128]), small[:, 50:51],
                    None, ALU.mult))
                bar("vector", lambda e: e.tensor_scalar(
                    aF(BB, [1, 128]), aF(AA, [1, 128]),
                    float(1.0 / (2 * np.pi)), None, ALU.mult))
                bar("vector", lambda e: e.tensor_copy(
                    statI[:, 0:128], aF(BB, [1, 128])))
                bar("vector", lambda e: e.tensor_copy(
                    aF(BB, [1, 128]), statI[:, 0:128]))
                bar("vector", lambda e: e.scalar_tensor_tensor(
                    aF(CC, [1, 128]), aF(BB, [1, 128]), -c1,
                    aF(AA, [1, 128]), ALU.mult, ALU.add))
                bar("vector", lambda e: e.scalar_tensor_tensor(
                    aF(CC, [1, 128]), aF(BB, [1, 128]), -c2,
                    aF(CC, [1, 128]), ALU.mult, ALU.add))
                bar("vector", lambda e: e.scalar_tensor_tensor(
                    aF(CC, [1, 128]), aF(BB, [1, 128]), -c3,
                    aF(CC, [1, 128]), ALU.mult, ALU.add))
                bar("vector", lambda e: e.tensor_scalar(
                    aF(DD, [1, 128]), aF(CC, [1, 128]), 0.0, None, ALU.add))
                bar("vector", lambda e: e.tensor_scalar(
                    aF(AA, [1, 128]), aF(DD, [1, 128]), -float(np.pi), None, ALU.is_lt))
                bar("vector", lambda e: e.tensor_scalar(
                    aF(BB, [1, 128]), aF(DD, [1, 128]), float(np.pi), None, ALU.is_gt))
                bar("vector", lambda e: e.tensor_tensor(
                    aF(AA, [1, 128]), aF(AA, [1, 128]), aF(BB, [1, 128]), ALU.subtract))
                op("vector", lambda e: e.scalar_tensor_tensor(
                    aF(DD, [1, 128]), aF(AA, [1, 128]), float(2 * np.pi),
                    aF(DD, [1, 128]), ALU.mult, ALU.add),
                   inc=("sV", 1), tag=f"{tag}_wr0{t}")
                wait("scalar", f"{tag}_wr0{t}")
                op("scalar", lambda e: e.activation(
                    aF(AA, [1, 128]), aF(DD, [1, 128]), AF.Sin),
                   inc=("sS", 1), tag=f"{tag}_sin0{t}")
                bar("vector", lambda e: e.tensor_scalar(
                    aF(BB, [1, 128]), aF(CC, [1, 128]), float(np.pi / 2), None, ALU.add))
                wait("vector", f"{tag}_sin0{t}")
                bar("vector", lambda e: e.tensor_scalar(
                    aF(CC, [1, 128]), aF(BB, [1, 128]), -float(np.pi), None, ALU.is_lt))
                bar("vector", lambda e: e.tensor_scalar(
                    aF(DD, [1, 128]), aF(BB, [1, 128]), float(np.pi), None, ALU.is_gt))
                bar("vector", lambda e: e.tensor_tensor(
                    aF(CC, [1, 128]), aF(CC, [1, 128]), aF(DD, [1, 128]), ALU.subtract))
                op("vector", lambda e: e.scalar_tensor_tensor(
                    aF(BB, [1, 128]), aF(CC, [1, 128]), float(2 * np.pi),
                    aF(BB, [1, 128]), ALU.mult, ALU.add),
                   inc=("sV", 1), tag=f"{tag}_wr1{t}")
                wait("scalar", f"{tag}_wr1{t}")
                op("scalar", lambda e, t=t: e.activation(
                    tabc[:, t * 128:(t + 1) * 128], aF(BB, [1, 128]), AF.Sin),
                   inc=("sS", 1), tag=f"{tag}_tab{t}")
                op("vector", lambda e, t=t: e.tensor_tensor(
                    tabs[:, t * 128:(t + 1) * 128], aF(AA, [1, 128]),
                    aF(5632, [1, 128]), ALU.mult),
                   inc=("sV", 1), tag=f"{tag}_tabs{t}")
            op("gpsimd", lambda e: e.dma_start(
                out=ws(0, [512, ND], [1, 512]),
                in_=ap(wk_h, 0, [[512, 128], [512 * 128, ND], [1, 512]])),
               inc=("sWA", 16), tag=f"{tag}_wk")
            op("gpsimd", lambda e: e.dma_start(
                out=ws(8192, [512, ND], [1, 512]),
                in_=ap(wv_h, 0, [[512, 128], [512 * 128, ND], [1, 512]])),
               inc=("sWA", 16), tag=f"{tag}_wv")
            op("gpsimd", lambda e: e.dma_start(
                out=ap(ckv, 0, [[4096, 128], [128, ND], [1, 128]]),
                in_=ap(wiq_h, 0, [[128, 128], [128 * 128, ND], [1, 128]])),
               inc=("sWA", 16), tag=f"{tag}_wiq")
            op("scalar", lambda e: e.dma_start(
                out=ap(small, 56, [[512, 1], [1, 256]]),
                in_=ap(cw_h, 0, [[256, 1], [1, 256]])),
               inc=("sXL", 16), tag=f"{tag}_cw")
            wait("vector", f"{tag}_cw")
            bar("vector", lambda e: e.tensor_copy(
                ap(arenaB, 4096, [[16384, 1], [1, 256]]),
                ap(small, 56, [[512, 1], [1, 256]])))
            op("vector", lambda e: e.engine_nop(), inc=("sV", 1), tag=f"{tag}_cwb")
            wait("tensor", f"{tag}_cwb")
            wait("tensor", "ones")
            op("tensor", lambda e: e.matmul(
                pb[2][:, 0:128], ap(ones1, 0, [[1024, 1], [1, 128]]),
                ap(arenaB, 4096, [[16384, 1], [1, 128]]),
                start=True, stop=True), inc=("sT", 1))
            op("tensor", lambda e: e.matmul(
                pb[2][:, 128:256], ap(ones1, 0, [[1024, 1], [1, 128]]),
                ap(arenaB, 4096 + 128, [[16384, 1], [1, 128]]),
                start=True, stop=True), inc=("sT", 1), tag=f"{tag}_cwt")
            wait("vector", f"{tag}_cwt")
            op("vector", lambda e: e.tensor_copy(
                aF(5120, [1, 256]), pb[2][:, 0:256]),
               inc=("sV", 1), tag=f"{tag}_cwtile")
            for t in range(NT):
                xoff = 0 if t % 2 == 0 else 2048
                if t >= 2:
                    wait("scalar", f"{tag}_afree{t - 2}")
                if t < 2 and PH["ffn1"]:
                    wait("scalar", "f1_dev15_3")
                wait("scalar", xrdy.format(t=t))
                op("scalar", lambda e, t=t, xoff=xoff: e.dma_start(
                    out=aF(xoff, [1, D]),
                    in_=ap(x_mid, t * 128 * D, [[D, 128], [1, D]])),
                   inc=("sXL", 16), tag=f"{tag}_xl{t}")
                rope_tab(t)
                wait("scalar", f"{tag}_xl{t}")
                wait("vector", f"{tag}_xl{t}")
                ln_stats(E, f"{tag}_ln{t}", xoff, save_col=t,
                         prev_user=(f"{tag}_xln{t - 1}" if t >= 1 else None))
                wait("scalar", f"{tag}_ln{t}_ab")
                if t >= 1:
                    wait("scalar", f"{tag}_xtt{t - 1}_{ND - 1}")
                op("scalar", lambda e, xoff=xoff: e.activation(
                    aB(0, [1, D]), aF(xoff, [1, D]),
                    AF.Identity, bias=small[:, 1:2], scale=small[:, 0:1]),
                   inc=("sS", 1), tag=f"{tag}_xln{t}")
                E.settag(f"{tag}_afree{t}", f"{tag}_xln{t}")
                wait("tensor", f"{tag}_xln{t}")
                for ft in range(ND):
                    op("tensor", lambda e, ft=ft, t=t: e.matmul(
                        pb[5][:, ft * 16 + t:ft * 16 + t + 1], aB(ft * 128, [1, 128]),
                        ones1[:, 0:1], start=True, stop=True),
                       inc=(("sT", 1) if (t == NT - 1 and ft == ND - 1) else None),
                       tag=(f"{tag}_csmm" if (t == NT - 1 and ft == ND - 1) else None))
                for ft in range(ND):
                    pdst = pt0 if ft % 2 == 0 else pt1
                    if ft >= 2:
                        wait("tensor", f"{tag}_xte{t}_{ft - 2}")
                    elif t >= 1:
                        wait("tensor", f"{tag}_xte{t - 1}_{ND - 2 + ft}")
                        if ft == 0:
                            wait("tensor", f"{tag}_kde{t - 1}")
                    op("tensor", lambda e, ft=ft, pdst=pdst: e.transpose(
                        pdst[:, 0:128], aB(ft * 128, [1, 128]),
                        ident[:, :]), inc=("sT", 1), tag=f"{tag}_xtt{t}_{ft}")
                    wait("vector", f"{tag}_xtt{t}_{ft}")
                    if ft == 0 and t >= 1:
                        wait("vector", f"{tag}_kmm{t - 1}")
                        wait("vector", f"{tag}_vmm{t - 1}")
                        if t - 1 < 4:
                            wait("vector", f"{tag}_qimm{t - 1}")
                    op("vector", lambda e, ft=ft, pdst=pdst: e.tensor_copy(
                        aB(2048 + ft * 128, [1, 128]), pdst[:, 0:128]),
                       inc=("sV", 1), tag=f"{tag}_xte{t}_{ft}")
                wait("tensor", f"{tag}_xte{t}_{ND - 1}")
                if t == 0:
                    wait("tensor", f"{tag}_wk")
                    wait("tensor", f"{tag}_wv")
                    wait("tensor", f"{tag}_wiq")
                else:
                    wait("tensor", f"{tag}_krope{t - 1}")
                    wait("tensor", f"{tag}_vev{t - 1}")
                for ft in range(ND):
                    op("tensor", lambda e, ft=ft: e.matmul(
                        pb[0][:, :], aB(2048 + ft * 128, [1, 128]),
                        ws(ft * 512, [1, 512]),
                        start=(ft == 0), stop=(ft == ND - 1)),
                       inc=(("sT", 1) if ft == ND - 1 else None),
                       tag=(f"{tag}_kmm{t}" if ft == ND - 1 else None))
                    op("tensor", lambda e, ft=ft: e.matmul(
                        pb[1][:, :], aB(2048 + ft * 128, [1, 128]),
                        ws(8192 + ft * 512, [1, 512]),
                        start=(ft == 0), stop=(ft == ND - 1)),
                       inc=(("sT", 1) if ft == ND - 1 else None),
                       tag=(f"{tag}_vmm{t}" if ft == ND - 1 else None))
                if t < 4:
                    if t >= 1:
                        wait("tensor", f"{tag}_qie{t - 1}")
                    for ft in range(ND):
                        op("tensor", lambda e, ft=ft: e.matmul(
                            pb[4][:, 0:128],
                            ap(ckv, ft * 128, [[4096, 128], [1, 128]]),
                            aB(2048 + ft * 128, [1, 128]),
                            start=(ft == 0), stop=(ft == ND - 1)),
                           inc=(("sT", 1) if ft == ND - 1 else None),
                           tag=(f"{tag}_qimm{t}" if ft == ND - 1 else None))
                    wait("vector", f"{tag}_qimm{t}")
                    op("vector", lambda e, t=t: e.tensor_copy(
                        ap(ckv, 3584 + t * 128, [[4096, 128], [1, 128]]),
                        pb[4][:, 0:128]),
                       inc=("sV", 1), tag=f"{tag}_qie{t}")
                wait("vector", f"{tag}_kmm{t}")
                wait("vector", f"{tag}_tabs{t}")
                wait("vector", f"{tag}_tab{t}")
                cosrep = ap(tabc, t * 128, [[2048, 128], [0, 4], [1, 128]])
                sinrep4 = ap(tabs, t * 128, [[2048, 128], [0, 4], [2, 64], [1, 2]])
                bar("vector", lambda e, cosrep=cosrep: e.tensor_tensor(
                    aF(4096, [128, 4], [1, 128]),
                    ap(pb[0], 0, [[512, 128], [128, 4], [1, 128]]),
                    cosrep, ALU.mult))
                ksw = ap(pb[0], 1, [[512, 128], [128, 4], [2, 64], [-1, 2]])
                bar("vector", lambda e, ksw=ksw, sinrep4=sinrep4: e.tensor_tensor(
                    aF(4608, [128, 4], [2, 64], [1, 2]), ksw, sinrep4, ALU.mult))
                op("vector", lambda e, t=t: e.tensor_tensor(
                    kv(t * 512, [1, 512]), aF(4096, [1, 512]),
                    aF(4608, [1, 512]), ALU.add),
                   inc=("sV", 1), tag=f"{tag}_krope{t}")
                wait("vector", f"{tag}_vmm{t}")
                op("vector", lambda e, t=t: e.tensor_copy(
                    kv(8192 + t * 512, [1, 512]), pb[1][:, :]),
                   inc=("sV", 1), tag=f"{tag}_vev{t}")
                wait("vector", f"{tag}_cwtile")
                cwa_rep = ap(arenaF, 5120, [[10240, 128], [0, 4], [1, 128]])
                cwb_rep = ap(arenaF, 5248, [[10240, 128], [0, 4], [1, 128]])
                kch3 = kv(t * 512, [128, 4], [1, 128])
                bar("vector", lambda e, kch3=kch3, cwa_rep=cwa_rep: e.tensor_tensor(
                    aF(4096, [128, 4], [1, 128]), kch3, cwa_rep, ALU.mult))
                bar("vector", lambda e: e.tensor_reduce(
                    aF(5376, [1, 4], [0, 1]), aF(4096, [128, 4], [1, 128]),
                    AX.X, ALU.add))
                bar("vector", lambda e, kch3=kch3, cwb_rep=cwb_rep: e.tensor_tensor(
                    aF(4096, [128, 4], [1, 128]), kch3, cwb_rep, ALU.mult))
                bar("vector", lambda e: e.tensor_reduce(
                    aF(5380, [1, 4], [0, 1]), aF(4096, [128, 4], [1, 128]),
                    AX.X, ALU.add))
                op("vector", lambda e: e.tensor_copy(
                    aB(4224, [1, 8]), aF(5376, [1, 8])), inc=("sV", 1),
                   tag=f"{tag}_kd{t}")
                wait("tensor", f"{tag}_kd{t}")
                if t >= 2:
                    wait("tensor", f"{tag}_kde{t - 2}")
                ptr = pt0 if t % 2 == 0 else pt1
                op("tensor", lambda e, ptr=ptr: e.transpose(
                    ptr[:, 0:128], aB(4224, [1, 128]), ident[:, :]),
                   inc=("sT", 1), tag=f"{tag}_kdt{t}")
                wait("vector", f"{tag}_kdt{t}")
                op("vector", lambda e, t=t, ptr=ptr: e.tensor_copy(
                    ap(arenaB, 4352 + t * 128, [[16384, 8], [1, 128]]),
                    ap(ptr, 0, [[1024, 8], [1, 128]])),
                   inc=("sV", 1), tag=f"{tag}_kde{t}")
            kdallB = ap(arenaB, 4352, [[16384, 8], [1, 2048]])
            kdF = ap(arenaF, 4096, [[10240, 8], [1, 2048]])
            kd3 = ap(arenaF, 4096, [[10240, 8], [16, 128], [1, 16]])
            mx3 = ap(arenaF, 6400, [[10240, 8], [1, 128], [0, 1]])
            mxb = ap(arenaF, 6400, [[10240, 8], [1, 128], [0, 16]])
            bar("vector", lambda e: e.tensor_copy(kdF, kdallB))
            bar("vector", lambda e: e.tensor_reduce(mx3, kd3, AX.X, ALU.max))
            bar("vector", lambda e: e.tensor_tensor(kd3, kd3, mxb, ALU.subtract))
            op("vector", lambda e: e.engine_nop(), inc=("sV", 1), tag=f"{tag}_kdsub")
            wait("scalar", f"{tag}_kdsub")
            op("scalar", lambda e: e.activation(kdF, kdF, AF.Exp),
               inc=("sS", 1), tag=f"{tag}_kdexp")
            wait("vector", f"{tag}_kdexp")
            bar("vector", lambda e: e.tensor_reduce(mx3, kd3, AX.X, ALU.add))
            bar("vector", lambda e: e.reciprocal(
                ap(arenaF, 6400, [[10240, 8], [1, 128]]),
                ap(arenaF, 6400, [[10240, 8], [1, 128]])))
            bar("vector", lambda e: e.tensor_tensor(kd3, kd3, mxb, ALU.mult))
            shm = [(i + 4) % 32 for i in range(32)]
            bar("vector", lambda e: e.stream_shuffle(
                ap(arenaF, 6656, [[10240, 32], [1, 2048]]),
                ap(arenaF, 4096, [[10240, 32], [1, 2048]]), shm))
            op("vector", lambda e: e.tensor_tensor(
                ap(arenaF, 4096, [[10240, 4], [1, 2048]]),
                ap(arenaF, 4096, [[10240, 4], [1, 2048]]),
                ap(arenaF, 6656, [[10240, 4], [1, 2048]]), ALU.add),
               inc=("sV", 1), tag=f"{tag}_wrow")
            for c in range(NT):
                bar("vector", lambda e, c=c: e.tensor_copy(
                    ap(arenaB, 6400, [[16384, 4], [1, 128]]),
                    ap(arenaF, 4096 + c * 128, [[10240, 4], [1, 128]])))
                op("vector", lambda e: e.engine_nop(), inc=("sV", 1),
                   tag=f"{tag}_wcb{c}")
                wait("tensor", f"{tag}_wcb{c}")
                if c >= 2:
                    wait("tensor", f"{tag}_wte{c - 2}")
                ptr = pt0 if c % 2 == 0 else pt1
                op("tensor", lambda e, ptr=ptr: e.transpose(
                    ptr[:, 0:128], ap(arenaB, 6400, [[16384, 128], [1, 128]]),
                    ident[:, :]), inc=("sT", 1), tag=f"{tag}_wtt{c}")
                wait("vector", f"{tag}_wtt{c}")
                op("vector", lambda e, c=c, ptr=ptr: e.tensor_copy(
                    ap(ckv, 3072 + c * 4, [[4096, 128], [1, 4]]),
                    ap(ptr, 0, [[1024, 128], [1, 4]])),
                   inc=("sV", 1), tag=f"{tag}_wte{c}")
            wait("vector", "indt")
            for c in range(NT):
                wrep = ap(ckv, 3072 + c * 4, [[4096, 128], [1, 4], [0, 128]])
                bar("vector", lambda e, c=c, wrep=wrep: e.tensor_tensor(
                    aB(6528, [128, 4], [1, 128]), kv(c * 512, [128, 4], [1, 128]),
                    wrep, ALU.mult))
                op("vector", lambda e, c=c, wrep=wrep: e.tensor_tensor(
                    aB(7040, [128, 4], [1, 128]),
                    kv(8192 + c * 512, [128, 4], [1, 128]), wrep, ALU.mult),
                   inc=("sV", 1), tag=f"{tag}_kvw{c}")
                wait("tensor", f"{tag}_kvw{c}")
                if c >= 1:
                    wait("tensor", f"{tag}_cke{c - 1}")
                op("tensor", lambda e: e.matmul(
                    pb[2][0:8, 0:512], indt[:, :], aB(6528, [1, 512]),
                    start=True, stop=True), inc=("sT", 1))
                op("tensor", lambda e: e.matmul(
                    pb[3][0:8, 0:512], indt[:, :], aB(7040, [1, 512]),
                    start=True, stop=True), inc=("sT", 1), tag=f"{tag}_ckmm{c}")
                wait("vector", f"{tag}_ckmm{c}")
                if c >= 2:
                    wait("vector", f"{tag}_ckdma{c - 2}")
                par = (c % 2) * 1024
                op("vector", lambda e, par=par: e.tensor_copy(
                    ap(arenaB, 8192 + par, [[16384, 8], [1, 512]]),
                    ap(pb[2], 0, [[512, 8], [1, 512]])), inc=("sV", 1))
                op("vector", lambda e, par=par: e.tensor_copy(
                    ap(arenaB, 8192 + par + 512, [[16384, 8], [1, 512]]),
                    ap(pb[3], 0, [[512, 8], [1, 512]])),
                   inc=("sV", 1), tag=f"{tag}_cke{c}")
                wait("gpsimd", f"{tag}_cke{c}")
                op("gpsimd", lambda e, c=c, par=par: e.dma_start(
                    out=ap(at_in, ATO_CK + c * 8 * 512, [[512, 8], [1, 512]]),
                    in_=ap(arenaB, 8192 + par, [[16384, 8], [1, 512]])),
                   inc=("sO", 16))
                op("gpsimd", lambda e, c=c, par=par: e.dma_start(
                    out=ap(at_in, ATO_CV + c * 8 * 512, [[512, 8], [1, 512]]),
                    in_=ap(arenaB, 8192 + par + 512, [[16384, 8], [1, 512]])),
                   inc=("sO", 16), tag=f"{tag}_ckdma{c}")
            E.settag(f"{tag}_ckst", f"{tag}_ckdma{NT - 1}")
            E.settag(f"{tag}_cvst", f"{tag}_ckdma{NT - 1}")
            wait("vector", f"{tag}_csmm")
            bar("vector", lambda e: e.tensor_reduce(
                aF(4096, [1, 16]),
                ap(pb[5], 0, [[512, 128], [16, 16], [1, 16]]), AX.X, ALU.add))
            op("vector", lambda e: e.tensor_copy(
                ap(ckv, 3136 + 0, [[4096, 128], [1, 16]]), aF(4096, [1, 16])),
               inc=("sV", 1), tag=f"{tag}_csev")
            wait("gpsimd", f"{tag}_csev")
            wait("gpsimd", f"{tag}_qie3")
            op("gpsimd", lambda e: e.dma_start(
                out=ap(at_in, ATO_QI, [[512, 128], [1, 512]]),
                in_=ap(ckv, 3584, [[4096, 128], [1, 512]])),
               inc=("sO", 16), tag=f"{tag}_qist")
            op("gpsimd", lambda e: e.dma_start(
                out=ap(at_in, ATO_CS, [[16, 128], [1, 16]]),
                in_=ap(ckv, 3136, [[4096, 128], [1, 16]])),
               inc=("sO", 16), tag=f"{tag}_csst")
            for tg in (f"{tag}_ckst", f"{tag}_cvst", f"{tag}_qist", f"{tag}_csst"):
                wait("gpsimd", tg)
            op("gpsimd", lambda e: e.collective_compute(
                "AllGather", ALU.bypass,
                replica_groups=[[0, 1, 2, 3], [4, 5, 6, 7]],
                ins=[at_in.ap().opt()], outs=[at_all.ap().opt()]),
               inc=("sC", 1), tag="ag_at")
            wait("gpsimd", "ag_at")
            for r in range(4):
                for h in range(4):
                    op("gpsimd", lambda e, r=r, h=h: e.dma_start(
                        out=ap(ckv, 2048 + h * 512 + r * 128,
                               [[4096, 128], [1, 128]]),
                        in_=ap(at_all, r * AT_N + ATO_CV + h * 128,
                               [[512, 128], [1, 128]])),
                       inc=("sO", 16), tag=f"{tag}_ucv{r}_{h}")
            for r in range(4):
                for h in range(4):
                    op("gpsimd", lambda e, r=r, h=h: e.dma_start(
                        out=kv((4 * r + h) * 128, [1, 128]),
                        in_=ap(at_all, r * AT_N + ATO_CK + h * 128,
                               [[512, 128], [1, 128]])),
                       inc=("sO", 16), tag=f"{tag}_uck{r}_{h}")
            op("gpsimd", lambda e: e.dma_start(
                out=kv(2048, [1, 512]),
                in_=ap(at_all, 0 * AT_N + ATO_QI, [[512, 128], [1, 512]])),
               inc=("sO", 16), tag=f"{tag}_uqi")
            for r in range(4):
                op("gpsimd", lambda e, r=r: e.dma_start(
                    out=kv(2560 + r * 16, [1, 16]),
                    in_=ap(at_all, r * AT_N + ATO_CS, [[16, 128], [1, 16]])),
                   inc=("sO", 16), tag=f"{tag}_ucs{r}")
            for r in range(4):
                wait("vector", f"{tag}_ucs{r}")
            bar("vector", lambda e: e.tensor_copy(
                aF(4096, [16, 4], [1, 16]), kv(2560, [16, 4], [1, 16])))
            bar("vector", lambda e: e.tensor_tensor(
                aF(4096, [1, 16]), aF(4096, [1, 16]), aF(4096 + 16, [1, 16]), ALU.add))
            bar("vector", lambda e: e.tensor_tensor(
                aF(4096 + 32, [1, 16]), aF(4096 + 32, [1, 16]),
                aF(4096 + 48, [1, 16]), ALU.add))
            bar("vector", lambda e: e.tensor_tensor(
                aF(4096, [1, 16]), aF(4096, [1, 16]), aF(4096 + 32, [1, 16]), ALU.add))
            op("vector", lambda e: e.tensor_copy(
                aB(6400, [1, 16]), aF(4096, [1, 16])), inc=("sV", 1),
               tag=f"{tag}_cst")
            op("gpsimd", lambda e: e.dma_start(
                out=kv(2816, [128, ND], [1, 128]),
                in_=ap(wik_h, 0, [[128, 128], [128 * 128, ND], [1, 128]])),
               inc=("sWA", 16), tag=f"{tag}_wik")
            wait("tensor", f"{tag}_cst")
            wait("tensor", f"{tag}_wik")
            wait("tensor", f"{tag}_uqi")
            for ft in range(ND):
                op("tensor", lambda e, ft=ft: e.matmul(
                    pb[4][0:1, 0:128], aB(6400 + ft, [1, 1]),
                    kv(2816 + ft * 128, [1, 128]),
                    start=(ft == 0), stop=(ft == ND - 1)),
                   inc=(("sT", 1) if ft == ND - 1 else None),
                   tag=(f"{tag}_kimm" if ft == ND - 1 else None))
            wait("vector", f"{tag}_kimm")
            op("vector", lambda e: e.tensor_copy(
                ap(arenaB, 6528, [[16384, 1], [1, 128]]),
                ap(pb[4], 0, [[512, 1], [1, 128]])), inc=("sV", 1),
               tag=f"{tag}_kime")
            wait("tensor", f"{tag}_kime")
            wait("tensor", f"{tag}_kde{NT - 1}")
            op("tensor", lambda e: e.transpose(
                pt0[:, 0:1], ap(arenaB, 6528, [[16384, 1], [1, 128]]),
                ident[0:1, 0:1]), inc=("sT", 1), tag=f"{tag}_kimt")
            wait("vector", f"{tag}_kimt")
            op("vector", lambda e: e.tensor_copy(
                ap(arenaB, 6656, [[16384, 128], [1, 1]]),
                ap(pt0, 0, [[1024, 128], [1, 1]])), inc=("sV", 1),
               tag=f"{tag}_kimc")
            wait("tensor", f"{tag}_kimc")
            op("tensor", lambda e: e.matmul(
                pb[4][0:1, 0:512], ap(arenaB, 6656, [[16384, 128], [1, 1]]),
                kv(2048, [1, 512]), start=True, stop=True),
               inc=("sT", 1), tag=f"{tag}_smm")
            wait("vector", f"{tag}_smm")
            op("vector", lambda e: e.tensor_copy(
                ap(arenaB, 6784, [[16384, 1], [1, 512]]),
                ap(pb[4], 0, [[512, 1], [1, 512]])), inc=("sV", 1),
               tag=f"{tag}_sev")
            wait("tensor", f"{tag}_sev")
            op("tensor", lambda e: e.matmul(
                pb[4][:, 0:512], ap(ones1, 0, [[1024, 1], [1, 128]]),
                ap(arenaB, 6784, [[16384, 1], [1, 512]]),
                start=True, stop=True), inc=("sT", 1), tag=f"{tag}_bmm")
            for c in range(4):
                op("tensor", lambda e, c=c: e.transpose(
                    pt1[:, 2 * c:2 * c + 1],
                    ap(arenaB, 6784 + c * 128, [[16384, 1], [1, 128]]),
                    ident[0:1, 0:1]), inc=(("sT", 1) if c == 3 else None),
                   tag=(f"{tag}_sct" if c == 3 else None))
            wait("vector", f"{tag}_bmm")
            op("vector", lambda e: e.tensor_copy(
                aF(4096, [1, 512]), pb[4][:, 0:512]), inc=("sV", 1))
            wait("vector", f"{tag}_sct")
            op("vector", lambda e: e.tensor_copy(
                aF(4608, [1, 4]), ap(pt1, 0, [[1024, 128], [2, 4]])),
               inc=("sV", 1), tag=f"{tag}_scev")
            wait("vector", f"{tag}_scev")
            for c in range(4):
                bar("vector", lambda e, c=c: e.tensor_scalar(
                    aF(5120, [1, 512]), aF(4096, [1, 512]),
                    ap(arenaF, 4608 + c, [[10240, 128], [1, 1]]), None, ALU.is_gt))
                bar("vector", lambda e, c=c: e.tensor_reduce(
                    ap(arenaF, 5632 + c, [[10240, 128], [1, 1]]),
                    aF(5120, [1, 512]), AX.X, ALU.add))
            bar("vector", lambda e: e.tensor_scalar(
                aF(5120, [1, 4]), aF(5632, [1, 4]), 64.0, 1.0,
                ALU.is_lt, ALU.subtract))
            op("vector", lambda e: e.tensor_scalar(
                aB(6912, [1, 4]), aF(5120, [1, 4]), 30000.0, None, ALU.mult),
               inc=("sV", 1), tag=f"{tag}_maskc")
            wait("tensor", f"{tag}_maskc")
            op("tensor", lambda e: e.transpose(
                pt0[0:4, 0:128], ap(arenaB, 6912, [[16384, 128], [1, 4]]),
                ident[:, :]), inc=("sT", 1), tag=f"{tag}_maskt")
            wait("vector", f"{tag}_maskt")
            op("vector", lambda e: e.tensor_copy(
                ap(arenaB, 7424, [[16384, 4], [1, 128]]),
                ap(pt0, 0, [[1024, 4], [1, 128]])), inc=("sV", 1),
               tag=f"{tag}_maske")
            wait("gpsimd", f"{tag}_maske")
            op("gpsimd", lambda e: e.dma_start(
                out=ap(bounce_b, 0, [[1, 512]]),
                in_=ap(arenaB, 7424, [[16384, 4], [1, 128]])),
               inc=("sO", 16), tag=f"{tag}_maskbo")
            wait("gpsimd", f"{tag}_maskbo")
            op("gpsimd", lambda e: e.dma_start(
                out=ap(ones1, 512, [[1024, 1], [1, 512]]),
                in_=ap(bounce_b, 0, [[1, 512]])),
               inc=("sO", 16), tag=f"{tag}_maskrow")
            for i in range(16):
                r, h = i // 4, i % 4
                wait("tensor", f"{tag}_uck{r}_{h}")
                ptr = pt0 if i % 2 == 0 else pt1
                if i >= 2:
                    wait("tensor", f"{tag}_ckfe{i - 2}")
                elif i == 0:
                    wait("tensor", f"{tag}_maske")
                else:
                    wait("tensor", f"{tag}_scev")
                op("tensor", lambda e, i=i, ptr=ptr: e.transpose(
                    ptr[:, 0:128], kv(i * 128, [1, 128]), ident[:, :]),
                   inc=("sT", 1), tag=f"{tag}_ckft{i}")
                wait("vector", f"{tag}_ckft{i}")
                op("vector", lambda e, i=i, r=r, h=h, ptr=ptr: e.tensor_copy(
                    ap(ckv, h * 512 + r * 128, [[4096, 128], [1, 128]]),
                    ptr[:, 0:128]), inc=("sV", 1), tag=f"{tag}_ckfe{i}")
            op("vector", lambda e: e.engine_nop(), inc=("sV", 1), tag=f"{tag}_adone")

        def bphase(E, tag):
            op, wait, bar = E.op, E.wait, E.bar
            xrdy = "f1_dst{t}" if PH["ffn1"] else "cp1_s{t}"
            for eng in ("scalar", "vector", "tensor", "sync", "gpsimd"):
                wait(eng, f"{tag}_adone")
                wait(eng, f"{tag}_maskrow")
                for r in range(4):
                    wait(eng, f"{tag}_ucv{r}_3")
            for g in range(4):
                for tc in range(4):
                    t = 4 * g + tc
                    xoff = tc * 2048
                    wait("scalar", xrdy.format(t=t))
                    if g >= 1:
                        wait("scalar", f"{tag}_xst{t - 4}")
                    op("scalar", lambda e, t=t, xoff=xoff: e.dma_start(
                        out=aF(xoff, [1, D]),
                        in_=ap(x_mid, t * 128 * D, [[D, 128], [1, D]])),
                       inc=("sXL", 16), tag=f"{tag}_bxl{t}")
                    wait("scalar", f"{tag}_bxl{t}")
                    if t >= 1:
                        wait("scalar", f"{tag}_bxtt{t - 1}")
                    op("scalar", lambda e, t=t, xoff=xoff: e.activation(
                        aB(8192, [1, D]), aF(xoff, [1, D]), AF.Identity,
                        bias=small[:, 32 + t:33 + t], scale=small[:, 16 + t:17 + t]),
                       inc=("sS", 1), tag=f"{tag}_bxln{t}")
                    wait("tensor", f"{tag}_bxln{t}")
                    for ft in range(ND):
                        pdst = pt0 if ft % 2 == 0 else pt1
                        if ft >= 2:
                            wait("tensor", f"{tag}_bxe{t}_{ft - 2}")
                        elif t >= 1:
                            wait("tensor", f"{tag}_bxe{t - 1}_{ND - 2 + ft}")
                        op("tensor", lambda e, ft=ft, pdst=pdst:
                           e.transpose(
                               pdst[:, 0:128], aB(8192 + ft * 128, [1, 128]),
                               ident[:, :]),
                           inc=("sT", 1),
                           tag=(f"{tag}_bxtt{t}" if ft == ND - 1 else f"{tag}_bxt{t}_{ft}"))
                        wait("vector", f"{tag}_bxtt{t}" if ft == ND - 1 else f"{tag}_bxt{t}_{ft}")
                        if ft == 0 and g >= 1:
                            wait("vector", f"{tag}_qmm{g - 1}_1_{tc}")
                        op("vector", lambda e, tc=tc, ft=ft, pdst=pdst:
                           e.tensor_copy(
                               aB(ft * 512 + tc * 128, [1, 128]),
                               pdst[:, 0:128]),
                           inc=("sV", 1), tag=f"{tag}_bxe{t}_{ft}")
                    E.settag(f"{tag}_bxe{t}", f"{tag}_bxe{t}_{ND - 1}")
                for qp in range(2):
                    for half in range(2):
                        qc = 2 * qp + half
                        woff = half * 8192
                        if qp == 1:
                            wait("gpsimd", f"{tag}_qmm{g}_0_3")
                            wait("gpsimd", f"{tag}_qmm{g}_0_3a")
                        elif g >= 1:
                            wait("gpsimd", f"{tag}_smm{g - 1}_{H - 1}_3")
                        op("gpsimd", lambda e, qc=qc, woff=woff: e.dma_start(
                            out=ws(woff, [512, ND], [1, 512]),
                            in_=ap(wq_h, qc * 512,
                                   [[D, 128], [D * 128, ND], [1, 512]])),
                           inc=("sWA", 16), tag=f"{tag}_wq{g}_{qc}")
                    for tc in range(4):
                        t = 4 * g + tc
                        pq0 = pb[0] if tc % 2 == 0 else pb[2]
                        pq1 = pb[1] if tc % 2 == 0 else pb[3]
                        wait("tensor", f"{tag}_wq{g}_{2 * qp}")
                        wait("tensor", f"{tag}_wq{g}_{2 * qp + 1}")
                        wait("tensor", f"{tag}_bxe{t}")
                        if not (g == 0 and qp == 0 and tc < 2):
                            pass
                        idx = (g * 2 + qp) * 4 + tc
                        if idx >= 2:
                            pidx = idx - 2
                            wait("tensor", f"{tag}_rope{pidx}")
                        for ft in range(ND):
                            op("tensor", lambda e, tc=tc, ft=ft, pq0=pq0: e.matmul(
                                pq0[:, :], aB(ft * 512 + tc * 128, [1, 128]),
                                ws(0 * 8192 + ft * 512, [1, 512]),
                                start=(ft == 0), stop=(ft == ND - 1)),
                               inc=(("sT", 1) if ft == ND - 1 else None),
                               tag=(f"{tag}_qmm{g}_{qp}_{tc}a" if ft == ND - 1 else None))
                            op("tensor", lambda e, tc=tc, ft=ft, pq1=pq1: e.matmul(
                                pq1[:, :], aB(ft * 512 + tc * 128, [1, 128]),
                                ws(8192 + ft * 512, [1, 512]),
                                start=(ft == 0), stop=(ft == ND - 1)),
                               inc=(("sT", 1) if ft == ND - 1 else None),
                               tag=(f"{tag}_qmm{g}_{qp}_{tc}" if ft == ND - 1 else None))
                        wait("vector", f"{tag}_qmm{g}_{qp}_{tc}a")
                        wait("vector", f"{tag}_qmm{g}_{qp}_{tc}")
                        for half, pq in ((0, pq0), (1, pq1)):
                            cosrep = ap(tabc, t * 128, [[2048, 128], [0, 4], [1, 128]])
                            sinrep4 = ap(tabs, t * 128,
                                         [[2048, 128], [0, 4], [2, 64], [1, 2]])
                            bar("vector", lambda e, pq=pq, cosrep=cosrep: e.tensor_tensor(
                                aF(8192, [128, 4], [1, 128]),
                                ap(pq, 0, [[512, 128], [128, 4], [1, 128]]),
                                cosrep, ALU.mult))
                            qsw = ap(pq, 1, [[512, 128], [128, 4], [2, 64], [-1, 2]])
                            bar("vector", lambda e, qsw=qsw, sinrep4=sinrep4: e.tensor_tensor(
                                aF(8704, [128, 4], [2, 64], [1, 2]), qsw, sinrep4,
                                ALU.mult))
                            tg = f"{tag}_rope{idx}" if half == 1 else None
                            op("vector", lambda e, tc=tc, qp=qp, half=half: e.tensor_tensor(
                                aB(8192 + tc * 2048 + qp * 1024 + half * 512, [1, 512]),
                                aF(8192, [1, 512]), aF(8704, [1, 512]), ALU.add),
                               inc=("sV", 1), tag=(tg if tg else f"{tag}_ropeh{idx}_{half}"))
                for tc in range(4):
                    t = 4 * g + tc
                    idx3 = (g * 2 + 1) * 4 + tc
                    wait("tensor", f"{tag}_rope{idx3}")
                    for ft in range(ND):
                        pdst = pt0 if ft % 2 == 0 else pt1
                        if ft >= 2:
                            wait("tensor", f"{tag}_qte{t}_{ft - 2}")
                        elif tc >= 1:
                            wait("tensor", f"{tag}_qte{t - 1}_{ND - 2 + ft}")
                        op("tensor", lambda e, tc=tc, ft=ft, pdst=pdst:
                           e.transpose(
                               pdst[:, 0:128],
                               aB(8192 + tc * 2048 + ft * 128, [1, 128]),
                               ident[:, :]), inc=("sT", 1),
                           tag=f"{tag}_qtt{t}_{ft}")
                        wait("vector", f"{tag}_qtt{t}_{ft}")
                        if ft == 0 and tc == 0 and g >= 1:
                            wait("vector", f"{tag}_wo_done{g - 1}")
                        op("vector", lambda e, tc=tc, ft=ft, pdst=pdst:
                           e.tensor_copy(
                               ws(ft * 512 + tc * 128, [1, 128]),
                               pdst[:, 0:128]),
                           inc=("sV", 1), tag=f"{tag}_qte{t}_{ft}")
                for tc in range(4):
                    wait("tensor", f"{tag}_qte{4 * g + tc}_{ND - 1}")
                for h in range(H):
                    hk = h // 4
                    for tc in range(4):
                        psc = pb[tc]
                        if h >= 1:
                            wait("tensor", f"{tag}_att{g}_{h - 1}_{tc}")
                        elif g >= 1:
                            wait("tensor", f"{tag}_att{g - 1}_{H - 1}_{tc}")
                        op("tensor", lambda e, h=h, tc=tc, psc=psc, hk=hk: e.matmul(
                            psc[:, :], ws(h * 512 + tc * 128, [1, 128]),
                            ap(ckv, hk * 512, [[4096, 128], [1, 512]]),
                            start=True, stop=False))
                        op("tensor", lambda e, psc=psc: e.matmul(
                            psc[:, :], ap(ones1, 0, [[1024, 1], [1, 128]]),
                            ap(ones1, 512, [[1024, 1], [1, 512]]),
                            start=False, stop=True),
                           inc=("sT", 1), tag=f"{tag}_smm{g}_{h}_{tc}")
                        wait("vector", f"{tag}_smm{g}_{h}_{tc}")
                        idx2 = (g * H + h) * 4 + tc
                        p2 = idx2 % 2
                        sm0 = 10 + p2 * 2
                        if idx2 >= 2:
                            pp = idx2 - 2
                            wait("vector", "%s_exp%d_%d_%d" % (
                                tag, pp // (4 * H), (pp // 4) % H, pp % 4))
                        bar("vector", lambda e, psc=psc, sm0=sm0: e.tensor_reduce(
                            small[:, sm0:sm0 + 1], ap(psc, 0, [[512, 128], [1, 512]]),
                            AX.X, ALU.max))
                        bar("vector", lambda e, sm0=sm0: e.tensor_scalar(
                            small[:, sm0:sm0 + 1], small[:, sm0:sm0 + 1],
                            -ISC, None, ALU.mult))
                        op("vector", lambda e: e.engine_nop(), inc=("sV", 1),
                           tag=f"{tag}_negm{g}_{h}_{tc}")
                        wait("scalar", f"{tag}_negm{g}_{h}_{tc}")
                        if idx2 >= 2:
                            pp = idx2 - 2
                            wait("scalar", "%s_att%d_%d_%d" % (
                                tag, pp // (4 * H), (pp // 4) % H, pp % 4))
                        op("scalar", lambda e, psc=psc, p2=p2, sm0=sm0: e.activation(
                            aF(8192 + p2 * 512, [1, 512]), psc[:, :], AF.Exp,
                            bias=small[:, sm0:sm0 + 1], scale=ISC),
                           inc=("sS", 1), tag=f"{tag}_exp{g}_{h}_{tc}")
                        wait("vector", f"{tag}_exp{g}_{h}_{tc}")
                        bar("vector", lambda e, p2=p2, sm0=sm0: e.tensor_reduce(
                            small[:, sm0 + 1:sm0 + 2], aF(8192 + p2 * 512, [1, 512]),
                            AX.X, ALU.add))
                        bar("vector", lambda e, sm0=sm0: e.reciprocal(
                            small[:, sm0 + 1:sm0 + 2], small[:, sm0 + 1:sm0 + 2]))
                        op("vector", lambda e, tc=tc, p2=p2, sm0=sm0: e.tensor_scalar(
                            aB(8192 + tc * 512, [1, 512]), aF(8192 + p2 * 512, [1, 512]),
                            small[:, sm0 + 1:sm0 + 2], None, ALU.mult),
                           inc=("sV", 1), tag=f"{tag}_att{g}_{h}_{tc}")
                    for tc in range(4):
                        wait("tensor", f"{tag}_att{g}_{h}_{tc}")
                        ptr = pt0 if tc % 2 == 0 else pt1
                        if tc >= 2:
                            wait("tensor", f"{tag}_atte{g}_{h}_{tc - 2}")
                        elif h >= 1:
                            wait("tensor", f"{tag}_atte{g}_{h - 1}_{tc + 2}")
                        for blt in range(4):
                            op("tensor", lambda e, tc=tc, blt=blt, ptr=ptr: e.transpose(
                                ptr[:, blt * 128:(blt + 1) * 128],
                                aB(8192 + tc * 512 + blt * 128, [1, 128]),
                                ident[:, :]),
                               inc=(("sT", 1) if blt == 3 else None),
                               tag=(f"{tag}_attt{g}_{h}_{tc}" if blt == 3 else None))
                        wait("vector", f"{tag}_attt{g}_{h}_{tc}")
                        op("vector", lambda e, tc=tc, ptr=ptr: e.tensor_copy(
                            ap(arenaB, 10240 + tc * 128,
                               [[16384, 128], [512, 4], [1, 128]]),
                            ap(ptr, 0, [[1024, 128], [128, 4], [1, 128]])),
                           inc=("sV", 1), tag=f"{tag}_atte{g}_{h}_{tc}")
                    for tc in range(4):
                        wait("tensor", f"{tag}_atte{g}_{h}_{tc}")
                    if h >= 1:
                        wait("tensor", f"{tag}_oev{g}_{h - 1}")
                    for blt in range(4):
                        op("tensor", lambda e, blt=blt, hk=hk: e.matmul(
                            pb[4][:, :],
                            ap(ckv, 2048 + hk * 512 + blt * 128,
                               [[4096, 128], [1, 128]]),
                            ap(arenaB, 10240 + blt * 512, [[16384, 128], [1, 512]]),
                            start=(blt == 0), stop=(blt == 3)),
                           inc=(("sT", 1) if blt == 3 else None),
                           tag=(f"{tag}_avmm{g}_{h}" if blt == 3 else None))
                    wait("vector", f"{tag}_avmm{g}_{h}")
                    op("vector", lambda e, h=h: e.tensor_copy(
                        kv(8192 + h * 512, [1, 512]), pb[4][:, :]),
                       inc=("sV", 1), tag=f"{tag}_oev{g}_{h}")
                for op2 in range(2):
                    for half in range(2):
                        oc = 2 * op2 + half
                        woff = half * 8192
                        wait("gpsimd", f"{tag}_smm{g}_{H - 1}_3")
                        if op2 == 1:
                            wait("gpsimd", f"{tag}_womm{g}_0_3")
                            wait("gpsimd", f"{tag}_womm{g}_0_3a")
                        op("gpsimd", lambda e, oc=oc, woff=woff: e.dma_start(
                            out=ws(woff, [512, ND], [1, 512]),
                            in_=ap(wo_h, oc * 512,
                                   [[D, 128], [D * 128, ND], [1, 512]])),
                           inc=("sWA", 16), tag=f"{tag}_wo{g}_{oc}")
                    for tc in range(4):
                        t = 4 * g + tc
                        po0 = pb[0] if tc % 2 == 0 else pb[2]
                        po1 = pb[1] if tc % 2 == 0 else pb[3]
                        wait("tensor", f"{tag}_wo{g}_{2 * op2}")
                        wait("tensor", f"{tag}_wo{g}_{2 * op2 + 1}")
                        wait("tensor", f"{tag}_oev{g}_{H - 1}")
                        idx = op2 * 4 + tc
                        if idx >= 2:
                            pidx = idx - 2
                            wait("tensor", f"{tag}_woev{g}_{pidx // 4}_{pidx % 4}")
                        for ft in range(ND):
                            op("tensor", lambda e, tc=tc, ft=ft, po0=po0: e.matmul(
                                po0[:, :], kv(8192 + ft * 512 + tc * 128, [1, 128]),
                                ws(0 + ft * 512, [1, 512]),
                                start=(ft == 0), stop=(ft == ND - 1)),
                               inc=(("sT", 1) if ft == ND - 1 else None),
                               tag=(f"{tag}_womm{g}_{op2}_{tc}a" if ft == ND - 1 else None))
                            op("tensor", lambda e, tc=tc, ft=ft, po1=po1: e.matmul(
                                po1[:, :], kv(8192 + ft * 512 + tc * 128, [1, 128]),
                                ws(8192 + ft * 512, [1, 512]),
                                start=(ft == 0), stop=(ft == ND - 1)),
                               inc=(("sT", 1) if ft == ND - 1 else None),
                               tag=(f"{tag}_womm{g}_{op2}_{tc}" if ft == ND - 1 else None))
                        wait("vector", f"{tag}_womm{g}_{op2}_{tc}a")
                        wait("vector", f"{tag}_womm{g}_{op2}_{tc}")
                        xoff = tc * 2048
                        for half, po in ((0, po0), (1, po1)):
                            oc = 2 * op2 + half
                            op("vector", lambda e, po=po, xoff=xoff, oc=oc:
                               e.scalar_tensor_tensor(
                                   aF(xoff + oc * 512, [1, 512]), po[:, :], 1.0,
                                   aF(xoff + oc * 512, [1, 512]), ALU.mult, ALU.add),
                               inc=("sV", 1),
                               tag=f"{tag}_woev{g}_{op2}_{tc}" if half == 1 else None)
                for tc in range(4):
                    t = 4 * g + tc
                    wait("gpsimd", f"{tag}_woev{g}_1_{tc}")
                    op("gpsimd", lambda e, t=t, tc=tc: e.dma_start(
                        out=ap(x_att, t * 128 * D, [[D, 128], [1, D]]),
                        in_=aF(tc * 2048, [1, D])),
                       inc=("sO", 16), tag=f"{tag}_xst{t}")
                op("vector", lambda e: e.engine_nop(), inc=("sV", 1),
                   tag=f"{tag}_wo_done{g}")
            op("vector", lambda e: e.engine_nop(), inc=("sV", 1), tag="at_done")

        def dbg_readback(E):
            op, wait = E.op, E.wait
            end_tag = "f2_dst15" if PH["ffn2"] else "cp3_s15"
            wait("sync", end_tag)
            for ci, (srch, cbase) in enumerate(
                    ((qg_in, 0), (qu_in, 16), (qd_in, 32))):
                for c in range(16):
                    gc = cbase + c
                    par = gc % 2
                    if gc >= 2:
                        wait("sync", f"dbgv{gc - 2}")
                    op("sync", lambda e, c=c, par=par, srch=srch: e.dma_start(
                        out=a8(par * 1024, [1, 1024]),
                        in_=ap(srch, c * 131072, [[1024, 128], [1, 1024]])),
                       inc=("sWL", 16), tag=f"dbgl{gc}")
                    wait("vector", f"dbgl{gc}")
                    if gc >= 2:
                        wait("vector", f"dbgs{gc - 2}")
                    op("vector", lambda e, par=par: e.tensor_copy(
                        aB(par * 1024, [1, 1024]), a8(par * 1024, [1, 1024])),
                       inc=("sV", 1), tag=f"dbgv{gc}")
                    wait("gpsimd", f"dbgv{gc}")
                    op("gpsimd", lambda e, gc=gc, par=par: e.dma_start(
                        out=ap(qdbg, gc * 131072, [[1024, 128], [1, 1024]]),
                        in_=aB(par * 1024, [1, 1024])),
                       inc=("sO", 16), tag=f"dbgs{gc}")
            wait("vector", "dbgs47")
            op("vector", lambda e: e.engine_nop(), inc=("sV", 1), tag="dbg_done")

        env.start("dry")
        program(env)
        with nc.Block() as blk:
            for engname, dec in (("sync", blk.sync), ("vector", blk.vector),
                                 ("scalar", blk.scalar), ("tensor", blk.tensor),
                                 ("gpsimd", blk.gpsimd)):
                def mk(name):
                    def f(e):
                        env.start("emit", name, e)
                        program(env)
                    return f
                dec(mk(engname))

    return nc


_NC_CACHE = {}


def kernel(L0_attn_x, L0_attn_wq, L0_attn_wk, L0_attn_wv, L0_attn_wo,
           L0_attn_wiq, L0_attn_wik, L0_attn_cwa, L0_attn_cwb,
           L0_ffn_w_gate, L0_ffn_w_up, L0_ffn_w_down):
    x = np.ascontiguousarray(np.asarray(L0_attn_x, dtype=np.float32)
                             ).reshape(2 * 8192, D)
    wq = np.ascontiguousarray(np.asarray(L0_attn_wq, dtype=np.float32))
    wk = np.ascontiguousarray(np.asarray(L0_attn_wk, dtype=np.float32))
    wv = np.ascontiguousarray(np.asarray(L0_attn_wv, dtype=np.float32))
    wo = np.ascontiguousarray(np.asarray(L0_attn_wo, dtype=np.float32))
    wiq = np.ascontiguousarray(np.asarray(L0_attn_wiq, dtype=np.float32))
    wik = np.ascontiguousarray(np.asarray(L0_attn_wik, dtype=np.float32))
    cw = np.ascontiguousarray(
        np.stack([np.asarray(L0_attn_cwa, dtype=np.float32),
                  np.asarray(L0_attn_cwb, dtype=np.float32)], axis=0))
    wg = np.ascontiguousarray(np.asarray(L0_ffn_w_gate, dtype=np.float32))
    wu = np.ascontiguousarray(np.asarray(L0_ffn_w_up, dtype=np.float32))
    wd = np.ascontiguousarray(np.asarray(L0_ffn_w_down, dtype=np.float32))

    if "nc" not in _NC_CACHE:
        _NC_CACHE["nc"] = build_program()
    nc = _NC_CACHE["nc"]

    in_maps = []
    for c in range(8):
        slot = c % 4
        in_maps.append({
            "x": np.ascontiguousarray(x[c * TC:(c + 1) * TC]),
            "wq": wq, "wk": wk, "wv": wv, "wo": wo,
            "wiq": wiq, "wik": wik, "cw": cw,
            "wg_shard": np.ascontiguousarray(wg[c * (II // 8):(c + 1) * (II // 8)]),
            "wu_shard": np.ascontiguousarray(wu[c * (II // 8):(c + 1) * (II // 8)]),
            "wd_shard": np.ascontiguousarray(wd[:, c * (II // 8):(c + 1) * (II // 8)]),
            "slot": np.array([[1.0 if slot == 0 else 0.0, float(slot), 0.0, 0.0]],
                             dtype=np.float32),
        })
    res = run_bass_kernel_spmd(nc, in_maps, core_ids=list(range(8)))
    _LAST_RES[0] = res
    out = np.concatenate([np.asarray(r["out"]) for r in res.results], axis=0)
    return out.reshape(2, 8192, D)


# version 08
# speedup vs baseline: 1.0087x; 1.0016x over previous
import numpy as np
import concourse.bass as bass
import concourse.mybir as mybir
from concourse.bass_utils import run_bass_kernel_spmd

F32 = mybir.dt.float32
BF16 = mybir.dt.bfloat16
F8 = mybir.dt.float8e4
I32 = mybir.dt.int32
ALU = mybir.AluOpType
AX = mybir.AxisListType
AF = mybir.ActivationFunctionType

D = 2048
II = 8192
TC = 2048
NT = 16
ND = 16
NI512 = 16
NIT = 64
H, HKV, HD = 16, 4, 128
NBLK = 512
THETA = 10000.0
EPS = 1e-5
ISC = float(1.0 / np.sqrt(128.0))

C250 = 250 << 23
CGA_X = 4 << 23
CGA_I = -(2 << 23)
CLIP_A = 0.119140625
C255 = 255 << 23
CW_BACK = 5 << 23
WCLIP = 6.9921875 / 2.0

PH = {"ffn1": True, "attn": True, "ffn2": True}
DBG = {"dump": False}
_LAST_RES = [None]

SEMS = ["sWL", "sWA", "sXL", "sPL", "sV", "sS", "sT", "sG", "sO", "sC", "sB", "sQ"]


class Env:
    def __init__(self):
        self.handles = None
        self.tags = {}
        self.mode = "dry"
        self.engine = None
        self.eng = None
        self.counts = {}
        self.barn = 0

    def start(self, mode, engine=None, eng=None):
        self.mode = mode
        self.engine = engine
        self.eng = eng
        self.counts = {s: 0 for s in SEMS}
        self.barn = 0

    def op(self, engine, fn, inc=None, tag=None):
        inst = None
        if self.mode == "emit" and engine == self.engine:
            inst = fn(self.eng)
        if inc is not None:
            sem, k = inc
            self.counts[sem] += k
            if tag is not None:
                self.tags[tag] = (sem, self.counts[sem])
            if inst is not None:
                inst.then_inc(self.handles[sem], k)
        return inst

    def wait(self, engine, tag):
        if tag not in self.tags:
            raise KeyError(f"wait on unset tag {tag!r}")
        sem, v = self.tags[tag]
        if v <= 0:
            return
        if self.mode == "emit" and engine == self.engine:
            self.eng.wait_ge(self.handles[sem], v)

    def bar(self, engine, fn):
        self.barn += 1
        t = ("bar", self.barn)
        self.op(engine, fn, inc=("sB", 1), tag=t)
        self.wait(engine, t)

    def settag(self, tag, other):
        self.tags[tag] = self.tags[other]

    def zerotag(self, tag):
        self.tags[tag] = ("sB", 0)


def build_program():
    nc = bass.Bass("TRN2", target_bir_lowering=False, debug=False,
                   detect_race_conditions=False)

    x_in = nc.dram_tensor("x", [TC, D], F32, kind="ExternalInput")
    wq_h = nc.dram_tensor("wq", [D, D], F32, kind="ExternalInput")
    wk_h = nc.dram_tensor("wk", [D, 512], F32, kind="ExternalInput")
    wv_h = nc.dram_tensor("wv", [D, 512], F32, kind="ExternalInput")
    wo_h = nc.dram_tensor("wo", [D, D], F32, kind="ExternalInput")
    wiq_h = nc.dram_tensor("wiq", [D, HD], F32, kind="ExternalInput")
    wik_h = nc.dram_tensor("wik", [D, HD], F32, kind="ExternalInput")
    cw_h = nc.dram_tensor("cw", [2, HD], F32, kind="ExternalInput")
    wg_h = nc.dram_tensor("wg_shard", [II // 8, D], F32, kind="ExternalInput")
    wu_h = nc.dram_tensor("wu_shard", [II // 8, D], F32, kind="ExternalInput")
    wd_h = nc.dram_tensor("wd_shard", [D, II // 8], F32, kind="ExternalInput")
    slot_h = nc.dram_tensor("slot", [1, 4], F32, kind="ExternalInput")
    out_h = nc.dram_tensor("out", [TC, D], F32, kind="ExternalOutput")

    _ko = {"kind": "ExternalOutput"} if DBG["dump"] else {}
    x_mid = nc.dram_tensor("x_mid", [TC, D], F32, **_ko)
    x_att = nc.dram_tensor("x_att", [TC, D], F32, **_ko)
    interq_s = nc.dram_tensor("interq_s", [NT, II, 128], BF16, **_ko)
    qg_in = nc.dram_tensor("qg_in", [2, ND, 128, 512], F8)
    qu_in = nc.dram_tensor("qu_in", [2, ND, 128, 512], F8)
    qd_in = nc.dram_tensor("qd_in", [8, 128, D], F8)
    qg_all = nc.dram_tensor("qg_all", [NI512, ND, 128, 512], F8, addr_space="Shared")
    qu_all = nc.dram_tensor("qu_all", [NI512, ND, 128, 512], F8, addr_space="Shared")
    qd_all = nc.dram_tensor("qd_all", [NIT, 128, D], F8, addr_space="Shared")
    ATO_CK = 0
    ATO_CV = 128 * 512
    ATO_QI = 2 * 128 * 512
    ATO_CS = 3 * 128 * 512
    AT_N = ATO_CS + D
    at_in = nc.dram_tensor("at_in", [AT_N], BF16)
    qdbg = (nc.dram_tensor("qdbg", [48 * 128, 1024], BF16, kind="ExternalOutput")
            if DBG.get("dump_w") else None)
    bounce_f = nc.dram_tensor("bounce_f", [1024], F32)
    bounce_b = nc.dram_tensor("bounce_b", [1024], BF16)
    at_all = nc.dram_tensor("at_all", [4, AT_N], BF16)

    env = Env()

    from contextlib import ExitStack
    with ExitStack() as st:
        ec = st.enter_context
        env.handles = {s: ec(nc.semaphore(s)) for s in SEMS}

        arenaB = ec(nc.sbuf_tensor("arenaB", [128, 16384], BF16))
        arenaF = ec(nc.sbuf_tensor("arenaF", [128, 10240], F32))
        stat = ec(nc.sbuf_tensor("stat", [128, 256], F32))
        statI = ec(nc.sbuf_tensor("statI", [128, 128], I32))
        small = ec(nc.sbuf_tensor("small", [128, 512], F32))
        ident = ec(nc.sbuf_tensor("ident", [128, 128], BF16))
        ones1 = ec(nc.sbuf_tensor("ones1", [128, 1024], BF16))
        indt = ec(nc.sbuf_tensor("indt", [128, 8], BF16))
        with nc.sbuf_tensor("qxT", [128, 32768], F8) as qxT, \
             nc.sbuf_tensor("arena8", [128, 40960], F8) as arena8:
            pass
        with nc.sbuf_tensor("qwd_res", [128, 131072], F8) as qwd_res:
            pass
        with nc.sbuf_tensor("ckv", [128, 4096], BF16) as ckv, \
             nc.sbuf_tensor("tabc", [128, 2048], F32) as tabc, \
             nc.sbuf_tensor("tabs", [128, 2048], F32) as tabs, \
             nc.sbuf_tensor("kvbuf", [128, 16384], BF16) as kvbuf, \
             nc.sbuf_tensor("wstream", [128, 16384], BF16) as wstream:
            pass
        pb = [ec(nc.psum_tensor(f"pb{i}", [128, 512], F32)) for i in range(6)]
        pt0 = ec(nc.psum_tensor("pt0", [128, 1024], BF16))
        pt1 = ec(nc.psum_tensor("pt1", [128, 1024], BF16))
        DBG["h"] = dict(arenaB=arenaB, arenaF=arenaF, stat=stat, statI=statI,
                        small=small, ident=ident, ones1=ones1, indt=indt,
                        qxT=qxT, arena8=arena8, qwd_res=qwd_res, ckv=ckv,
                        tabc=tabc, tabs=tabs, kvbuf=kvbuf, wstream=wstream)

        def ap(h, off, pattern):
            return bass.AP(h, off, pattern)

        def a8(off, *dims):
            return ap(arena8, off, [[40960, 128]] + list(dims))

        def aB(off, *dims):
            return ap(arenaB, off, [[16384, 128]] + list(dims))

        def aF(off, *dims):
            return ap(arenaF, off, [[10240, 128]] + list(dims))

        def kv(off, *dims):
            return ap(kvbuf, off, [[16384, 128]] + list(dims))

        def ws(off, *dims):
            return ap(wstream, off, [[16384, 128]] + list(dims))

        def program(E):
            op, wait, bar = E.op, E.wait, E.bar

            op("gpsimd", lambda e: e.memset(ident[:, :], 1.0))
            op("gpsimd", lambda e: e.affine_select(
                ident[:, :], ident[:, :], [[1, 128]], ALU.is_equal, 0.0,
                base=0, channel_multiplier=-1), inc=("sG", 1), tag="ident")
            op("gpsimd", lambda e: e.memset(ones1[:, :], 1.0), inc=("sG", 1),
               tag="ones")
            op("gpsimd", lambda e: e.iota(
                statI[:, 0:8], [[-16, 8]], channel_multiplier=1,
                allow_small_or_imprecise_dtypes=True),
               inc=("sG", 1), tag="indi")
            wait("vector", "indi")
            bar("vector", lambda e: e.tensor_copy(stat[:, 192:200], statI[:, 0:8]))
            bar("vector", lambda e: e.tensor_scalar(
                stat[:, 208:216], stat[:, 192:200], 0.0, 0.5, ALU.is_ge, ALU.mult))
            bar("vector", lambda e: e.scalar_tensor_tensor(
                stat[:, 192:200], stat[:, 192:200], 16.0, stat[:, 208:216],
                ALU.is_lt, ALU.mult))
            op("vector", lambda e: e.tensor_copy(indt[:, :], stat[:, 192:200]),
               inc=("sV", 1), tag="indt")

            phase_w(E)

            if PH["ffn1"]:
                gphase(E, "f1", x_in, None)
                dphase(E, "f1", x_in, x_mid, None, "ag_d", "f1")
            else:
                copy_dram(E, "cp1", x_in, x_mid, None)
            if PH["attn"]:
                aphase(E, "at")
                if not PH.get("stop_after_a"):
                    bphase(E, "at")
            else:
                copy_dram(E, "cp2", x_mid, x_att,
                          ("f1_dst{t}" if PH["ffn1"] else "cp1_s{t}"),
                          entry=("f1_dev15_3" if PH["ffn1"] else None))
            if PH.get("stop_after_a"):
                v = E.counts["sO"]
                if E.mode == "emit" and E.engine == "sync":
                    E.eng.wait_ge(E.handles["sO"], v)
                return
            if PH["ffn2"]:
                rdy = "at_xst{t}" if PH["attn"] else "cp2_s{t}"
                gphase(E, "f2", x_att, rdy)
                dphase(E, "f2", x_att, out_h, rdy, None, "f2")
            else:
                rdy = "at_xst{t}" if PH["attn"] else "cp2_s{t}"
                copy_dram(E, "cp3", x_att, out_h, rdy,
                          entry=("at_woev3_1_3" if PH["attn"] else "cp2_s15"))
            if DBG.get("dump_w"):
                dbg_readback(E)

            v = E.counts["sO"]
            if E.mode == "emit" and E.engine == "sync":
                E.eng.wait_ge(E.handles["sO"], v)

        def wq_chain(E, key, ncols, prev_tr=None):
            op, wait, bar = E.op, E.wait, E.bar
            nb = ncols // 32
            amax = stat[:, 0:nb]
            mt = statI[:, 0:nb]
            fasc = stat[:, 64:64 + nb]
            gasc = stat[:, 128:128 + nb]
            xs3 = aF(0, [32, nb], [1, 32])
            bar("vector", lambda e: e.tensor_reduce(
                ap(stat, 0, [[256, 128], [1, nb], [0, 1]]), xs3, AX.X,
                ALU.max, apply_absolute_value=True))
            bar("vector", lambda e: e.tensor_scalar(
                mt, amax.bitcast(I32), 0x7F800000, None, ALU.bitwise_and))
            bar("vector", lambda e: e.tensor_scalar(
                fasc.bitcast(I32), mt, -1, C255, ALU.mult, ALU.add))
            bar("vector", lambda e: e.tensor_scalar(
                gasc.bitcast(I32), mt, CW_BACK, None, ALU.add))
            fab = ap(stat, 64, [[256, 128], [1, nb], [0, 32]])
            gab = ap(stat, 128, [[256, 128], [1, nb], [0, 32]])
            t0f = aF(2048, [1, ncols])
            t03 = aF(2048, [32, nb], [1, 32])
            bar("vector", lambda e: e.tensor_tensor(t03, xs3, fab, ALU.mult))
            bar("vector", lambda e: e.tensor_scalar(
                t0f, t0f, WCLIP, -WCLIP, ALU.min, ALU.max))
            op("vector", lambda e: e.engine_nop(), inc=("sV", 1), tag=f"{key}_t0")
            wait("scalar", f"{key}_t0")
            aw = aF(4096, [1, ncols])
            op("scalar", lambda e: e.activation(aw, t0f, AF.Abs),
               inc=("sS", 1), tag=f"{key}_aw")
            wait("vector", f"{key}_aw")
            ge1 = aF(6144, [1, ncols])
            ge2 = aF(8192, [1, ncols])
            s2e = aF(4096, [1, ncols])
            bar("vector", lambda e: e.tensor_scalar(ge1, aw, 1.0, None, ALU.is_ge))
            bar("vector", lambda e: e.tensor_scalar(ge2, aw, 2.0, None, ALU.is_ge))
            bar("vector", lambda e: e.tensor_scalar(s2e, ge1, -0.5, 1.0, ALU.mult, ALU.add))
            bar("vector", lambda e: e.scalar_tensor_tensor(s2e, ge2, -0.25, s2e, ALU.mult, ALU.add))
            bar("vector", lambda e: e.tensor_scalar(s2e, s2e, 2.0 ** -7, None, ALU.mult))
            t8 = a8(0, [1, ncols])
            bar("vector", lambda e: e.tensor_tensor(t8, t0f, s2e, ALU.mult))
            bar("vector", lambda e: e.tensor_scalar(ge1, ge1, 1.0, 1.0, ALU.mult, ALU.add))
            bar("vector", lambda e: e.scalar_tensor_tensor(ge1, ge2, 2.0, ge1, ALU.mult, ALU.add))
            bar("vector", lambda e: e.tensor_scalar(ge1, ge1, 2.0 ** 7, None, ALU.mult))
            bar("vector", lambda e: e.tensor_tensor(t0f, t8, ge1, ALU.mult))
            qwb3 = aB(0, [32, nb], [1, 32])
            if prev_tr is not None:
                wait("vector", prev_tr)
            op("vector", lambda e: e.tensor_tensor(qwb3, t03, gab, ALU.mult),
               inc=("sV", 1), tag=f"{key}_qwb")

        def wq_transposes(E, key, ntiles, prev_store):
            op, wait = E.op, E.wait
            wait("tensor", f"{key}_qwb")
            for ft in range(ntiles):
                pdst = pt0 if ft % 2 == 0 else pt1
                if ft >= 2:
                    wait("tensor", f"{key}_ev{ft - 2}")
                op("tensor", lambda e, ft=ft, pdst=pdst: e.transpose(
                    pdst[:, 0:128], aB(ft * 128, [1, 128]), ident[:, :]),
                   inc=("sT", 1), tag=f"{key}_tr{ft}")
                wait("vector", f"{key}_tr{ft}")
                if ft == 0 and prev_store is not None:
                    wait("vector", prev_store)
                op("vector", lambda e, ft=ft, pdst=pdst: e.tensor_copy(
                    a8(4096 + ft * 128, [1, 128]), pdst[:, 0:128]),
                   inc=("sV", 1), tag=f"{key}_ev{ft}")

        def phase_w(E):
            op, wait = E.op, E.wait
            wait("tensor", "ident")
            E.zerotag("wfree_0")
            prev_free = "wfree_0"
            prev_store = None
            prev_tr = None
            for kind, src, dst in (("g", wg_h, qg_in), ("u", wu_h, qu_in)):
                for j in range(8):
                    key = f"w{kind}{j}"
                    wait("scalar", prev_free)
                    op("scalar", lambda e, src=src, j=j: e.dma_start(
                        out=aF(0, [1, D]),
                        in_=ap(src, j * 128 * D, [[D, 128], [1, D]])),
                       inc=("sXL", 16), tag=f"{key}_ld")
                    wait("vector", f"{key}_ld")
                    wq_chain(E, key, D, prev_tr)
                    prev_free = f"{key}_t0"
                    wq_transposes(E, key, ND, prev_store)
                    wait("gpsimd", f"{key}_ev{ND - 1}")
                    op("gpsimd", lambda e, j=j, dst=dst: e.dma_start(
                        out=ap(dst, (j // 4) * (ND * 128 * 512) + (j % 4) * 128,
                               [[512, 128], [128 * 512, ND], [1, 128]]),
                        in_=a8(4096, [128, ND], [1, 128])),
                       inc=("sO", 16), tag=f"{key}_st")
                    prev_store = f"{key}_st"
                    prev_tr = f"{key}_tr{ND - 1}"
            for j in range(16):
                key = f"wd{j}"
                wait("scalar", prev_free)
                op("scalar", lambda e, j=j: e.dma_start(
                    out=aF(0, [1, 1024]),
                    in_=ap(wd_h, j * 128 * 1024, [[1024, 128], [1, 1024]])),
                   inc=("sXL", 16), tag=f"{key}_ld")
                wait("vector", f"{key}_ld")
                wq_chain(E, key, 1024, prev_tr)
                prev_free = f"{key}_t0"
                wq_transposes(E, key, 8, prev_store)
                wait("gpsimd", f"{key}_ev7")
                op("gpsimd", lambda e, j=j: e.dma_start(
                    out=ap(qd_in, j * 128, [[D, 128], [128 * D, 8], [1, 128]]),
                    in_=a8(4096, [128, 8], [1, 128])),
                   inc=("sO", 16), tag=f"{key}_st")
                prev_store = f"{key}_st"
                prev_tr = f"{key}_tr7"
            for t in ("wg7_st", "wu7_st", "wd15_st"):
                wait("gpsimd", t)
            if PH.get("dly_pre_ag"):
                for _ in range(PH["dly_pre_ag"]):
                    op("gpsimd", lambda e: e.engine_nop())
            op("gpsimd", lambda e: e.collective_compute(
                "AllGather", ALU.bypass, replica_groups=[list(range(8))],
                ins=[qg_in.ap().opt()], outs=[qg_all.ap().opt()]),
               inc=("sC", 1), tag="ag_g")
            op("gpsimd", lambda e: e.collective_compute(
                "AllGather", ALU.bypass, replica_groups=[list(range(8))],
                ins=[qu_in.ap().opt()], outs=[qu_all.ap().opt()]),
               inc=("sC", 1), tag="ag_u")
            op("gpsimd", lambda e: e.collective_compute(
                "AllGather", ALU.bypass, replica_groups=[list(range(8))],
                ins=[qd_in.ap().opt()], outs=[qd_all.ap().opt()]),
               inc=("sC", 1), tag="ag_d")

        def ln_stats(E, key, xoff, save_col=None, prev_user=None):
            op, wait, bar = E.op, E.wait, E.bar
            if prev_user is not None:
                E.wait("vector", prev_user)
            xf = aF(xoff, [1, D])
            scr = aF(8192, [1, D])
            op("scalar", lambda e: e.activation(
                scr, xf, AF.Identity, accum_out=small[:, 2:3]), inc=("sS", 1))
            op("scalar", lambda e: e.activation(
                scr, xf, AF.Square, accum_out=small[:, 3:4]),
               inc=("sS", 1), tag=f"{key}_sums")
            wait("vector", f"{key}_sums")
            bar("vector", lambda e: e.tensor_scalar(
                small[:, 4:5], small[:, 2:3], 1.0 / D, None, ALU.mult))
            bar("vector", lambda e: e.tensor_tensor(
                small[:, 5:6], small[:, 4:5], small[:, 4:5], ALU.mult))
            bar("vector", lambda e: e.scalar_tensor_tensor(
                small[:, 6:7], small[:, 3:4], 1.0 / D, small[:, 5:6],
                ALU.mult, ALU.subtract))
            bar("vector", lambda e: e.tensor_scalar(
                small[:, 6:7], small[:, 6:7], EPS, None, ALU.add))
            op("vector", lambda e: e.engine_nop(), inc=("sV", 1), tag=f"{key}_var")
            wait("scalar", f"{key}_var")
            op("scalar", lambda e: e.activation(
                small[:, 7:8], small[:, 6:7], AF.Sqrt), inc=("sS", 1), tag=f"{key}_sd")
            wait("vector", f"{key}_sd")
            bar("vector", lambda e: e.reciprocal(small[:, 8:9], small[:, 7:8]))
            bar("vector", lambda e: e.tensor_tensor(
                small[:, 9:10], small[:, 8:9], small[:, 8:9], ALU.mult))
            bar("vector", lambda e: e.tensor_tensor(
                small[:, 9:10], small[:, 9:10], small[:, 6:7], ALU.mult))
            bar("vector", lambda e: e.tensor_scalar(
                small[:, 9:10], small[:, 9:10], -0.5, 1.5, ALU.mult, ALU.add))
            bar("vector", lambda e: e.tensor_tensor(
                small[:, 0:1], small[:, 8:9], small[:, 9:10], ALU.mult))
            bar("vector", lambda e: e.scalar_tensor_tensor(
                small[:, 1:2], small[:, 4:5], -1.0, small[:, 0:1],
                ALU.mult, ALU.mult))
            if save_col is not None:
                bar("vector", lambda e, c=save_col: e.tensor_copy(
                    small[:, 16 + c:17 + c], small[:, 0:1]))
                bar("vector", lambda e, c=save_col: e.tensor_copy(
                    small[:, 32 + c:33 + c], small[:, 1:2]))
            op("vector", lambda e: e.engine_nop(), inc=("sV", 1), tag=f"{key}_ab")

        def act_quant(E, tag, srcoff, n, dst_fn, cga, tqoff=7168):
            op, wait, bar = E.op, E.wait, E.bar
            nb = n // 32
            amax = stat[:, 192:192 + nb]
            mt = statI[:, 64:64 + nb]
            fasc = stat[:, 0:nb]
            gasc = stat[:, 64:64 + nb]
            src3 = aF(srcoff, [32, nb], [1, 32])
            bar("vector", lambda e: e.tensor_reduce(
                ap(stat, 192, [[256, 128], [1, nb], [0, 1]]), src3, AX.X,
                ALU.max, apply_absolute_value=True))
            bar("vector", lambda e: e.tensor_scalar(
                mt, amax.bitcast(I32), 0x7F800000, None, ALU.bitwise_and))
            bar("vector", lambda e: e.tensor_scalar(
                fasc.bitcast(I32), mt, -1, C250, ALU.mult, ALU.add))
            bar("vector", lambda e: e.tensor_scalar(
                gasc.bitcast(I32), mt, cga, None, ALU.add))
            fab = ap(stat, 0, [[256, 128], [1, nb], [0, 32]])
            gab = ap(stat, 64, [[256, 128], [1, nb], [0, 32]])
            tq3 = aF(tqoff, [32, nb], [1, 32])
            tqf = aF(tqoff, [1, n])
            bar("vector", lambda e: e.tensor_tensor(tq3, src3, fab, ALU.mult))
            t8 = a8(32768, [1, n])
            t83 = a8(32768, [32, nb], [1, 32])
            bar("vector", lambda e: e.tensor_scalar(
                t8, tqf, CLIP_A, -CLIP_A, ALU.min, ALU.max))
            op("vector", lambda e: e.tensor_tensor(dst_fn(), t83, gab, ALU.mult),
               inc=("sV", 1), tag=tag)

        def gphase(E, tag, src_h, rdy):
            op, wait, bar = E.op, E.wait, E.bar
            if tag == "f2":
                for eng in ("scalar", "vector", "tensor", "sync", "gpsimd"):
                    wait(eng, "at_done" if PH["attn"] else "cp2_s15")
            for t in range(NT):
                xoff = 0 if t % 2 == 0 else 2048
                if t >= 2:
                    wait("scalar", f"{tag}_pfree{t - 2}")
                if rdy is not None:
                    wait("scalar", rdy.format(t=t))
                if t == 0 and tag == "f1":
                    wait("scalar", "wd15_qwb")
                op("scalar", lambda e, t=t, xoff=xoff: e.dma_start(
                    out=aF(xoff, [1, D]),
                    in_=ap(src_h, t * 128 * D, [[D, 128], [1, D]])),
                   inc=("sXL", 16), tag=f"{tag}_xl{t}")
                wait("scalar", f"{tag}_xl{t}")
                wait("vector", f"{tag}_xl{t}")
                ln_stats(E, f"{tag}_ln{t}", xoff,
                         prev_user=(f"{tag}_xln{t - 1}" if t >= 1 else None))
                wait("scalar", f"{tag}_ln{t}_ab")
                if t >= 1:
                    wait("scalar", f"{tag}_qx{t - 1}")
                op("scalar", lambda e, xoff=xoff: e.activation(
                    aF(4096, [1, D]), aF(xoff, [1, D]),
                    AF.Identity, bias=small[:, 1:2], scale=small[:, 0:1]),
                   inc=("sS", 1), tag=f"{tag}_xln{t}")
                E.settag(f"{tag}_pfree{t}", f"{tag}_xln{t}")
                wait("vector", f"{tag}_xln{t}")
                if t >= 1:
                    wait("vector", f"{tag}_qxt{t - 1}_{ND - 1}")
                act_quant(E, f"{tag}_qx{t}", 4096, D,
                          lambda: aB(0, [32, 64], [1, 32]), CGA_X, tqoff=6144)
                wait("tensor", f"{tag}_qx{t}")
                for ft in range(ND):
                    pdst = pt0 if ft % 2 == 0 else pt1
                    if ft >= 2:
                        wait("tensor", f"{tag}_qxe{t}_{ft - 2}")
                    elif t >= 1:
                        wait("tensor", f"{tag}_qxe{t - 1}_{ND - 2 + ft}")
                    op("tensor", lambda e, ft=ft, pdst=pdst: e.transpose(
                        pdst[:, 0:128], aB(ft * 128, [1, 128]),
                        ident[:, :]), inc=("sT", 1), tag=f"{tag}_qxt{t}_{ft}")
                    wait("vector", f"{tag}_qxt{t}_{ft}")
                    op("vector", lambda e, t=t, ft=ft, pdst=pdst:
                       e.tensor_copy(
                           ap(qxT, ft * TC + t * 128, [[32768, 128], [1, 128]]),
                           pdst[:, 0:128]),
                       inc=("sV", 1), tag=f"{tag}_qxe{t}_{ft}")
            def iq_stage(gi):
                isub, t = gi // NT, gi % NT
                par = gi % 2
                ptr = pt0 if par == 0 else pt1
                wait("tensor", f"{tag}_qi{isub}_{t}")
                if gi >= 2:
                    pi = gi - 2
                    wait("tensor", f"{tag}_iqe{pi // NT}_{pi % NT}")
                else:
                    wait("tensor", f"{tag}_qxe{NT - 1}_{ND - 2 + par}")
                for it in range(4):
                    op("tensor", lambda e, it=it, ptr=ptr, par=par: e.transpose(
                        ptr[:, it * 128:(it + 1) * 128],
                        aB(14336 + par * 512 + it * 128, [1, 128]), ident[:, :]),
                       inc=(("sT", 1) if it == 3 else None),
                       tag=(f"{tag}_iqt{isub}_{t}" if it == 3 else None))
                wait("vector", f"{tag}_iqt{isub}_{t}")
                if gi >= 2:
                    pi = gi - 2
                    wait("vector", f"{tag}_iqs{pi // NT}_{pi % NT}")
                op("vector", lambda e, ptr=ptr, par=par: e.tensor_copy(
                    aB(15360 + par * 512, [1, 512]), ptr[:, 0:512]),
                   inc=("sV", 1), tag=f"{tag}_iqe{isub}_{t}")
                wait("gpsimd", f"{tag}_iqe{isub}_{t}")
                op("gpsimd", lambda e, isub=isub, t=t, par=par: e.dma_start(
                    out=ap(interq_s, t * II * 128 + isub * 512 * 128,
                           [[128, 128], [128 * 128, 4], [1, 128]]),
                    in_=aB(15360 + par * 512, [128, 4], [1, 128])),
                   inc=("sO", 16), tag=f"{tag}_iqs{isub}_{t}")

            for isub in range(NI512):
                parw = isub % 2
                goff = parw * 8192
                uoff = 16384 + parw * 8192
                if isub >= 2:
                    wait("sync", f"{tag}_wfree{isub - 2}")
                if isub == 0:
                    wait("sync", "ag_g")
                    wait("sync", "ag_u")
                    if PH.get("dly_post_ag"):
                        for _ in range(PH["dly_post_ag"]):
                            op("sync", lambda e: e.nop())
                    wait("sync", f"{tag}_qxe0_{ND - 1}")
                op("sync", lambda e, isub=isub, goff=goff: e.dma_start(
                    out=a8(goff, [1, 8192]),
                    in_=ap(qg_all, isub * ND * 128 * 512,
                           [[512, 128], [128 * 512, ND], [1, 512]])),
                   inc=("sWL", 16), tag=f"{tag}_wg{isub}")
                op("sync", lambda e, isub=isub, uoff=uoff: e.dma_start(
                    out=a8(uoff, [1, 8192]),
                    in_=ap(qu_all, isub * ND * 128 * 512,
                           [[512, 128], [128 * 512, ND], [1, 512]])),
                   inc=("sWL", 16), tag=f"{tag}_wu{isub}")
                for t in range(NT):
                    gi = isub * NT + t
                    par = gi % 2
                    b = (gi % 3) * 2
                    pg, pu = pb[b], pb[b + 1]
                    wait("tensor", f"{tag}_wg{isub}")
                    wait("tensor", f"{tag}_wu{isub}")
                    if isub == 0:
                        wait("tensor", f"{tag}_qxe{t}_{ND - 1}")
                    if gi >= 3:
                        pi = gi - 3
                        wait("tensor", f"{tag}_int{pi // NT}_{pi % NT}")
                    for ft in range(ND):
                        op("tensor", lambda e, t=t, ft=ft, pg=pg, goff=goff: e.matmul(
                            pg[:, :],
                            ap(qxT, ft * TC + t * 128, [[32768, 128], [1, 128]]),
                            a8(goff + ft * 512, [1, 512]),
                            start=(ft == 0), stop=(ft == ND - 1)),
                           inc=(("sT", 1) if ft == ND - 1 else None),
                           tag=(f"{tag}_mmg{isub}_{t}" if ft == ND - 1 else None))
                        op("tensor", lambda e, t=t, ft=ft, pu=pu, uoff=uoff: e.matmul(
                            pu[:, :],
                            ap(qxT, ft * TC + t * 128, [[32768, 128], [1, 128]]),
                            a8(uoff + ft * 512, [1, 512]),
                            start=(ft == 0), stop=(ft == ND - 1)),
                           inc=(("sT", 1) if ft == ND - 1 else None),
                           tag=(f"{tag}_mmu{isub}_{t}" if ft == ND - 1 else None))
                    if t == NT - 1:
                        E.settag(f"{tag}_wfree{isub}", f"{tag}_mmu{isub}_{t}")
                    wait("scalar", f"{tag}_mmg{isub}_{t}")
                    if gi >= 2:
                        pi = gi - 2
                        wait("scalar", f"{tag}_int{pi // NT}_{pi % NT}")
                    op("scalar", lambda e, pg=pg, par=par: e.activation(
                        aF(6144 + par * 512, [1, 512]), pg[:, :], AF.Silu,
                        scale=2.0 ** -6),
                       inc=("sS", 1), tag=f"{tag}_silu{isub}_{t}")
                    wait("vector", f"{tag}_silu{isub}_{t}")
                    wait("vector", f"{tag}_mmu{isub}_{t}")
                    bar("vector", lambda e, pu=pu, par=par: e.tensor_tensor(
                        aF(7168 + par * 512, [1, 512]), aF(6144 + par * 512, [1, 512]),
                        pu[:, :], ALU.mult))
                    op("vector", lambda e: e.engine_nop(), inc=("sV", 1),
                       tag=f"{tag}_int{isub}_{t}")
                    if gi >= 2:
                        pi = gi - 2
                        wait("vector", f"{tag}_iqt{pi // NT}_{pi % NT}")
                    act_quant(E, f"{tag}_qi{isub}_{t}", 7168 + par * 512, 512,
                              lambda par=par: aB(14336 + par * 512, [32, 16], [1, 32]),
                              CGA_I, tqoff=8192)
                    if gi >= 1:
                        iq_stage(gi - 1)
            iq_stage(NI512 * NT - 1)

        def dphase(E, tag, src_h, dst_h, rdy, agtag, gtag):
            op, wait, bar = E.op, E.wait, E.bar
            wait("sync", f"{gtag}_mmu{NI512 - 1}_{NT - 1}")
            wait("sync", f"{gtag}_qi{NI512 - 1}_{NT - 1}")
            if agtag:
                wait("sync", agtag)
            for r in range(8):
                op("sync", lambda e, r=r: e.dma_start(
                    out=ap(qwd_res, r * 8 * D, [[131072, 128], [D, 8], [1, D]]),
                    in_=ap(qd_all, r * 8 * 128 * D, [[D, 128], [128 * D, 8], [1, D]])),
                   inc=("sWL", 16), tag=f"{tag}_dw{r}")
            for t in range(NT):
                poff = (t % 2) * 8192
                if t >= 2:
                    wait("scalar", f"{tag}_plfree{t - 2}")
                if t < 2:
                    wait("scalar", f"{gtag}_iqs{NI512 - 1}_{NT - 1}")
                else:
                    wait("scalar", f"{gtag}_iqs{NI512 - 1}_{t}")
                op("scalar", lambda e, t=t, poff=poff: e.dma_start(
                    out=aB(poff, [1, 8192]),
                    in_=ap(interq_s, t * II * 128,
                           [[128, 128], [128 * 128, NIT], [1, 128]])),
                   inc=("sPL", 16), tag=f"{tag}_pl{t}")
                xoff = 0 if t % 2 == 0 else 2048
                if t >= 2:
                    wait("scalar", f"{tag}_xrfree{t - 2}")
                if rdy is not None:
                    wait("scalar", rdy.format(t=t))
                op("scalar", lambda e, t=t, xoff=xoff: e.dma_start(
                    out=aF(xoff, [1, D]),
                    in_=ap(src_h, t * 128 * D, [[D, 128], [1, D]])),
                   inc=("sXL", 16), tag=f"{tag}_xr{t}")
                wait("tensor", f"{tag}_pl{t}")
                if t == 0:
                    for r in range(8):
                        wait("tensor", f"{tag}_dw{r}")
                banks = [(4 * t + k) % 6 for k in range(4)]
                for k in range(4):
                    n = 4 * t + k - 6
                    if n >= 0:
                        wait("tensor", f"{tag}_dev{n // 4}_{n % 4}")
                for it in range(NIT):
                    for k in range(4):
                        bk = pb[banks[k]]
                        op("tensor", lambda e, it=it, k=k, bk=bk, poff=poff: e.matmul(
                            bk[:, :], aB(poff + it * 128, [1, 128]),
                            ap(qwd_res, it * D + k * 512, [[131072, 128], [1, 512]]),
                            start=(it == 0), stop=(it == NIT - 1)),
                           inc=(("sT", 1) if it == NIT - 1 else None),
                           tag=(f"{tag}_dmm{t}_{k}" if it == NIT - 1 else None))
                E.settag(f"{tag}_plfree{t}", f"{tag}_dmm{t}_3")
                ooff = 4096 if t % 2 == 0 else 6144
                for k in range(4):
                    bk = pb[banks[k]]
                    wait("vector", f"{tag}_dmm{t}_{k}")
                    if k == 0:
                        wait("vector", f"{tag}_xr{t}")
                        if t >= 2:
                            wait("vector", f"{tag}_dst{t - 2}")
                    op("vector", lambda e, k=k, bk=bk, xoff=xoff, ooff=ooff:
                       e.scalar_tensor_tensor(
                           aF(ooff + k * 512, [1, 512]), bk[:, :],
                           0.5 * 2.0 ** -6, aF(xoff + k * 512, [1, 512]),
                           ALU.mult, ALU.add),
                       inc=("sV", 1), tag=f"{tag}_dev{t}_{k}")
                E.settag(f"{tag}_xrfree{t}", f"{tag}_dev{t}_3")
                wait("gpsimd", f"{tag}_dev{t}_3")
                op("gpsimd", lambda e, t=t, ooff=ooff: e.dma_start(
                    out=ap(dst_h, t * 128 * D, [[D, 128], [1, D]]),
                    in_=aF(ooff, [1, D])),
                   inc=("sO", 16), tag=f"{tag}_dst{t}")

        def copy_dram(E, tag, src, dst, rdy, entry=None):
            op, wait = E.op, E.wait
            for t in range(NT):
                if t == 0 and tag == "cp1":
                    wait("scalar", "wd15_qwb")
                if t < 2 and entry is not None:
                    wait("scalar", entry)
                if t >= 2:
                    wait("scalar", f"{tag}_s{t - 2}")
                if rdy is not None:
                    wait("scalar", rdy.format(t=t))
                xoff = 0 if t % 2 == 0 else 2048
                op("scalar", lambda e, t=t, xoff=xoff: e.dma_start(
                    out=aF(xoff, [1, D]),
                    in_=ap(src, t * 128 * D, [[D, 128], [1, D]])),
                   inc=("sXL", 16), tag=f"{tag}_l{t}")
                wait("gpsimd", f"{tag}_l{t}")
                op("gpsimd", lambda e, t=t, xoff=xoff: e.dma_start(
                    out=ap(dst, t * 128 * D, [[D, 128], [1, D]]),
                    in_=aF(xoff, [1, D])),
                   inc=("sO", 16), tag=f"{tag}_s{t}")

        def aphase(E, tag):
            op, wait, bar = E.op, E.wait, E.bar
            fdone = "f1_dmm15_3" if PH["ffn1"] else "cp1_s15"
            for eng in ("gpsimd", "scalar", "vector", "tensor", "sync"):
                wait(eng, fdone)
            xrdy = "f1_dst{t}" if PH["ffn1"] else "cp1_s{t}"
            op("gpsimd", lambda e: e.iota(
                statI[:, 0:128], [[1, 64], [0, 2]], channel_multiplier=0,
                allow_small_or_imprecise_dtypes=True),
               inc=("sG", 1), tag=f"{tag}_iotap")
            wait("vector", f"{tag}_iotap")
            bar("vector", lambda e: e.tensor_copy(aF(5760, [1, 128]), statI[:, 0:128]))
            op("vector", lambda e: e.engine_nop(), inc=("sV", 1), tag=f"{tag}_j2")
            wait("scalar", f"{tag}_j2")
            op("scalar", lambda e: e.activation(
                aF(5504, [1, 128]), aF(5760, [1, 128]),
                AF.Exp, scale=-float(np.log(THETA)) / 64.0),
               inc=("sS", 1), tag=f"{tag}_freqs")
            wait("gpsimd", f"{tag}_j2")
            op("gpsimd", lambda e: e.iota(
                statI[:, 0:1], [[0, 1]], channel_multiplier=1,
                allow_small_or_imprecise_dtypes=True),
               inc=("sG", 1), tag=f"{tag}_iotac")
            op("scalar", lambda e: e.dma_start(
                out=ap(small, 40, [[512, 1], [1, 4]]), in_=slot_h.ap()),
               inc=("sXL", 16), tag=f"{tag}_slot")
            wait("vector", f"{tag}_slot")
            bar("vector", lambda e: e.tensor_copy(
                ap(arenaB, 4300, [[16384, 1], [1, 4]]),
                ap(small, 40, [[512, 1], [1, 4]])))
            op("vector", lambda e: e.engine_nop(), inc=("sV", 1),
               tag=f"{tag}_slotbf")
            wait("tensor", f"{tag}_slotbf")
            wait("tensor", "ones")
            op("tensor", lambda e: e.matmul(
                pb[3][:, 0:4], ap(ones1, 0, [[1024, 1], [1, 128]]),
                ap(arenaB, 4300, [[16384, 1], [1, 4]]),
                start=True, stop=True), inc=("sT", 1), tag=f"{tag}_slotmm")
            wait("vector", f"{tag}_slotmm")
            op("vector", lambda e: e.tensor_copy(
                ap(small, 44, [[512, 128], [1, 4]]), pb[3][:, 0:4]),
               inc=("sV", 1), tag=f"{tag}_slotb")
            wait("vector", f"{tag}_slotb")
            wait("vector", f"{tag}_iotac")
            wait("vector", f"{tag}_freqs")
            bar("vector", lambda e: e.tensor_copy(small[:, 48:49], statI[:, 0:1]))
            op("vector", lambda e: e.scalar_tensor_tensor(
                small[:, 49:50], small[:, 45:46], 2048.0, small[:, 48:49],
                ALU.mult, ALU.add), inc=("sV", 1), tag=f"{tag}_pos")
            wait("gpsimd", f"{tag}_j2")
            wait("gpsimd", f"{tag}_pos")
            op("gpsimd", lambda e: e.iota(
                statI[:, 0:128], [[0, 64], [1, 2]], channel_multiplier=0,
                allow_small_or_imprecise_dtypes=True),
               inc=("sG", 1), tag=f"{tag}_sgni")
            wait("vector", f"{tag}_sgni")
            bar("vector", lambda e: e.tensor_copy(aF(5632, [1, 128]), statI[:, 0:128]))
            op("vector", lambda e: e.tensor_scalar(
                aF(5632, [1, 128]), aF(5632, [1, 128]), 2.0, -1.0,
                ALU.mult, ALU.add), inc=("sV", 1), tag=f"{tag}_sgnr")
            c1 = float(np.float32(6.28125))
            c2 = float(np.float32(0.001934051513671875))
            c3 = float(2 * np.pi) - c1 - c2

            def rope_tab(t):
                AA, BB, CC, DD = 5760, 5888, 6016, 6144
                if t == 0:
                    wait("vector", f"{tag}_sgnr")
                    wait("vector", f"{tag}_freqs")
                bar("vector", lambda e, t=t: e.tensor_scalar(
                    small[:, 50:51], small[:, 49:50], float(128 * t), None, ALU.add))
                bar("vector", lambda e: e.tensor_scalar(
                    aF(AA, [1, 128]), aF(5504, [1, 128]), small[:, 50:51],
                    None, ALU.mult))
                bar("vector", lambda e: e.tensor_scalar(
                    aF(BB, [1, 128]), aF(AA, [1, 128]),
                    float(1.0 / (2 * np.pi)), None, ALU.mult))
                bar("vector", lambda e: e.tensor_copy(
                    statI[:, 0:128], aF(BB, [1, 128])))
                bar("vector", lambda e: e.tensor_copy(
                    aF(BB, [1, 128]), statI[:, 0:128]))
                bar("vector", lambda e: e.scalar_tensor_tensor(
                    aF(CC, [1, 128]), aF(BB, [1, 128]), -c1,
                    aF(AA, [1, 128]), ALU.mult, ALU.add))
                bar("vector", lambda e: e.scalar_tensor_tensor(
                    aF(CC, [1, 128]), aF(BB, [1, 128]), -c2,
                    aF(CC, [1, 128]), ALU.mult, ALU.add))
                bar("vector", lambda e: e.scalar_tensor_tensor(
                    aF(CC, [1, 128]), aF(BB, [1, 128]), -c3,
                    aF(CC, [1, 128]), ALU.mult, ALU.add))
                bar("vector", lambda e: e.tensor_scalar(
                    aF(DD, [1, 128]), aF(CC, [1, 128]), 0.0, None, ALU.add))
                bar("vector", lambda e: e.tensor_scalar(
                    aF(AA, [1, 128]), aF(DD, [1, 128]), -float(np.pi), None, ALU.is_lt))
                bar("vector", lambda e: e.tensor_scalar(
                    aF(BB, [1, 128]), aF(DD, [1, 128]), float(np.pi), None, ALU.is_gt))
                bar("vector", lambda e: e.tensor_tensor(
                    aF(AA, [1, 128]), aF(AA, [1, 128]), aF(BB, [1, 128]), ALU.subtract))
                op("vector", lambda e: e.scalar_tensor_tensor(
                    aF(DD, [1, 128]), aF(AA, [1, 128]), float(2 * np.pi),
                    aF(DD, [1, 128]), ALU.mult, ALU.add),
                   inc=("sV", 1), tag=f"{tag}_wr0{t}")
                wait("scalar", f"{tag}_wr0{t}")
                op("scalar", lambda e: e.activation(
                    aF(AA, [1, 128]), aF(DD, [1, 128]), AF.Sin),
                   inc=("sS", 1), tag=f"{tag}_sin0{t}")
                bar("vector", lambda e: e.tensor_scalar(
                    aF(BB, [1, 128]), aF(CC, [1, 128]), float(np.pi / 2), None, ALU.add))
                wait("vector", f"{tag}_sin0{t}")
                bar("vector", lambda e: e.tensor_scalar(
                    aF(CC, [1, 128]), aF(BB, [1, 128]), -float(np.pi), None, ALU.is_lt))
                bar("vector", lambda e: e.tensor_scalar(
                    aF(DD, [1, 128]), aF(BB, [1, 128]), float(np.pi), None, ALU.is_gt))
                bar("vector", lambda e: e.tensor_tensor(
                    aF(CC, [1, 128]), aF(CC, [1, 128]), aF(DD, [1, 128]), ALU.subtract))
                op("vector", lambda e: e.scalar_tensor_tensor(
                    aF(BB, [1, 128]), aF(CC, [1, 128]), float(2 * np.pi),
                    aF(BB, [1, 128]), ALU.mult, ALU.add),
                   inc=("sV", 1), tag=f"{tag}_wr1{t}")
                wait("scalar", f"{tag}_wr1{t}")
                op("scalar", lambda e, t=t: e.activation(
                    tabc[:, t * 128:(t + 1) * 128], aF(BB, [1, 128]), AF.Sin),
                   inc=("sS", 1), tag=f"{tag}_tab{t}")
                op("vector", lambda e, t=t: e.tensor_tensor(
                    tabs[:, t * 128:(t + 1) * 128], aF(AA, [1, 128]),
                    aF(5632, [1, 128]), ALU.mult),
                   inc=("sV", 1), tag=f"{tag}_tabs{t}")
            op("gpsimd", lambda e: e.dma_start(
                out=ws(0, [512, ND], [1, 512]),
                in_=ap(wk_h, 0, [[512, 128], [512 * 128, ND], [1, 512]])),
               inc=("sWA", 16), tag=f"{tag}_wk")
            op("gpsimd", lambda e: e.dma_start(
                out=ws(8192, [512, ND], [1, 512]),
                in_=ap(wv_h, 0, [[512, 128], [512 * 128, ND], [1, 512]])),
               inc=("sWA", 16), tag=f"{tag}_wv")
            op("gpsimd", lambda e: e.dma_start(
                out=ap(ckv, 0, [[4096, 128], [128, ND], [1, 128]]),
                in_=ap(wiq_h, 0, [[128, 128], [128 * 128, ND], [1, 128]])),
               inc=("sWA", 16), tag=f"{tag}_wiq")
            op("scalar", lambda e: e.dma_start(
                out=ap(small, 56, [[512, 1], [1, 256]]),
                in_=ap(cw_h, 0, [[256, 1], [1, 256]])),
               inc=("sXL", 16), tag=f"{tag}_cw")
            wait("vector", f"{tag}_cw")
            bar("vector", lambda e: e.tensor_copy(
                ap(arenaB, 4096, [[16384, 1], [1, 256]]),
                ap(small, 56, [[512, 1], [1, 256]])))
            op("vector", lambda e: e.engine_nop(), inc=("sV", 1), tag=f"{tag}_cwb")
            wait("tensor", f"{tag}_cwb")
            wait("tensor", "ones")
            op("tensor", lambda e: e.matmul(
                pb[2][:, 0:128], ap(ones1, 0, [[1024, 1], [1, 128]]),
                ap(arenaB, 4096, [[16384, 1], [1, 128]]),
                start=True, stop=True), inc=("sT", 1))
            op("tensor", lambda e: e.matmul(
                pb[2][:, 128:256], ap(ones1, 0, [[1024, 1], [1, 128]]),
                ap(arenaB, 4096 + 128, [[16384, 1], [1, 128]]),
                start=True, stop=True), inc=("sT", 1), tag=f"{tag}_cwt")
            wait("vector", f"{tag}_cwt")
            op("vector", lambda e: e.tensor_copy(
                aF(5120, [1, 256]), pb[2][:, 0:256]),
               inc=("sV", 1), tag=f"{tag}_cwtile")
            for t in range(NT):
                xoff = 0 if t % 2 == 0 else 2048
                if t >= 2:
                    wait("scalar", f"{tag}_afree{t - 2}")
                if t < 2 and PH["ffn1"]:
                    wait("scalar", "f1_dev15_3")
                wait("scalar", xrdy.format(t=t))
                op("scalar", lambda e, t=t, xoff=xoff: e.dma_start(
                    out=aF(xoff, [1, D]),
                    in_=ap(x_mid, t * 128 * D, [[D, 128], [1, D]])),
                   inc=("sXL", 16), tag=f"{tag}_xl{t}")
                rope_tab(t)
                wait("scalar", f"{tag}_xl{t}")
                wait("vector", f"{tag}_xl{t}")
                ln_stats(E, f"{tag}_ln{t}", xoff, save_col=t,
                         prev_user=(f"{tag}_xln{t - 1}" if t >= 1 else None))
                wait("scalar", f"{tag}_ln{t}_ab")
                if t >= 1:
                    wait("scalar", f"{tag}_xtt{t - 1}_{ND - 1}")
                op("scalar", lambda e, xoff=xoff: e.activation(
                    aB(0, [1, D]), aF(xoff, [1, D]),
                    AF.Identity, bias=small[:, 1:2], scale=small[:, 0:1]),
                   inc=("sS", 1), tag=f"{tag}_xln{t}")
                E.settag(f"{tag}_afree{t}", f"{tag}_xln{t}")
                wait("tensor", f"{tag}_xln{t}")
                for ft in range(ND):
                    op("tensor", lambda e, ft=ft, t=t: e.matmul(
                        pb[5][:, ft * 16 + t:ft * 16 + t + 1], aB(ft * 128, [1, 128]),
                        ones1[:, 0:1], start=True, stop=True),
                       inc=(("sT", 1) if (t == NT - 1 and ft == ND - 1) else None),
                       tag=(f"{tag}_csmm" if (t == NT - 1 and ft == ND - 1) else None))
                for ft in range(ND):
                    pdst = pt0 if ft % 2 == 0 else pt1
                    if ft >= 2:
                        wait("tensor", f"{tag}_xte{t}_{ft - 2}")
                    elif t >= 1:
                        wait("tensor", f"{tag}_xte{t - 1}_{ND - 2 + ft}")
                        if ft == 0:
                            wait("tensor", f"{tag}_kde{t - 1}")
                    op("tensor", lambda e, ft=ft, pdst=pdst: e.transpose(
                        pdst[:, 0:128], aB(ft * 128, [1, 128]),
                        ident[:, :]), inc=("sT", 1), tag=f"{tag}_xtt{t}_{ft}")
                    wait("vector", f"{tag}_xtt{t}_{ft}")
                    if ft == 0 and t >= 1:
                        wait("vector", f"{tag}_kmm{t - 1}")
                        wait("vector", f"{tag}_vmm{t - 1}")
                        if t - 1 < 4:
                            wait("vector", f"{tag}_qimm{t - 1}")
                    op("vector", lambda e, ft=ft, pdst=pdst: e.tensor_copy(
                        aB(2048 + ft * 128, [1, 128]), pdst[:, 0:128]),
                       inc=("sV", 1), tag=f"{tag}_xte{t}_{ft}")
                wait("tensor", f"{tag}_xte{t}_{ND - 1}")
                if t == 0:
                    wait("tensor", f"{tag}_wk")
                    wait("tensor", f"{tag}_wv")
                    wait("tensor", f"{tag}_wiq")
                else:
                    wait("tensor", f"{tag}_krope{t - 1}")
                    wait("tensor", f"{tag}_vev{t - 1}")
                for ft in range(ND):
                    op("tensor", lambda e, ft=ft: e.matmul(
                        pb[0][:, :], aB(2048 + ft * 128, [1, 128]),
                        ws(ft * 512, [1, 512]),
                        start=(ft == 0), stop=(ft == ND - 1)),
                       inc=(("sT", 1) if ft == ND - 1 else None),
                       tag=(f"{tag}_kmm{t}" if ft == ND - 1 else None))
                    op("tensor", lambda e, ft=ft: e.matmul(
                        pb[1][:, :], aB(2048 + ft * 128, [1, 128]),
                        ws(8192 + ft * 512, [1, 512]),
                        start=(ft == 0), stop=(ft == ND - 1)),
                       inc=(("sT", 1) if ft == ND - 1 else None),
                       tag=(f"{tag}_vmm{t}" if ft == ND - 1 else None))
                if t < 4:
                    if t >= 1:
                        wait("tensor", f"{tag}_qie{t - 1}")
                    for ft in range(ND):
                        op("tensor", lambda e, ft=ft: e.matmul(
                            pb[4][:, 0:128],
                            ap(ckv, ft * 128, [[4096, 128], [1, 128]]),
                            aB(2048 + ft * 128, [1, 128]),
                            start=(ft == 0), stop=(ft == ND - 1)),
                           inc=(("sT", 1) if ft == ND - 1 else None),
                           tag=(f"{tag}_qimm{t}" if ft == ND - 1 else None))
                    wait("vector", f"{tag}_qimm{t}")
                    op("vector", lambda e, t=t: e.tensor_copy(
                        ap(ckv, 3584 + t * 128, [[4096, 128], [1, 128]]),
                        pb[4][:, 0:128]),
                       inc=("sV", 1), tag=f"{tag}_qie{t}")
                wait("vector", f"{tag}_kmm{t}")
                wait("vector", f"{tag}_tabs{t}")
                wait("vector", f"{tag}_tab{t}")
                cosrep = ap(tabc, t * 128, [[2048, 128], [0, 4], [1, 128]])
                sinrep4 = ap(tabs, t * 128, [[2048, 128], [0, 4], [2, 64], [1, 2]])
                bar("vector", lambda e, cosrep=cosrep: e.tensor_tensor(
                    aF(4096, [128, 4], [1, 128]),
                    ap(pb[0], 0, [[512, 128], [128, 4], [1, 128]]),
                    cosrep, ALU.mult))
                ksw = ap(pb[0], 1, [[512, 128], [128, 4], [2, 64], [-1, 2]])
                bar("vector", lambda e, ksw=ksw, sinrep4=sinrep4: e.tensor_tensor(
                    aF(4608, [128, 4], [2, 64], [1, 2]), ksw, sinrep4, ALU.mult))
                op("vector", lambda e, t=t: e.tensor_tensor(
                    kv(t * 512, [1, 512]), aF(4096, [1, 512]),
                    aF(4608, [1, 512]), ALU.add),
                   inc=("sV", 1), tag=f"{tag}_krope{t}")
                wait("vector", f"{tag}_vmm{t}")
                op("vector", lambda e, t=t: e.tensor_copy(
                    kv(8192 + t * 512, [1, 512]), pb[1][:, :]),
                   inc=("sV", 1), tag=f"{tag}_vev{t}")
                wait("vector", f"{tag}_cwtile")
                cwa_rep = ap(arenaF, 5120, [[10240, 128], [0, 4], [1, 128]])
                cwb_rep = ap(arenaF, 5248, [[10240, 128], [0, 4], [1, 128]])
                kch3 = kv(t * 512, [128, 4], [1, 128])
                bar("vector", lambda e, kch3=kch3, cwa_rep=cwa_rep: e.tensor_tensor(
                    aF(4096, [128, 4], [1, 128]), kch3, cwa_rep, ALU.mult))
                bar("vector", lambda e: e.tensor_reduce(
                    aF(5376, [1, 4], [0, 1]), aF(4096, [128, 4], [1, 128]),
                    AX.X, ALU.add))
                bar("vector", lambda e, kch3=kch3, cwb_rep=cwb_rep: e.tensor_tensor(
                    aF(4096, [128, 4], [1, 128]), kch3, cwb_rep, ALU.mult))
                bar("vector", lambda e: e.tensor_reduce(
                    aF(5380, [1, 4], [0, 1]), aF(4096, [128, 4], [1, 128]),
                    AX.X, ALU.add))
                op("vector", lambda e: e.tensor_copy(
                    aB(4224, [1, 8]), aF(5376, [1, 8])), inc=("sV", 1),
                   tag=f"{tag}_kd{t}")
                wait("tensor", f"{tag}_kd{t}")
                if t >= 2:
                    wait("tensor", f"{tag}_kde{t - 2}")
                ptr = pt0 if t % 2 == 0 else pt1
                op("tensor", lambda e, ptr=ptr: e.transpose(
                    ptr[:, 0:128], aB(4224, [1, 128]), ident[:, :]),
                   inc=("sT", 1), tag=f"{tag}_kdt{t}")
                wait("vector", f"{tag}_kdt{t}")
                op("vector", lambda e, t=t, ptr=ptr: e.tensor_copy(
                    ap(arenaB, 4352 + t * 128, [[16384, 8], [1, 128]]),
                    ap(ptr, 0, [[1024, 8], [1, 128]])),
                   inc=("sV", 1), tag=f"{tag}_kde{t}")
            kdallB = ap(arenaB, 4352, [[16384, 8], [1, 2048]])
            kdF = ap(arenaF, 4096, [[10240, 8], [1, 2048]])
            kd3 = ap(arenaF, 4096, [[10240, 8], [16, 128], [1, 16]])
            mx3 = ap(arenaF, 6400, [[10240, 8], [1, 128], [0, 1]])
            mxb = ap(arenaF, 6400, [[10240, 8], [1, 128], [0, 16]])
            bar("vector", lambda e: e.tensor_copy(kdF, kdallB))
            bar("vector", lambda e: e.tensor_reduce(mx3, kd3, AX.X, ALU.max))
            bar("vector", lambda e: e.tensor_tensor(kd3, kd3, mxb, ALU.subtract))
            op("vector", lambda e: e.engine_nop(), inc=("sV", 1), tag=f"{tag}_kdsub")
            wait("scalar", f"{tag}_kdsub")
            op("scalar", lambda e: e.activation(kdF, kdF, AF.Exp),
               inc=("sS", 1), tag=f"{tag}_kdexp")
            wait("vector", f"{tag}_kdexp")
            bar("vector", lambda e: e.tensor_reduce(mx3, kd3, AX.X, ALU.add))
            bar("vector", lambda e: e.reciprocal(
                ap(arenaF, 6400, [[10240, 8], [1, 128]]),
                ap(arenaF, 6400, [[10240, 8], [1, 128]])))
            bar("vector", lambda e: e.tensor_tensor(kd3, kd3, mxb, ALU.mult))
            shm = [(i + 4) % 32 for i in range(32)]
            bar("vector", lambda e: e.stream_shuffle(
                ap(arenaF, 6656, [[10240, 32], [1, 2048]]),
                ap(arenaF, 4096, [[10240, 32], [1, 2048]]), shm))
            op("vector", lambda e: e.tensor_tensor(
                ap(arenaF, 4096, [[10240, 4], [1, 2048]]),
                ap(arenaF, 4096, [[10240, 4], [1, 2048]]),
                ap(arenaF, 6656, [[10240, 4], [1, 2048]]), ALU.add),
               inc=("sV", 1), tag=f"{tag}_wrow")
            for c in range(NT):
                bar("vector", lambda e, c=c: e.tensor_copy(
                    ap(arenaB, 6400, [[16384, 4], [1, 128]]),
                    ap(arenaF, 4096 + c * 128, [[10240, 4], [1, 128]])))
                op("vector", lambda e: e.engine_nop(), inc=("sV", 1),
                   tag=f"{tag}_wcb{c}")
                wait("tensor", f"{tag}_wcb{c}")
                if c >= 2:
                    wait("tensor", f"{tag}_wte{c - 2}")
                ptr = pt0 if c % 2 == 0 else pt1
                op("tensor", lambda e, ptr=ptr: e.transpose(
                    ptr[:, 0:128], ap(arenaB, 6400, [[16384, 128], [1, 128]]),
                    ident[:, :]), inc=("sT", 1), tag=f"{tag}_wtt{c}")
                wait("vector", f"{tag}_wtt{c}")
                op("vector", lambda e, c=c, ptr=ptr: e.tensor_copy(
                    ap(ckv, 3072 + c * 4, [[4096, 128], [1, 4]]),
                    ap(ptr, 0, [[1024, 128], [1, 4]])),
                   inc=("sV", 1), tag=f"{tag}_wte{c}")
            wait("vector", "indt")
            for c in range(NT):
                wrep = ap(ckv, 3072 + c * 4, [[4096, 128], [1, 4], [0, 128]])
                bar("vector", lambda e, c=c, wrep=wrep: e.tensor_tensor(
                    aB(6528, [128, 4], [1, 128]), kv(c * 512, [128, 4], [1, 128]),
                    wrep, ALU.mult))
                op("vector", lambda e, c=c, wrep=wrep: e.tensor_tensor(
                    aB(7040, [128, 4], [1, 128]),
                    kv(8192 + c * 512, [128, 4], [1, 128]), wrep, ALU.mult),
                   inc=("sV", 1), tag=f"{tag}_kvw{c}")
                wait("tensor", f"{tag}_kvw{c}")
                if c >= 1:
                    wait("tensor", f"{tag}_cke{c - 1}")
                op("tensor", lambda e: e.matmul(
                    pb[2][0:8, 0:512], indt[:, :], aB(6528, [1, 512]),
                    start=True, stop=True), inc=("sT", 1))
                op("tensor", lambda e: e.matmul(
                    pb[3][0:8, 0:512], indt[:, :], aB(7040, [1, 512]),
                    start=True, stop=True), inc=("sT", 1), tag=f"{tag}_ckmm{c}")
                wait("vector", f"{tag}_ckmm{c}")
                if c >= 2:
                    wait("vector", f"{tag}_ckdma{c - 2}")
                par = (c % 2) * 1024
                op("vector", lambda e, par=par: e.tensor_copy(
                    ap(arenaB, 8192 + par, [[16384, 8], [1, 512]]),
                    ap(pb[2], 0, [[512, 8], [1, 512]])), inc=("sV", 1))
                op("vector", lambda e, par=par: e.tensor_copy(
                    ap(arenaB, 8192 + par + 512, [[16384, 8], [1, 512]]),
                    ap(pb[3], 0, [[512, 8], [1, 512]])),
                   inc=("sV", 1), tag=f"{tag}_cke{c}")
                wait("gpsimd", f"{tag}_cke{c}")
                op("gpsimd", lambda e, c=c, par=par: e.dma_start(
                    out=ap(at_in, ATO_CK + c * 8 * 512, [[512, 8], [1, 512]]),
                    in_=ap(arenaB, 8192 + par, [[16384, 8], [1, 512]])),
                   inc=("sO", 16))
                op("gpsimd", lambda e, c=c, par=par: e.dma_start(
                    out=ap(at_in, ATO_CV + c * 8 * 512, [[512, 8], [1, 512]]),
                    in_=ap(arenaB, 8192 + par + 512, [[16384, 8], [1, 512]])),
                   inc=("sO", 16), tag=f"{tag}_ckdma{c}")
            E.settag(f"{tag}_ckst", f"{tag}_ckdma{NT - 1}")
            E.settag(f"{tag}_cvst", f"{tag}_ckdma{NT - 1}")
            wait("vector", f"{tag}_csmm")
            bar("vector", lambda e: e.tensor_reduce(
                aF(4096, [1, 16]),
                ap(pb[5], 0, [[512, 128], [16, 16], [1, 16]]), AX.X, ALU.add))
            op("vector", lambda e: e.tensor_copy(
                ap(ckv, 3136 + 0, [[4096, 128], [1, 16]]), aF(4096, [1, 16])),
               inc=("sV", 1), tag=f"{tag}_csev")
            wait("gpsimd", f"{tag}_csev")
            wait("gpsimd", f"{tag}_qie3")
            op("gpsimd", lambda e: e.dma_start(
                out=ap(at_in, ATO_QI, [[512, 128], [1, 512]]),
                in_=ap(ckv, 3584, [[4096, 128], [1, 512]])),
               inc=("sO", 16), tag=f"{tag}_qist")
            op("gpsimd", lambda e: e.dma_start(
                out=ap(at_in, ATO_CS, [[16, 128], [1, 16]]),
                in_=ap(ckv, 3136, [[4096, 128], [1, 16]])),
               inc=("sO", 16), tag=f"{tag}_csst")
            for tg in (f"{tag}_ckst", f"{tag}_cvst", f"{tag}_qist", f"{tag}_csst"):
                wait("gpsimd", tg)
            op("gpsimd", lambda e: e.collective_compute(
                "AllGather", ALU.bypass,
                replica_groups=[[0, 1, 2, 3], [4, 5, 6, 7]],
                ins=[at_in.ap().opt()], outs=[at_all.ap().opt()]),
               inc=("sC", 1), tag="ag_at")
            wait("gpsimd", "ag_at")
            for r in range(4):
                for h in range(4):
                    op("gpsimd", lambda e, r=r, h=h: e.dma_start(
                        out=ap(ckv, 2048 + h * 512 + r * 128,
                               [[4096, 128], [1, 128]]),
                        in_=ap(at_all, r * AT_N + ATO_CV + h * 128,
                               [[512, 128], [1, 128]])),
                       inc=("sO", 16), tag=f"{tag}_ucv{r}_{h}")
            for r in range(4):
                for h in range(4):
                    op("gpsimd", lambda e, r=r, h=h: e.dma_start(
                        out=kv((4 * r + h) * 128, [1, 128]),
                        in_=ap(at_all, r * AT_N + ATO_CK + h * 128,
                               [[512, 128], [1, 128]])),
                       inc=("sO", 16), tag=f"{tag}_uck{r}_{h}")
            op("gpsimd", lambda e: e.dma_start(
                out=kv(2048, [1, 512]),
                in_=ap(at_all, 0 * AT_N + ATO_QI, [[512, 128], [1, 512]])),
               inc=("sO", 16), tag=f"{tag}_uqi")
            for r in range(4):
                op("gpsimd", lambda e, r=r: e.dma_start(
                    out=kv(2560 + r * 16, [1, 16]),
                    in_=ap(at_all, r * AT_N + ATO_CS, [[16, 128], [1, 16]])),
                   inc=("sO", 16), tag=f"{tag}_ucs{r}")
            for r in range(4):
                wait("vector", f"{tag}_ucs{r}")
            bar("vector", lambda e: e.tensor_copy(
                aF(4096, [16, 4], [1, 16]), kv(2560, [16, 4], [1, 16])))
            bar("vector", lambda e: e.tensor_tensor(
                aF(4096, [1, 16]), aF(4096, [1, 16]), aF(4096 + 16, [1, 16]), ALU.add))
            bar("vector", lambda e: e.tensor_tensor(
                aF(4096 + 32, [1, 16]), aF(4096 + 32, [1, 16]),
                aF(4096 + 48, [1, 16]), ALU.add))
            bar("vector", lambda e: e.tensor_tensor(
                aF(4096, [1, 16]), aF(4096, [1, 16]), aF(4096 + 32, [1, 16]), ALU.add))
            op("vector", lambda e: e.tensor_copy(
                aB(6400, [1, 16]), aF(4096, [1, 16])), inc=("sV", 1),
               tag=f"{tag}_cst")
            op("gpsimd", lambda e: e.dma_start(
                out=kv(2816, [128, ND], [1, 128]),
                in_=ap(wik_h, 0, [[128, 128], [128 * 128, ND], [1, 128]])),
               inc=("sWA", 16), tag=f"{tag}_wik")
            wait("tensor", f"{tag}_cst")
            wait("tensor", f"{tag}_wik")
            wait("tensor", f"{tag}_uqi")
            for ft in range(ND):
                op("tensor", lambda e, ft=ft: e.matmul(
                    pb[4][0:1, 0:128], aB(6400 + ft, [1, 1]),
                    kv(2816 + ft * 128, [1, 128]),
                    start=(ft == 0), stop=(ft == ND - 1)),
                   inc=(("sT", 1) if ft == ND - 1 else None),
                   tag=(f"{tag}_kimm" if ft == ND - 1 else None))
            wait("vector", f"{tag}_kimm")
            op("vector", lambda e: e.tensor_copy(
                ap(arenaB, 6528, [[16384, 1], [1, 128]]),
                ap(pb[4], 0, [[512, 1], [1, 128]])), inc=("sV", 1),
               tag=f"{tag}_kime")
            wait("tensor", f"{tag}_kime")
            wait("tensor", f"{tag}_kde{NT - 1}")
            op("tensor", lambda e: e.transpose(
                pt0[:, 0:1], ap(arenaB, 6528, [[16384, 1], [1, 128]]),
                ident[0:1, 0:1]), inc=("sT", 1), tag=f"{tag}_kimt")
            wait("vector", f"{tag}_kimt")
            op("vector", lambda e: e.tensor_copy(
                ap(arenaB, 6656, [[16384, 128], [1, 1]]),
                ap(pt0, 0, [[1024, 128], [1, 1]])), inc=("sV", 1),
               tag=f"{tag}_kimc")
            wait("tensor", f"{tag}_kimc")
            op("tensor", lambda e: e.matmul(
                pb[4][0:1, 0:512], ap(arenaB, 6656, [[16384, 128], [1, 1]]),
                kv(2048, [1, 512]), start=True, stop=True),
               inc=("sT", 1), tag=f"{tag}_smm")
            wait("vector", f"{tag}_smm")
            op("vector", lambda e: e.tensor_copy(
                ap(arenaB, 6784, [[16384, 1], [1, 512]]),
                ap(pb[4], 0, [[512, 1], [1, 512]])), inc=("sV", 1),
               tag=f"{tag}_sev")
            wait("tensor", f"{tag}_sev")
            op("tensor", lambda e: e.matmul(
                pb[4][:, 0:512], ap(ones1, 0, [[1024, 1], [1, 128]]),
                ap(arenaB, 6784, [[16384, 1], [1, 512]]),
                start=True, stop=True), inc=("sT", 1), tag=f"{tag}_bmm")
            for c in range(4):
                op("tensor", lambda e, c=c: e.transpose(
                    pt1[:, 2 * c:2 * c + 1],
                    ap(arenaB, 6784 + c * 128, [[16384, 1], [1, 128]]),
                    ident[0:1, 0:1]), inc=(("sT", 1) if c == 3 else None),
                   tag=(f"{tag}_sct" if c == 3 else None))
            wait("vector", f"{tag}_bmm")
            op("vector", lambda e: e.tensor_copy(
                aF(4096, [1, 512]), pb[4][:, 0:512]), inc=("sV", 1))
            wait("vector", f"{tag}_sct")
            op("vector", lambda e: e.tensor_copy(
                aF(4608, [1, 4]), ap(pt1, 0, [[1024, 128], [2, 4]])),
               inc=("sV", 1), tag=f"{tag}_scev")
            wait("vector", f"{tag}_scev")
            for c in range(4):
                bar("vector", lambda e, c=c: e.tensor_scalar(
                    aF(5120, [1, 512]), aF(4096, [1, 512]),
                    ap(arenaF, 4608 + c, [[10240, 128], [1, 1]]), None, ALU.is_gt))
                bar("vector", lambda e, c=c: e.tensor_reduce(
                    ap(arenaF, 5632 + c, [[10240, 128], [1, 1]]),
                    aF(5120, [1, 512]), AX.X, ALU.add))
            bar("vector", lambda e: e.tensor_scalar(
                aF(5120, [1, 4]), aF(5632, [1, 4]), 64.0, 1.0,
                ALU.is_lt, ALU.subtract))
            op("vector", lambda e: e.tensor_scalar(
                aB(6912, [1, 4]), aF(5120, [1, 4]), 30000.0, None, ALU.mult),
               inc=("sV", 1), tag=f"{tag}_maskc")
            wait("tensor", f"{tag}_maskc")
            op("tensor", lambda e: e.transpose(
                pt0[0:4, 0:128], ap(arenaB, 6912, [[16384, 128], [1, 4]]),
                ident[:, :]), inc=("sT", 1), tag=f"{tag}_maskt")
            wait("vector", f"{tag}_maskt")
            op("vector", lambda e: e.tensor_copy(
                ap(arenaB, 7424, [[16384, 4], [1, 128]]),
                ap(pt0, 0, [[1024, 4], [1, 128]])), inc=("sV", 1),
               tag=f"{tag}_maske")
            wait("gpsimd", f"{tag}_maske")
            op("gpsimd", lambda e: e.dma_start(
                out=ap(bounce_b, 0, [[1, 512]]),
                in_=ap(arenaB, 7424, [[16384, 4], [1, 128]])),
               inc=("sO", 16), tag=f"{tag}_maskbo")
            wait("gpsimd", f"{tag}_maskbo")
            op("gpsimd", lambda e: e.dma_start(
                out=ap(ones1, 512, [[1024, 1], [1, 512]]),
                in_=ap(bounce_b, 0, [[1, 512]])),
               inc=("sO", 16), tag=f"{tag}_maskrow")
            for i in range(16):
                r, h = i // 4, i % 4
                wait("tensor", f"{tag}_uck{r}_{h}")
                ptr = pt0 if i % 2 == 0 else pt1
                if i >= 2:
                    wait("tensor", f"{tag}_ckfe{i - 2}")
                elif i == 0:
                    wait("tensor", f"{tag}_maske")
                else:
                    wait("tensor", f"{tag}_scev")
                op("tensor", lambda e, i=i, ptr=ptr: e.transpose(
                    ptr[:, 0:128], kv(i * 128, [1, 128]), ident[:, :]),
                   inc=("sT", 1), tag=f"{tag}_ckft{i}")
                wait("vector", f"{tag}_ckft{i}")
                op("vector", lambda e, i=i, r=r, h=h, ptr=ptr: e.tensor_copy(
                    ap(ckv, h * 512 + r * 128, [[4096, 128], [1, 128]]),
                    ptr[:, 0:128]), inc=("sV", 1), tag=f"{tag}_ckfe{i}")
            op("vector", lambda e: e.engine_nop(), inc=("sV", 1), tag=f"{tag}_adone")

        def bphase(E, tag):
            op, wait, bar = E.op, E.wait, E.bar
            xrdy = "f1_dst{t}" if PH["ffn1"] else "cp1_s{t}"
            for eng in ("scalar", "vector", "tensor", "sync", "gpsimd"):
                wait(eng, f"{tag}_adone")
                wait(eng, f"{tag}_maskrow")
                for r in range(4):
                    wait(eng, f"{tag}_ucv{r}_3")
            for g in range(4):
                for tc in range(4):
                    t = 4 * g + tc
                    xoff = tc * 2048
                    wait("scalar", xrdy.format(t=t))
                    if g >= 1:
                        wait("scalar", f"{tag}_xst{t - 4}")
                    op("scalar", lambda e, t=t, xoff=xoff: e.dma_start(
                        out=aF(xoff, [1, D]),
                        in_=ap(x_mid, t * 128 * D, [[D, 128], [1, D]])),
                       inc=("sXL", 16), tag=f"{tag}_bxl{t}")
                    wait("scalar", f"{tag}_bxl{t}")
                    if t >= 1:
                        wait("scalar", f"{tag}_bxtt{t - 1}")
                    op("scalar", lambda e, t=t, xoff=xoff: e.activation(
                        aB(8192, [1, D]), aF(xoff, [1, D]), AF.Identity,
                        bias=small[:, 32 + t:33 + t], scale=small[:, 16 + t:17 + t]),
                       inc=("sS", 1), tag=f"{tag}_bxln{t}")
                    wait("tensor", f"{tag}_bxln{t}")
                    for ft in range(ND):
                        pdst = pt0 if ft % 2 == 0 else pt1
                        if ft >= 2:
                            wait("tensor", f"{tag}_bxe{t}_{ft - 2}")
                        elif t >= 1:
                            wait("tensor", f"{tag}_bxe{t - 1}_{ND - 2 + ft}")
                        op("tensor", lambda e, ft=ft, pdst=pdst:
                           e.transpose(
                               pdst[:, 0:128], aB(8192 + ft * 128, [1, 128]),
                               ident[:, :]),
                           inc=("sT", 1),
                           tag=(f"{tag}_bxtt{t}" if ft == ND - 1 else f"{tag}_bxt{t}_{ft}"))
                        wait("vector", f"{tag}_bxtt{t}" if ft == ND - 1 else f"{tag}_bxt{t}_{ft}")
                        if ft == 0 and g >= 1:
                            wait("vector", f"{tag}_qmm{g - 1}_1_{tc}")
                        op("vector", lambda e, tc=tc, ft=ft, pdst=pdst:
                           e.tensor_copy(
                               aB(ft * 512 + tc * 128, [1, 128]),
                               pdst[:, 0:128]),
                           inc=("sV", 1), tag=f"{tag}_bxe{t}_{ft}")
                    E.settag(f"{tag}_bxe{t}", f"{tag}_bxe{t}_{ND - 1}")
                for qp in range(2):
                    for half in range(2):
                        qc = 2 * qp + half
                        woff = half * 8192
                        if qp == 1:
                            wait("gpsimd", f"{tag}_qmm{g}_0_3")
                            wait("gpsimd", f"{tag}_qmm{g}_0_3a")
                        elif g >= 1:
                            wait("gpsimd", f"{tag}_smm{g - 1}_{H - 1}_3")
                        op("gpsimd", lambda e, qc=qc, woff=woff: e.dma_start(
                            out=ws(woff, [512, ND], [1, 512]),
                            in_=ap(wq_h, qc * 512,
                                   [[D, 128], [D * 128, ND], [1, 512]])),
                           inc=("sWA", 16), tag=f"{tag}_wq{g}_{qc}")
                    for tc in range(4):
                        t = 4 * g + tc
                        pq0 = pb[0] if tc % 2 == 0 else pb[2]
                        pq1 = pb[1] if tc % 2 == 0 else pb[3]
                        if g >= 1 and qp == 0:
                            b0, b1 = (0, 1) if tc % 2 == 0 else (2, 3)
                            wait("tensor", f"{tag}_att{g - 1}_{H - 1}_{b0}")
                            wait("tensor", f"{tag}_att{g - 1}_{H - 1}_{b1}")
                        wait("tensor", f"{tag}_wq{g}_{2 * qp}")
                        wait("tensor", f"{tag}_wq{g}_{2 * qp + 1}")
                        wait("tensor", f"{tag}_bxe{t}")
                        if not (g == 0 and qp == 0 and tc < 2):
                            pass
                        idx = (g * 2 + qp) * 4 + tc
                        if idx >= 2:
                            pidx = idx - 2
                            wait("tensor", f"{tag}_rope{pidx}")
                        for ft in range(ND):
                            op("tensor", lambda e, tc=tc, ft=ft, pq0=pq0: e.matmul(
                                pq0[:, :], aB(ft * 512 + tc * 128, [1, 128]),
                                ws(0 * 8192 + ft * 512, [1, 512]),
                                start=(ft == 0), stop=(ft == ND - 1)),
                               inc=(("sT", 1) if ft == ND - 1 else None),
                               tag=(f"{tag}_qmm{g}_{qp}_{tc}a" if ft == ND - 1 else None))
                            op("tensor", lambda e, tc=tc, ft=ft, pq1=pq1: e.matmul(
                                pq1[:, :], aB(ft * 512 + tc * 128, [1, 128]),
                                ws(8192 + ft * 512, [1, 512]),
                                start=(ft == 0), stop=(ft == ND - 1)),
                               inc=(("sT", 1) if ft == ND - 1 else None),
                               tag=(f"{tag}_qmm{g}_{qp}_{tc}" if ft == ND - 1 else None))
                        wait("vector", f"{tag}_qmm{g}_{qp}_{tc}a")
                        wait("vector", f"{tag}_qmm{g}_{qp}_{tc}")
                        for half, pq in ((0, pq0), (1, pq1)):
                            cosrep = ap(tabc, t * 128, [[2048, 128], [0, 4], [1, 128]])
                            sinrep4 = ap(tabs, t * 128,
                                         [[2048, 128], [0, 4], [2, 64], [1, 2]])
                            bar("vector", lambda e, pq=pq, cosrep=cosrep: e.tensor_tensor(
                                aF(8192, [128, 4], [1, 128]),
                                ap(pq, 0, [[512, 128], [128, 4], [1, 128]]),
                                cosrep, ALU.mult))
                            qsw = ap(pq, 1, [[512, 128], [128, 4], [2, 64], [-1, 2]])
                            bar("vector", lambda e, qsw=qsw, sinrep4=sinrep4: e.tensor_tensor(
                                aF(8704, [128, 4], [2, 64], [1, 2]), qsw, sinrep4,
                                ALU.mult))
                            tg = f"{tag}_rope{idx}" if half == 1 else None
                            op("vector", lambda e, tc=tc, qp=qp, half=half: e.tensor_tensor(
                                aB(8192 + tc * 2048 + qp * 1024 + half * 512, [1, 512]),
                                aF(8192, [1, 512]), aF(8704, [1, 512]), ALU.add),
                               inc=("sV", 1), tag=(tg if tg else f"{tag}_ropeh{idx}_{half}"))
                for tc in range(4):
                    t = 4 * g + tc
                    idx3 = (g * 2 + 1) * 4 + tc
                    wait("tensor", f"{tag}_rope{idx3}")
                    for ft in range(ND):
                        pdst = pt0 if ft % 2 == 0 else pt1
                        if ft >= 2:
                            wait("tensor", f"{tag}_qte{t}_{ft - 2}")
                        elif tc >= 1:
                            wait("tensor", f"{tag}_qte{t - 1}_{ND - 2 + ft}")
                        op("tensor", lambda e, tc=tc, ft=ft, pdst=pdst:
                           e.transpose(
                               pdst[:, 0:128],
                               aB(8192 + tc * 2048 + ft * 128, [1, 128]),
                               ident[:, :]), inc=("sT", 1),
                           tag=f"{tag}_qtt{t}_{ft}")
                        wait("vector", f"{tag}_qtt{t}_{ft}")
                        if ft == 0 and tc == 0 and g >= 1:
                            wait("vector", f"{tag}_wo_done{g - 1}")
                        op("vector", lambda e, tc=tc, ft=ft, pdst=pdst:
                           e.tensor_copy(
                               ws(ft * 512 + tc * 128, [1, 128]),
                               pdst[:, 0:128]),
                           inc=("sV", 1), tag=f"{tag}_qte{t}_{ft}")
                for tc in range(4):
                    wait("tensor", f"{tag}_qte{4 * g + tc}_{ND - 1}")
                for h in range(H):
                    hk = h // 4
                    for tc in range(4):
                        psc = pb[tc]
                        if h >= 1:
                            wait("tensor", f"{tag}_att{g}_{h - 1}_{tc}")
                        elif g >= 1:
                            wait("tensor", f"{tag}_att{g - 1}_{H - 1}_{tc}")
                        op("tensor", lambda e, h=h, tc=tc, psc=psc, hk=hk: e.matmul(
                            psc[:, :], ws(h * 512 + tc * 128, [1, 128]),
                            ap(ckv, hk * 512, [[4096, 128], [1, 512]]),
                            start=True, stop=False))
                        op("tensor", lambda e, psc=psc: e.matmul(
                            psc[:, :], ap(ones1, 0, [[1024, 1], [1, 128]]),
                            ap(ones1, 512, [[1024, 1], [1, 512]]),
                            start=False, stop=True),
                           inc=("sT", 1), tag=f"{tag}_smm{g}_{h}_{tc}")
                        wait("vector", f"{tag}_smm{g}_{h}_{tc}")
                        idx2 = (g * H + h) * 4 + tc
                        p2 = idx2 % 2
                        sm0 = 10 + p2 * 2
                        if idx2 >= 2:
                            pp = idx2 - 2
                            wait("vector", "%s_exp%d_%d_%d" % (
                                tag, pp // (4 * H), (pp // 4) % H, pp % 4))
                        bar("vector", lambda e, psc=psc, sm0=sm0: e.tensor_reduce(
                            small[:, sm0:sm0 + 1], ap(psc, 0, [[512, 128], [1, 512]]),
                            AX.X, ALU.max))
                        bar("vector", lambda e, sm0=sm0: e.tensor_scalar(
                            small[:, sm0:sm0 + 1], small[:, sm0:sm0 + 1],
                            -ISC, None, ALU.mult))
                        op("vector", lambda e: e.engine_nop(), inc=("sV", 1),
                           tag=f"{tag}_negm{g}_{h}_{tc}")
                        wait("scalar", f"{tag}_negm{g}_{h}_{tc}")
                        if idx2 >= 2:
                            pp = idx2 - 2
                            wait("scalar", "%s_att%d_%d_%d" % (
                                tag, pp // (4 * H), (pp // 4) % H, pp % 4))
                        op("scalar", lambda e, psc=psc, p2=p2, sm0=sm0: e.activation(
                            aF(8192 + p2 * 512, [1, 512]), psc[:, :], AF.Exp,
                            bias=small[:, sm0:sm0 + 1], scale=ISC),
                           inc=("sS", 1), tag=f"{tag}_exp{g}_{h}_{tc}")
                        wait("vector", f"{tag}_exp{g}_{h}_{tc}")
                        bar("vector", lambda e, p2=p2, sm0=sm0: e.tensor_reduce(
                            small[:, sm0 + 1:sm0 + 2], aF(8192 + p2 * 512, [1, 512]),
                            AX.X, ALU.add))
                        bar("vector", lambda e, sm0=sm0: e.reciprocal(
                            small[:, sm0 + 1:sm0 + 2], small[:, sm0 + 1:sm0 + 2]))
                        op("vector", lambda e, tc=tc, p2=p2, sm0=sm0: e.tensor_scalar(
                            aB(8192 + tc * 512, [1, 512]), aF(8192 + p2 * 512, [1, 512]),
                            small[:, sm0 + 1:sm0 + 2], None, ALU.mult),
                           inc=("sV", 1), tag=f"{tag}_att{g}_{h}_{tc}")
                    for tc in range(4):
                        wait("tensor", f"{tag}_att{g}_{h}_{tc}")
                        ptr = pt0 if tc % 2 == 0 else pt1
                        if tc >= 2:
                            wait("tensor", f"{tag}_atte{g}_{h}_{tc - 2}")
                        elif h >= 1:
                            wait("tensor", f"{tag}_atte{g}_{h - 1}_{tc + 2}")
                        for blt in range(4):
                            op("tensor", lambda e, tc=tc, blt=blt, ptr=ptr: e.transpose(
                                ptr[:, blt * 128:(blt + 1) * 128],
                                aB(8192 + tc * 512 + blt * 128, [1, 128]),
                                ident[:, :]),
                               inc=(("sT", 1) if blt == 3 else None),
                               tag=(f"{tag}_attt{g}_{h}_{tc}" if blt == 3 else None))
                        wait("vector", f"{tag}_attt{g}_{h}_{tc}")
                        op("vector", lambda e, tc=tc, ptr=ptr: e.tensor_copy(
                            ap(arenaB, 10240 + tc * 128,
                               [[16384, 128], [512, 4], [1, 128]]),
                            ap(ptr, 0, [[1024, 128], [128, 4], [1, 128]])),
                           inc=("sV", 1), tag=f"{tag}_atte{g}_{h}_{tc}")
                    for tc in range(4):
                        wait("tensor", f"{tag}_atte{g}_{h}_{tc}")
                    if h >= 1:
                        wait("tensor", f"{tag}_oev{g}_{h - 1}")
                    for blt in range(4):
                        op("tensor", lambda e, blt=blt, hk=hk: e.matmul(
                            pb[4][:, :],
                            ap(ckv, 2048 + hk * 512 + blt * 128,
                               [[4096, 128], [1, 128]]),
                            ap(arenaB, 10240 + blt * 512, [[16384, 128], [1, 512]]),
                            start=(blt == 0), stop=(blt == 3)),
                           inc=(("sT", 1) if blt == 3 else None),
                           tag=(f"{tag}_avmm{g}_{h}" if blt == 3 else None))
                    wait("vector", f"{tag}_avmm{g}_{h}")
                    op("vector", lambda e, h=h: e.tensor_copy(
                        kv(8192 + h * 512, [1, 512]), pb[4][:, :]),
                       inc=("sV", 1), tag=f"{tag}_oev{g}_{h}")
                for op2 in range(2):
                    for half in range(2):
                        oc = 2 * op2 + half
                        woff = half * 8192
                        wait("gpsimd", f"{tag}_smm{g}_{H - 1}_3")
                        if op2 == 1:
                            wait("gpsimd", f"{tag}_womm{g}_0_3")
                            wait("gpsimd", f"{tag}_womm{g}_0_3a")
                        op("gpsimd", lambda e, oc=oc, woff=woff: e.dma_start(
                            out=ws(woff, [512, ND], [1, 512]),
                            in_=ap(wo_h, oc * 512,
                                   [[D, 128], [D * 128, ND], [1, 512]])),
                           inc=("sWA", 16), tag=f"{tag}_wo{g}_{oc}")
                    for tc in range(4):
                        t = 4 * g + tc
                        po0 = pb[0] if tc % 2 == 0 else pb[2]
                        po1 = pb[1] if tc % 2 == 0 else pb[3]
                        wait("tensor", f"{tag}_wo{g}_{2 * op2}")
                        wait("tensor", f"{tag}_wo{g}_{2 * op2 + 1}")
                        wait("tensor", f"{tag}_oev{g}_{H - 1}")
                        idx = op2 * 4 + tc
                        if idx >= 2:
                            pidx = idx - 2
                            wait("tensor", f"{tag}_woev{g}_{pidx // 4}_{pidx % 4}")
                        for ft in range(ND):
                            op("tensor", lambda e, tc=tc, ft=ft, po0=po0: e.matmul(
                                po0[:, :], kv(8192 + ft * 512 + tc * 128, [1, 128]),
                                ws(0 + ft * 512, [1, 512]),
                                start=(ft == 0), stop=(ft == ND - 1)),
                               inc=(("sT", 1) if ft == ND - 1 else None),
                               tag=(f"{tag}_womm{g}_{op2}_{tc}a" if ft == ND - 1 else None))
                            op("tensor", lambda e, tc=tc, ft=ft, po1=po1: e.matmul(
                                po1[:, :], kv(8192 + ft * 512 + tc * 128, [1, 128]),
                                ws(8192 + ft * 512, [1, 512]),
                                start=(ft == 0), stop=(ft == ND - 1)),
                               inc=(("sT", 1) if ft == ND - 1 else None),
                               tag=(f"{tag}_womm{g}_{op2}_{tc}" if ft == ND - 1 else None))
                        wait("vector", f"{tag}_womm{g}_{op2}_{tc}a")
                        wait("vector", f"{tag}_womm{g}_{op2}_{tc}")
                        xoff = tc * 2048
                        for half, po in ((0, po0), (1, po1)):
                            oc = 2 * op2 + half
                            op("vector", lambda e, po=po, xoff=xoff, oc=oc:
                               e.scalar_tensor_tensor(
                                   aF(xoff + oc * 512, [1, 512]), po[:, :], 1.0,
                                   aF(xoff + oc * 512, [1, 512]), ALU.mult, ALU.add),
                               inc=("sV", 1),
                               tag=f"{tag}_woev{g}_{op2}_{tc}" if half == 1 else None)
                for tc in range(4):
                    t = 4 * g + tc
                    wait("gpsimd", f"{tag}_woev{g}_1_{tc}")
                    op("gpsimd", lambda e, t=t, tc=tc: e.dma_start(
                        out=ap(x_att, t * 128 * D, [[D, 128], [1, D]]),
                        in_=aF(tc * 2048, [1, D])),
                       inc=("sO", 16), tag=f"{tag}_xst{t}")
                op("vector", lambda e: e.engine_nop(), inc=("sV", 1),
                   tag=f"{tag}_wo_done{g}")
            op("vector", lambda e: e.engine_nop(), inc=("sV", 1), tag="at_done")

        def dbg_readback(E):
            op, wait = E.op, E.wait
            end_tag = "f2_dst15" if PH["ffn2"] else "cp3_s15"
            wait("sync", end_tag)
            for ci, (srch, cbase) in enumerate(
                    ((qg_in, 0), (qu_in, 16), (qd_in, 32))):
                for c in range(16):
                    gc = cbase + c
                    par = gc % 2
                    if gc >= 2:
                        wait("sync", f"dbgv{gc - 2}")
                    op("sync", lambda e, c=c, par=par, srch=srch: e.dma_start(
                        out=a8(par * 1024, [1, 1024]),
                        in_=ap(srch, c * 131072, [[1024, 128], [1, 1024]])),
                       inc=("sWL", 16), tag=f"dbgl{gc}")
                    wait("vector", f"dbgl{gc}")
                    if gc >= 2:
                        wait("vector", f"dbgs{gc - 2}")
                    op("vector", lambda e, par=par: e.tensor_copy(
                        aB(par * 1024, [1, 1024]), a8(par * 1024, [1, 1024])),
                       inc=("sV", 1), tag=f"dbgv{gc}")
                    wait("gpsimd", f"dbgv{gc}")
                    op("gpsimd", lambda e, gc=gc, par=par: e.dma_start(
                        out=ap(qdbg, gc * 131072, [[1024, 128], [1, 1024]]),
                        in_=aB(par * 1024, [1, 1024])),
                       inc=("sO", 16), tag=f"dbgs{gc}")
            wait("vector", "dbgs47")
            op("vector", lambda e: e.engine_nop(), inc=("sV", 1), tag="dbg_done")

        env.start("dry")
        program(env)
        with nc.Block() as blk:
            for engname, dec in (("sync", blk.sync), ("vector", blk.vector),
                                 ("scalar", blk.scalar), ("tensor", blk.tensor),
                                 ("gpsimd", blk.gpsimd)):
                def mk(name):
                    def f(e):
                        env.start("emit", name, e)
                        program(env)
                    return f
                dec(mk(engname))

    return nc


_NC_CACHE = {}


def kernel(L0_attn_x, L0_attn_wq, L0_attn_wk, L0_attn_wv, L0_attn_wo,
           L0_attn_wiq, L0_attn_wik, L0_attn_cwa, L0_attn_cwb,
           L0_ffn_w_gate, L0_ffn_w_up, L0_ffn_w_down):
    x = np.ascontiguousarray(np.asarray(L0_attn_x, dtype=np.float32)
                             ).reshape(2 * 8192, D)
    wq = np.ascontiguousarray(np.asarray(L0_attn_wq, dtype=np.float32))
    wk = np.ascontiguousarray(np.asarray(L0_attn_wk, dtype=np.float32))
    wv = np.ascontiguousarray(np.asarray(L0_attn_wv, dtype=np.float32))
    wo = np.ascontiguousarray(np.asarray(L0_attn_wo, dtype=np.float32))
    wiq = np.ascontiguousarray(np.asarray(L0_attn_wiq, dtype=np.float32))
    wik = np.ascontiguousarray(np.asarray(L0_attn_wik, dtype=np.float32))
    cw = np.ascontiguousarray(
        np.stack([np.asarray(L0_attn_cwa, dtype=np.float32),
                  np.asarray(L0_attn_cwb, dtype=np.float32)], axis=0))
    wg = np.ascontiguousarray(np.asarray(L0_ffn_w_gate, dtype=np.float32))
    wu = np.ascontiguousarray(np.asarray(L0_ffn_w_up, dtype=np.float32))
    wd = np.ascontiguousarray(np.asarray(L0_ffn_w_down, dtype=np.float32))

    if "nc" not in _NC_CACHE:
        _NC_CACHE["nc"] = build_program()
    nc = _NC_CACHE["nc"]

    in_maps = []
    for c in range(8):
        slot = c % 4
        in_maps.append({
            "x": np.ascontiguousarray(x[c * TC:(c + 1) * TC]),
            "wq": wq, "wk": wk, "wv": wv, "wo": wo,
            "wiq": wiq, "wik": wik, "cw": cw,
            "wg_shard": np.ascontiguousarray(wg[c * (II // 8):(c + 1) * (II // 8)]),
            "wu_shard": np.ascontiguousarray(wu[c * (II // 8):(c + 1) * (II // 8)]),
            "wd_shard": np.ascontiguousarray(wd[:, c * (II // 8):(c + 1) * (II // 8)]),
            "slot": np.array([[1.0 if slot == 0 else 0.0, float(slot), 0.0, 0.0]],
                             dtype=np.float32),
        })
    res = run_bass_kernel_spmd(nc, in_maps, core_ids=list(range(8)))
    _LAST_RES[0] = res
    out = np.concatenate([np.asarray(r["out"]) for r in res.results], axis=0)
    return out.reshape(2, 8192, D)


# version 09 (final)
# speedup vs baseline: 1.0306x; 1.0217x over previous
import numpy as np
import concourse.bass as bass
import concourse.mybir as mybir
from concourse.bass_utils import run_bass_kernel_spmd

F32 = mybir.dt.float32
BF16 = mybir.dt.bfloat16
F8 = mybir.dt.float8e4
I32 = mybir.dt.int32
ALU = mybir.AluOpType
AX = mybir.AxisListType
AF = mybir.ActivationFunctionType

D = 2048
II = 8192
TC = 2048
NT = 16
ND = 16
NI512 = 16
NIT = 64
H, HKV, HD = 16, 4, 128
NBLK = 512
THETA = 10000.0
EPS = 1e-5
ISC = float(1.0 / np.sqrt(128.0))

C250 = 250 << 23
CGA_X = 4 << 23
CGA_I = -(2 << 23)
CLIP_A = 0.119140625
C255 = 255 << 23
CW_BACK = 5 << 23
WCLIP = 6.9921875 / 2.0

PH = {"ffn1": True, "attn": True, "ffn2": True}
DBG = {"dump": False}
_LAST_RES = [None]

SEMS = ["sWL", "sWA", "sXL", "sPL", "sV", "sS", "sT", "sG", "sO", "sC", "sB", "sQ"]


class Env:
    def __init__(self):
        self.handles = None
        self.tags = {}
        self.mode = "dry"
        self.engine = None
        self.eng = None
        self.counts = {}
        self.barn = 0

    def start(self, mode, engine=None, eng=None):
        self.mode = mode
        self.engine = engine
        self.eng = eng
        self.counts = {s: 0 for s in SEMS}
        self.barn = 0

    def op(self, engine, fn, inc=None, tag=None):
        inst = None
        if self.mode == "emit" and engine == self.engine:
            inst = fn(self.eng)
        if inc is not None:
            sem, k = inc
            self.counts[sem] += k
            if tag is not None:
                self.tags[tag] = (sem, self.counts[sem])
            if inst is not None:
                inst.then_inc(self.handles[sem], k)
        return inst

    def wait(self, engine, tag):
        if tag not in self.tags:
            raise KeyError(f"wait on unset tag {tag!r}")
        sem, v = self.tags[tag]
        if v <= 0:
            return
        if self.mode == "emit" and engine == self.engine:
            self.eng.wait_ge(self.handles[sem], v)

    def bar(self, engine, fn):
        self.barn += 1
        t = ("bar", self.barn)
        self.op(engine, fn, inc=("sB", 1), tag=t)
        self.wait(engine, t)

    def settag(self, tag, other):
        self.tags[tag] = self.tags[other]

    def zerotag(self, tag):
        self.tags[tag] = ("sB", 0)


def build_program():
    nc = bass.Bass("TRN2", target_bir_lowering=False, debug=False,
                   detect_race_conditions=False)

    x_in = nc.dram_tensor("x", [TC, D], F32, kind="ExternalInput")
    wq_h = nc.dram_tensor("wq", [D, D], F32, kind="ExternalInput")
    wk_h = nc.dram_tensor("wk", [D, 512], F32, kind="ExternalInput")
    wv_h = nc.dram_tensor("wv", [D, 512], F32, kind="ExternalInput")
    wo_h = nc.dram_tensor("wo", [D, D], F32, kind="ExternalInput")
    wiq_h = nc.dram_tensor("wiq", [D, HD], F32, kind="ExternalInput")
    wik_h = nc.dram_tensor("wik", [D, HD], F32, kind="ExternalInput")
    cw_h = nc.dram_tensor("cw", [2, HD], F32, kind="ExternalInput")
    wg_h = nc.dram_tensor("wg_shard", [II // 8, D], F32, kind="ExternalInput")
    wu_h = nc.dram_tensor("wu_shard", [II // 8, D], F32, kind="ExternalInput")
    wd_h = nc.dram_tensor("wd_shard", [D, II // 8], F32, kind="ExternalInput")
    slot_h = nc.dram_tensor("slot", [1, 4], F32, kind="ExternalInput")
    out_h = nc.dram_tensor("out", [TC, D], F32, kind="ExternalOutput")

    _ko = {"kind": "ExternalOutput"} if DBG["dump"] else {}
    x_mid = nc.dram_tensor("x_mid", [TC, D], F32, **_ko)
    x_att = nc.dram_tensor("x_att", [TC, D], F32, **_ko)
    interq_s = nc.dram_tensor("interq_s", [NT, II, 128], BF16, **_ko)
    qg_in = nc.dram_tensor("qg_in", [2, ND, 128, 512], F8)
    qu_in = nc.dram_tensor("qu_in", [2, ND, 128, 512], F8)
    qd_in = nc.dram_tensor("qd_in", [8, 128, D], F8)
    qg_all = nc.dram_tensor("qg_all", [NI512, ND, 128, 512], F8, addr_space="Shared")
    qu_all = nc.dram_tensor("qu_all", [NI512, ND, 128, 512], F8, addr_space="Shared")
    qd_all = nc.dram_tensor("qd_all", [NIT, 128, D], F8, addr_space="Shared")
    ATO_CK = 0
    ATO_CV = 128 * 512
    ATO_QI = 2 * 128 * 512
    ATO_CS = 3 * 128 * 512
    AT_N = ATO_CS + D
    at_in = nc.dram_tensor("at_in", [AT_N], BF16)
    qdbg = (nc.dram_tensor("qdbg", [48 * 128, 1024], BF16, kind="ExternalOutput")
            if DBG.get("dump_w") else None)
    bounce_f = nc.dram_tensor("bounce_f", [1024], F32)
    bounce_b = nc.dram_tensor("bounce_b", [1024], BF16)
    at_all = nc.dram_tensor("at_all", [4, AT_N], BF16)

    env = Env()

    from contextlib import ExitStack
    with ExitStack() as st:
        ec = st.enter_context
        env.handles = {s: ec(nc.semaphore(s)) for s in SEMS}

        arenaB = ec(nc.sbuf_tensor("arenaB", [128, 16384], BF16))
        arenaF = ec(nc.sbuf_tensor("arenaF", [128, 10240], F32))
        stat = ec(nc.sbuf_tensor("stat", [128, 256], F32))
        statI = ec(nc.sbuf_tensor("statI", [128, 128], I32))
        small = ec(nc.sbuf_tensor("small", [128, 512], F32))
        ident = ec(nc.sbuf_tensor("ident", [128, 128], BF16))
        ones1 = ec(nc.sbuf_tensor("ones1", [128, 1024], BF16))
        indt = ec(nc.sbuf_tensor("indt", [128, 8], BF16))
        with nc.sbuf_tensor("qxT", [128, 32768], F8) as qxT, \
             nc.sbuf_tensor("arena8", [128, 40960], F8) as arena8:
            pass
        with nc.sbuf_tensor("qwd_res", [128, 131072], F8) as qwd_res:
            pass
        with nc.sbuf_tensor("ckv", [128, 4096], BF16) as ckv, \
             nc.sbuf_tensor("tabc", [128, 2048], F32) as tabc, \
             nc.sbuf_tensor("tabs", [128, 2048], F32) as tabs, \
             nc.sbuf_tensor("kvbuf", [128, 16384], BF16) as kvbuf, \
             nc.sbuf_tensor("wstream", [128, 16384], BF16) as wstream:
            pass
        pb = [ec(nc.psum_tensor(f"pb{i}", [128, 512], F32)) for i in range(6)]
        pt0 = ec(nc.psum_tensor("pt0", [128, 1024], BF16))
        pt1 = ec(nc.psum_tensor("pt1", [128, 1024], BF16))
        DBG["h"] = dict(arenaB=arenaB, arenaF=arenaF, stat=stat, statI=statI,
                        small=small, ident=ident, ones1=ones1, indt=indt,
                        qxT=qxT, arena8=arena8, qwd_res=qwd_res, ckv=ckv,
                        tabc=tabc, tabs=tabs, kvbuf=kvbuf, wstream=wstream)

        def ap(h, off, pattern):
            return bass.AP(h, off, pattern)

        def a8(off, *dims):
            return ap(arena8, off, [[40960, 128]] + list(dims))

        def aB(off, *dims):
            return ap(arenaB, off, [[16384, 128]] + list(dims))

        def aF(off, *dims):
            return ap(arenaF, off, [[10240, 128]] + list(dims))

        def kv(off, *dims):
            return ap(kvbuf, off, [[16384, 128]] + list(dims))

        def ws(off, *dims):
            return ap(wstream, off, [[16384, 128]] + list(dims))

        def program(E):
            op, wait, bar = E.op, E.wait, E.bar

            op("gpsimd", lambda e: e.memset(ident[:, :], 1.0))
            op("gpsimd", lambda e: e.affine_select(
                ident[:, :], ident[:, :], [[1, 128]], ALU.is_equal, 0.0,
                base=0, channel_multiplier=-1), inc=("sG", 1), tag="ident")
            op("gpsimd", lambda e: e.memset(ones1[:, :], 1.0), inc=("sG", 1),
               tag="ones")
            op("gpsimd", lambda e: e.iota(
                statI[:, 0:8], [[-16, 8]], channel_multiplier=1,
                allow_small_or_imprecise_dtypes=True),
               inc=("sG", 1), tag="indi")
            wait("vector", "indi")
            bar("vector", lambda e: e.tensor_copy(stat[:, 192:200], statI[:, 0:8]))
            bar("vector", lambda e: e.tensor_scalar(
                stat[:, 208:216], stat[:, 192:200], 0.0, 0.5, ALU.is_ge, ALU.mult))
            bar("vector", lambda e: e.scalar_tensor_tensor(
                stat[:, 192:200], stat[:, 192:200], 16.0, stat[:, 208:216],
                ALU.is_lt, ALU.mult))
            op("vector", lambda e: e.tensor_copy(indt[:, :], stat[:, 192:200]),
               inc=("sV", 1), tag="indt")

            phase_w(E)

            if PH["ffn1"]:
                gphase(E, "f1", x_in, None)
                dphase(E, "f1", x_in, x_mid, None, "ag_d", "f1")
            else:
                copy_dram(E, "cp1", x_in, x_mid, None)
            if PH["attn"]:
                aphase(E, "at")
                if not PH.get("stop_after_a"):
                    bphase(E, "at")
            else:
                copy_dram(E, "cp2", x_mid, x_att,
                          ("f1_dst{t}" if PH["ffn1"] else "cp1_s{t}"),
                          entry=("f1_dev15_3" if PH["ffn1"] else None))
            if PH.get("stop_after_a"):
                v = E.counts["sO"]
                if E.mode == "emit" and E.engine == "sync":
                    E.eng.wait_ge(E.handles["sO"], v)
                return
            if PH["ffn2"]:
                rdy = "at_xst{t}" if PH["attn"] else "cp2_s{t}"
                gphase(E, "f2", x_att, rdy)
                dphase(E, "f2", x_att, out_h, rdy, None, "f2")
            else:
                rdy = "at_xst{t}" if PH["attn"] else "cp2_s{t}"
                copy_dram(E, "cp3", x_att, out_h, rdy,
                          entry=("at_woev3_1_3" if PH["attn"] else "cp2_s15"))
            if DBG.get("dump_w"):
                dbg_readback(E)

            v = E.counts["sO"]
            if E.mode == "emit" and E.engine == "sync":
                E.eng.wait_ge(E.handles["sO"], v)

        def wq_chain(E, key, ncols, prev_tr=None):
            op, wait, bar = E.op, E.wait, E.bar
            nb = ncols // 32
            amax = stat[:, 0:nb]
            mt = statI[:, 0:nb]
            fasc = stat[:, 64:64 + nb]
            gasc = stat[:, 128:128 + nb]
            xs3 = aF(0, [32, nb], [1, 32])
            bar("vector", lambda e: e.tensor_reduce(
                ap(stat, 0, [[256, 128], [1, nb], [0, 1]]), xs3, AX.X,
                ALU.max, apply_absolute_value=True))
            bar("vector", lambda e: e.tensor_scalar(
                mt, amax.bitcast(I32), 0x7F800000, None, ALU.bitwise_and))
            bar("vector", lambda e: e.tensor_scalar(
                fasc.bitcast(I32), mt, -1, C255, ALU.mult, ALU.add))
            bar("vector", lambda e: e.tensor_scalar(
                gasc.bitcast(I32), mt, CW_BACK, None, ALU.add))
            fab = ap(stat, 64, [[256, 128], [1, nb], [0, 32]])
            gab = ap(stat, 128, [[256, 128], [1, nb], [0, 32]])
            t0f = aF(2048, [1, ncols])
            t03 = aF(2048, [32, nb], [1, 32])
            bar("vector", lambda e: e.tensor_tensor(t03, xs3, fab, ALU.mult))
            bar("vector", lambda e: e.tensor_scalar(
                t0f, t0f, WCLIP, -WCLIP, ALU.min, ALU.max))
            op("vector", lambda e: e.engine_nop(), inc=("sV", 1), tag=f"{key}_t0")
            wait("scalar", f"{key}_t0")
            aw = aF(4096, [1, ncols])
            op("scalar", lambda e: e.activation(aw, t0f, AF.Abs),
               inc=("sS", 1), tag=f"{key}_aw")
            wait("vector", f"{key}_aw")
            ge1 = aF(6144, [1, ncols])
            ge2 = aF(8192, [1, ncols])
            s2e = aF(4096, [1, ncols])
            bar("vector", lambda e: e.tensor_scalar(ge1, aw, 1.0, None, ALU.is_ge))
            bar("vector", lambda e: e.tensor_scalar(ge2, aw, 2.0, None, ALU.is_ge))
            bar("vector", lambda e: e.tensor_scalar(s2e, ge1, -0.5, 1.0, ALU.mult, ALU.add))
            bar("vector", lambda e: e.scalar_tensor_tensor(s2e, ge2, -0.25, s2e, ALU.mult, ALU.add))
            bar("vector", lambda e: e.tensor_scalar(s2e, s2e, 2.0 ** -7, None, ALU.mult))
            t8 = a8(0, [1, ncols])
            bar("vector", lambda e: e.tensor_tensor(t8, t0f, s2e, ALU.mult))
            bar("vector", lambda e: e.tensor_scalar(ge1, ge1, 1.0, 1.0, ALU.mult, ALU.add))
            bar("vector", lambda e: e.scalar_tensor_tensor(ge1, ge2, 2.0, ge1, ALU.mult, ALU.add))
            bar("vector", lambda e: e.tensor_scalar(ge1, ge1, 2.0 ** 7, None, ALU.mult))
            bar("vector", lambda e: e.tensor_tensor(t0f, t8, ge1, ALU.mult))
            qwb3 = aB(0, [32, nb], [1, 32])
            if prev_tr is not None:
                wait("vector", prev_tr)
            op("vector", lambda e: e.tensor_tensor(qwb3, t03, gab, ALU.mult),
               inc=("sV", 1), tag=f"{key}_qwb")

        def wq_transposes(E, key, ntiles, prev_store):
            op, wait = E.op, E.wait
            wait("tensor", f"{key}_qwb")
            for ft in range(ntiles):
                pdst = pt0 if ft % 2 == 0 else pt1
                if ft >= 2:
                    wait("tensor", f"{key}_ev{ft - 2}")
                op("tensor", lambda e, ft=ft, pdst=pdst: e.transpose(
                    pdst[:, 0:128], aB(ft * 128, [1, 128]), ident[:, :]),
                   inc=("sT", 1), tag=f"{key}_tr{ft}")
                wait("vector", f"{key}_tr{ft}")
                if ft == 0 and prev_store is not None:
                    wait("vector", prev_store)
                op("vector", lambda e, ft=ft, pdst=pdst: e.tensor_copy(
                    a8(4096 + ft * 128, [1, 128]), pdst[:, 0:128]),
                   inc=("sV", 1), tag=f"{key}_ev{ft}")

        def phase_w(E):
            op, wait = E.op, E.wait
            wait("tensor", "ident")
            E.zerotag("wfree_0")
            prev_free = "wfree_0"
            prev_store = None
            prev_tr = None
            for kind, src, dst in (("g", wg_h, qg_in), ("u", wu_h, qu_in)):
                for j in range(8):
                    key = f"w{kind}{j}"
                    wait("scalar", prev_free)
                    op("scalar", lambda e, src=src, j=j: e.dma_start(
                        out=aF(0, [1, D]),
                        in_=ap(src, j * 128 * D, [[D, 128], [1, D]])),
                       inc=("sXL", 16), tag=f"{key}_ld")
                    wait("vector", f"{key}_ld")
                    wq_chain(E, key, D, prev_tr)
                    prev_free = f"{key}_t0"
                    wq_transposes(E, key, ND, prev_store)
                    wait("gpsimd", f"{key}_ev{ND - 1}")
                    op("gpsimd", lambda e, j=j, dst=dst: e.dma_start(
                        out=ap(dst, (j // 4) * (ND * 128 * 512) + (j % 4) * 128,
                               [[512, 128], [128 * 512, ND], [1, 128]]),
                        in_=a8(4096, [128, ND], [1, 128])),
                       inc=("sO", 16), tag=f"{key}_st")
                    prev_store = f"{key}_st"
                    prev_tr = f"{key}_tr{ND - 1}"
            for j in range(16):
                key = f"wd{j}"
                wait("scalar", prev_free)
                op("scalar", lambda e, j=j: e.dma_start(
                    out=aF(0, [1, 1024]),
                    in_=ap(wd_h, j * 128 * 1024, [[1024, 128], [1, 1024]])),
                   inc=("sXL", 16), tag=f"{key}_ld")
                wait("vector", f"{key}_ld")
                wq_chain(E, key, 1024, prev_tr)
                prev_free = f"{key}_t0"
                wq_transposes(E, key, 8, prev_store)
                wait("gpsimd", f"{key}_ev7")
                op("gpsimd", lambda e, j=j: e.dma_start(
                    out=ap(qd_in, j * 128, [[D, 128], [128 * D, 8], [1, 128]]),
                    in_=a8(4096, [128, 8], [1, 128])),
                   inc=("sO", 16), tag=f"{key}_st")
                prev_store = f"{key}_st"
                prev_tr = f"{key}_tr7"
            for t in ("wg7_st", "wu7_st", "wd15_st"):
                wait("gpsimd", t)
            if PH.get("dly_pre_ag"):
                for _ in range(PH["dly_pre_ag"]):
                    op("gpsimd", lambda e: e.engine_nop())
            op("gpsimd", lambda e: e.collective_compute(
                "AllGather", ALU.bypass, replica_groups=[list(range(8))],
                ins=[qg_in.ap().opt()], outs=[qg_all.ap().opt()]),
               inc=("sC", 1), tag="ag_g")
            op("gpsimd", lambda e: e.collective_compute(
                "AllGather", ALU.bypass, replica_groups=[list(range(8))],
                ins=[qu_in.ap().opt()], outs=[qu_all.ap().opt()]),
               inc=("sC", 1), tag="ag_u")
            op("gpsimd", lambda e: e.collective_compute(
                "AllGather", ALU.bypass, replica_groups=[list(range(8))],
                ins=[qd_in.ap().opt()], outs=[qd_all.ap().opt()]),
               inc=("sC", 1), tag="ag_d")

        def ln_stats(E, key, xoff, save_col=None, prev_user=None):
            op, wait, bar = E.op, E.wait, E.bar
            if prev_user is not None:
                E.wait("vector", prev_user)
            xf = aF(xoff, [1, D])
            scr = aF(8192, [1, D])
            op("scalar", lambda e: e.activation(
                scr, xf, AF.Identity, accum_out=small[:, 2:3]), inc=("sS", 1))
            op("scalar", lambda e: e.activation(
                scr, xf, AF.Square, accum_out=small[:, 3:4]),
               inc=("sS", 1), tag=f"{key}_sums")
            wait("vector", f"{key}_sums")
            bar("vector", lambda e: e.tensor_scalar(
                small[:, 4:5], small[:, 2:3], 1.0 / D, None, ALU.mult))
            bar("vector", lambda e: e.tensor_tensor(
                small[:, 5:6], small[:, 4:5], small[:, 4:5], ALU.mult))
            bar("vector", lambda e: e.scalar_tensor_tensor(
                small[:, 6:7], small[:, 3:4], 1.0 / D, small[:, 5:6],
                ALU.mult, ALU.subtract))
            bar("vector", lambda e: e.tensor_scalar(
                small[:, 6:7], small[:, 6:7], EPS, None, ALU.add))
            op("vector", lambda e: e.engine_nop(), inc=("sV", 1), tag=f"{key}_var")
            wait("scalar", f"{key}_var")
            op("scalar", lambda e: e.activation(
                small[:, 7:8], small[:, 6:7], AF.Sqrt), inc=("sS", 1), tag=f"{key}_sd")
            wait("vector", f"{key}_sd")
            bar("vector", lambda e: e.reciprocal(small[:, 8:9], small[:, 7:8]))
            bar("vector", lambda e: e.tensor_tensor(
                small[:, 9:10], small[:, 8:9], small[:, 8:9], ALU.mult))
            bar("vector", lambda e: e.tensor_tensor(
                small[:, 9:10], small[:, 9:10], small[:, 6:7], ALU.mult))
            bar("vector", lambda e: e.tensor_scalar(
                small[:, 9:10], small[:, 9:10], -0.5, 1.5, ALU.mult, ALU.add))
            bar("vector", lambda e: e.tensor_tensor(
                small[:, 0:1], small[:, 8:9], small[:, 9:10], ALU.mult))
            bar("vector", lambda e: e.scalar_tensor_tensor(
                small[:, 1:2], small[:, 4:5], -1.0, small[:, 0:1],
                ALU.mult, ALU.mult))
            if save_col is not None:
                bar("vector", lambda e, c=save_col: e.tensor_copy(
                    small[:, 16 + c:17 + c], small[:, 0:1]))
                bar("vector", lambda e, c=save_col: e.tensor_copy(
                    small[:, 32 + c:33 + c], small[:, 1:2]))
            op("vector", lambda e: e.engine_nop(), inc=("sV", 1), tag=f"{key}_ab")

        def act_quant(E, tag, srcoff, n, dst_fn, cga, tqoff=7168):
            op, wait, bar = E.op, E.wait, E.bar
            nb = n // 32
            amax = stat[:, 192:192 + nb]
            mt = statI[:, 64:64 + nb]
            fasc = stat[:, 0:nb]
            gasc = stat[:, 64:64 + nb]
            src3 = aF(srcoff, [32, nb], [1, 32])
            bar("vector", lambda e: e.tensor_reduce(
                ap(stat, 192, [[256, 128], [1, nb], [0, 1]]), src3, AX.X,
                ALU.max, apply_absolute_value=True))
            bar("vector", lambda e: e.tensor_scalar(
                mt, amax.bitcast(I32), 0x7F800000, None, ALU.bitwise_and))
            bar("vector", lambda e: e.tensor_scalar(
                fasc.bitcast(I32), mt, -1, C250, ALU.mult, ALU.add))
            bar("vector", lambda e: e.tensor_scalar(
                gasc.bitcast(I32), mt, cga, None, ALU.add))
            fab = ap(stat, 0, [[256, 128], [1, nb], [0, 32]])
            gab = ap(stat, 64, [[256, 128], [1, nb], [0, 32]])
            tq3 = aF(tqoff, [32, nb], [1, 32])
            tqf = aF(tqoff, [1, n])
            bar("vector", lambda e: e.tensor_tensor(tq3, src3, fab, ALU.mult))
            t8 = a8(32768, [1, n])
            t83 = a8(32768, [32, nb], [1, 32])
            bar("vector", lambda e: e.tensor_scalar(
                t8, tqf, CLIP_A, -CLIP_A, ALU.min, ALU.max))
            op("vector", lambda e: e.tensor_tensor(dst_fn(), t83, gab, ALU.mult),
               inc=("sV", 1), tag=tag)

        def gphase(E, tag, src_h, rdy):
            op, wait, bar = E.op, E.wait, E.bar
            if tag == "f2":
                for eng in ("scalar", "vector", "tensor", "sync", "gpsimd"):
                    wait(eng, "at_done" if PH["attn"] else "cp2_s15")
            for t in range(NT):
                xoff = 0 if t % 2 == 0 else 2048
                if t >= 2:
                    wait("scalar", f"{tag}_pfree{t - 2}")
                if rdy is not None:
                    wait("scalar", rdy.format(t=t))
                if t == 0 and tag == "f1":
                    wait("scalar", "wd15_qwb")
                op("scalar", lambda e, t=t, xoff=xoff: e.dma_start(
                    out=aF(xoff, [1, D]),
                    in_=ap(src_h, t * 128 * D, [[D, 128], [1, D]])),
                   inc=("sXL", 16), tag=f"{tag}_xl{t}")
                wait("scalar", f"{tag}_xl{t}")
                wait("vector", f"{tag}_xl{t}")
                ln_stats(E, f"{tag}_ln{t}", xoff,
                         prev_user=(f"{tag}_xln{t - 1}" if t >= 1 else None))
                wait("scalar", f"{tag}_ln{t}_ab")
                if t >= 1:
                    wait("scalar", f"{tag}_qx{t - 1}")
                op("scalar", lambda e, xoff=xoff: e.activation(
                    aF(4096, [1, D]), aF(xoff, [1, D]),
                    AF.Identity, bias=small[:, 1:2], scale=small[:, 0:1]),
                   inc=("sS", 1), tag=f"{tag}_xln{t}")
                E.settag(f"{tag}_pfree{t}", f"{tag}_xln{t}")
                wait("vector", f"{tag}_xln{t}")
                if t >= 1:
                    wait("vector", f"{tag}_qxt{t - 1}_{ND - 1}")
                act_quant(E, f"{tag}_qx{t}", 4096, D,
                          lambda: aB(0, [32, 64], [1, 32]), CGA_X, tqoff=6144)
                wait("tensor", f"{tag}_qx{t}")
                for ft in range(ND):
                    pdst = pt0 if ft % 2 == 0 else pt1
                    if ft >= 2:
                        wait("tensor", f"{tag}_qxe{t}_{ft - 2}")
                    elif t >= 1:
                        wait("tensor", f"{tag}_qxe{t - 1}_{ND - 2 + ft}")
                    op("tensor", lambda e, ft=ft, pdst=pdst: e.transpose(
                        pdst[:, 0:128], aB(ft * 128, [1, 128]),
                        ident[:, :]), inc=("sT", 1), tag=f"{tag}_qxt{t}_{ft}")
                    wait("vector", f"{tag}_qxt{t}_{ft}")
                    op("vector", lambda e, t=t, ft=ft, pdst=pdst:
                       e.tensor_copy(
                           ap(qxT, ft * TC + t * 128, [[32768, 128], [1, 128]]),
                           pdst[:, 0:128]),
                       inc=("sV", 1), tag=f"{tag}_qxe{t}_{ft}")
            def iq_stage(gi):
                isub, t = gi // NT, gi % NT
                par = gi % 2
                ptr = pt0 if par == 0 else pt1
                wait("tensor", f"{tag}_qi{isub}_{t}")
                if gi >= 2:
                    pi = gi - 2
                    wait("tensor", f"{tag}_iqe{pi // NT}_{pi % NT}")
                else:
                    wait("tensor", f"{tag}_qxe{NT - 1}_{ND - 2 + par}")
                for it in range(4):
                    op("tensor", lambda e, it=it, ptr=ptr, par=par: e.transpose(
                        ptr[:, it * 128:(it + 1) * 128],
                        aB(14336 + par * 512 + it * 128, [1, 128]), ident[:, :]),
                       inc=(("sT", 1) if it == 3 else None),
                       tag=(f"{tag}_iqt{isub}_{t}" if it == 3 else None))
                wait("vector", f"{tag}_iqt{isub}_{t}")
                if gi >= 2:
                    pi = gi - 2
                    wait("vector", f"{tag}_iqs{pi // NT}_{pi % NT}")
                op("vector", lambda e, ptr=ptr, par=par: e.tensor_copy(
                    aB(15360 + par * 512, [1, 512]), ptr[:, 0:512]),
                   inc=("sV", 1), tag=f"{tag}_iqe{isub}_{t}")
                wait("gpsimd", f"{tag}_iqe{isub}_{t}")
                op("gpsimd", lambda e, isub=isub, t=t, par=par: e.dma_start(
                    out=ap(interq_s, t * II * 128 + isub * 512 * 128,
                           [[128, 128], [128 * 128, 4], [1, 128]]),
                    in_=aB(15360 + par * 512, [128, 4], [1, 128])),
                   inc=("sO", 16), tag=f"{tag}_iqs{isub}_{t}")

            for isub in range(NI512):
                parw = isub % 2
                goff = parw * 8192
                uoff = 16384 + parw * 8192
                if isub >= 2:
                    wait("sync", f"{tag}_wfree{isub - 2}")
                if isub == 0:
                    wait("sync", "ag_g")
                    wait("sync", "ag_u")
                    if PH.get("dly_post_ag"):
                        for _ in range(PH["dly_post_ag"]):
                            op("sync", lambda e: e.nop())
                    wait("sync", f"{tag}_qxe0_{ND - 1}")
                op("sync", lambda e, isub=isub, goff=goff: e.dma_start(
                    out=a8(goff, [1, 8192]),
                    in_=ap(qg_all, isub * ND * 128 * 512,
                           [[512, 128], [128 * 512, ND], [1, 512]])),
                   inc=("sWL", 16), tag=f"{tag}_wg{isub}")
                op("sync", lambda e, isub=isub, uoff=uoff: e.dma_start(
                    out=a8(uoff, [1, 8192]),
                    in_=ap(qu_all, isub * ND * 128 * 512,
                           [[512, 128], [128 * 512, ND], [1, 512]])),
                   inc=("sWL", 16), tag=f"{tag}_wu{isub}")
                for t in range(NT):
                    gi = isub * NT + t
                    par = gi % 2
                    b = (gi % 3) * 2
                    pg, pu = pb[b], pb[b + 1]
                    wait("tensor", f"{tag}_wg{isub}")
                    wait("tensor", f"{tag}_wu{isub}")
                    if isub == 0:
                        wait("tensor", f"{tag}_qxe{t}_{ND - 1}")
                    if gi >= 3:
                        pi = gi - 3
                        wait("tensor", f"{tag}_int{pi // NT}_{pi % NT}")
                    for ft in range(ND):
                        op("tensor", lambda e, t=t, ft=ft, pg=pg, goff=goff: e.matmul(
                            pg[:, :],
                            ap(qxT, ft * TC + t * 128, [[32768, 128], [1, 128]]),
                            a8(goff + ft * 512, [1, 512]),
                            start=(ft == 0), stop=(ft == ND - 1)),
                           inc=(("sT", 1) if ft == ND - 1 else None),
                           tag=(f"{tag}_mmg{isub}_{t}" if ft == ND - 1 else None))
                        op("tensor", lambda e, t=t, ft=ft, pu=pu, uoff=uoff: e.matmul(
                            pu[:, :],
                            ap(qxT, ft * TC + t * 128, [[32768, 128], [1, 128]]),
                            a8(uoff + ft * 512, [1, 512]),
                            start=(ft == 0), stop=(ft == ND - 1)),
                           inc=(("sT", 1) if ft == ND - 1 else None),
                           tag=(f"{tag}_mmu{isub}_{t}" if ft == ND - 1 else None))
                    if t == NT - 1:
                        E.settag(f"{tag}_wfree{isub}", f"{tag}_mmu{isub}_{t}")
                    wait("scalar", f"{tag}_mmg{isub}_{t}")
                    if gi >= 2:
                        pi = gi - 2
                        wait("scalar", f"{tag}_int{pi // NT}_{pi % NT}")
                    op("scalar", lambda e, pg=pg, par=par: e.activation(
                        aF(6144 + par * 512, [1, 512]), pg[:, :], AF.Silu,
                        scale=2.0 ** -6),
                       inc=("sS", 1), tag=f"{tag}_silu{isub}_{t}")
                    wait("vector", f"{tag}_silu{isub}_{t}")
                    wait("vector", f"{tag}_mmu{isub}_{t}")
                    bar("vector", lambda e, pu=pu, par=par: e.tensor_tensor(
                        aF(7168 + par * 512, [1, 512]), aF(6144 + par * 512, [1, 512]),
                        pu[:, :], ALU.mult))
                    op("vector", lambda e: e.engine_nop(), inc=("sV", 1),
                       tag=f"{tag}_int{isub}_{t}")
                    if gi >= 2:
                        pi = gi - 2
                        wait("vector", f"{tag}_iqt{pi // NT}_{pi % NT}")
                    act_quant(E, f"{tag}_qi{isub}_{t}", 7168 + par * 512, 512,
                              lambda par=par: aB(14336 + par * 512, [32, 16], [1, 32]),
                              CGA_I, tqoff=8192)
                    if gi >= 1:
                        iq_stage(gi - 1)
            iq_stage(NI512 * NT - 1)

        def dphase(E, tag, src_h, dst_h, rdy, agtag, gtag):
            op, wait, bar = E.op, E.wait, E.bar
            wait("sync", f"{gtag}_mmu{NI512 - 1}_{NT - 1}")
            wait("sync", f"{gtag}_qi{NI512 - 1}_{NT - 1}")
            if agtag:
                wait("sync", agtag)
            for r in range(8):
                op("sync", lambda e, r=r: e.dma_start(
                    out=ap(qwd_res, r * 8 * D, [[131072, 128], [D, 8], [1, D]]),
                    in_=ap(qd_all, r * 8 * 128 * D, [[D, 128], [128 * D, 8], [1, D]])),
                   inc=("sWL", 16), tag=f"{tag}_dw{r}")
            for t in range(NT):
                poff = (t % 2) * 8192
                if t >= 2:
                    wait("scalar", f"{tag}_plfree{t - 2}")
                if t < 2:
                    wait("scalar", f"{gtag}_iqs{NI512 - 1}_{NT - 1}")
                else:
                    wait("scalar", f"{gtag}_iqs{NI512 - 1}_{t}")
                op("scalar", lambda e, t=t, poff=poff: e.dma_start(
                    out=aB(poff, [1, 8192]),
                    in_=ap(interq_s, t * II * 128,
                           [[128, 128], [128 * 128, NIT], [1, 128]])),
                   inc=("sPL", 16), tag=f"{tag}_pl{t}")
                xoff = 0 if t % 2 == 0 else 2048
                if t >= 2:
                    wait("scalar", f"{tag}_xrfree{t - 2}")
                if rdy is not None:
                    wait("scalar", rdy.format(t=t))
                op("scalar", lambda e, t=t, xoff=xoff: e.dma_start(
                    out=aF(xoff, [1, D]),
                    in_=ap(src_h, t * 128 * D, [[D, 128], [1, D]])),
                   inc=("sXL", 16), tag=f"{tag}_xr{t}")
                wait("tensor", f"{tag}_pl{t}")
                if t == 0:
                    for r in range(8):
                        wait("tensor", f"{tag}_dw{r}")
                banks = [(4 * t + k) % 6 for k in range(4)]
                for k in range(4):
                    n = 4 * t + k - 6
                    if n >= 0:
                        wait("tensor", f"{tag}_dev{n // 4}_{n % 4}")
                for it in range(NIT):
                    for k in range(4):
                        bk = pb[banks[k]]
                        op("tensor", lambda e, it=it, k=k, bk=bk, poff=poff: e.matmul(
                            bk[:, :], aB(poff + it * 128, [1, 128]),
                            ap(qwd_res, it * D + k * 512, [[131072, 128], [1, 512]]),
                            start=(it == 0), stop=(it == NIT - 1)),
                           inc=(("sT", 1) if it == NIT - 1 else None),
                           tag=(f"{tag}_dmm{t}_{k}" if it == NIT - 1 else None))
                E.settag(f"{tag}_plfree{t}", f"{tag}_dmm{t}_3")
                ooff = 4096 if t % 2 == 0 else 6144
                for k in range(4):
                    bk = pb[banks[k]]
                    wait("vector", f"{tag}_dmm{t}_{k}")
                    if k == 0:
                        wait("vector", f"{tag}_xr{t}")
                        if t >= 2:
                            wait("vector", f"{tag}_dst{t - 2}")
                    op("vector", lambda e, k=k, bk=bk, xoff=xoff, ooff=ooff:
                       e.scalar_tensor_tensor(
                           aF(ooff + k * 512, [1, 512]), bk[:, :],
                           0.5 * 2.0 ** -6, aF(xoff + k * 512, [1, 512]),
                           ALU.mult, ALU.add),
                       inc=("sV", 1), tag=f"{tag}_dev{t}_{k}")
                E.settag(f"{tag}_xrfree{t}", f"{tag}_dev{t}_3")
                wait("gpsimd", f"{tag}_dev{t}_3")
                op("gpsimd", lambda e, t=t, ooff=ooff: e.dma_start(
                    out=ap(dst_h, t * 128 * D, [[D, 128], [1, D]]),
                    in_=aF(ooff, [1, D])),
                   inc=("sO", 16), tag=f"{tag}_dst{t}")

        def copy_dram(E, tag, src, dst, rdy, entry=None):
            op, wait = E.op, E.wait
            for t in range(NT):
                if t == 0 and tag == "cp1":
                    wait("scalar", "wd15_qwb")
                if t < 2 and entry is not None:
                    wait("scalar", entry)
                if t >= 2:
                    wait("scalar", f"{tag}_s{t - 2}")
                if rdy is not None:
                    wait("scalar", rdy.format(t=t))
                xoff = 0 if t % 2 == 0 else 2048
                op("scalar", lambda e, t=t, xoff=xoff: e.dma_start(
                    out=aF(xoff, [1, D]),
                    in_=ap(src, t * 128 * D, [[D, 128], [1, D]])),
                   inc=("sXL", 16), tag=f"{tag}_l{t}")
                wait("gpsimd", f"{tag}_l{t}")
                op("gpsimd", lambda e, t=t, xoff=xoff: e.dma_start(
                    out=ap(dst, t * 128 * D, [[D, 128], [1, D]]),
                    in_=aF(xoff, [1, D])),
                   inc=("sO", 16), tag=f"{tag}_s{t}")

        def aphase(E, tag):
            op, wait, bar = E.op, E.wait, E.bar
            fdone = "f1_dmm15_3" if PH["ffn1"] else "cp1_s15"
            for eng in ("gpsimd", "scalar", "vector", "tensor", "sync"):
                wait(eng, fdone)
            xrdy = "f1_dst{t}" if PH["ffn1"] else "cp1_s{t}"
            op("gpsimd", lambda e: e.iota(
                statI[:, 0:128], [[1, 64], [0, 2]], channel_multiplier=0,
                allow_small_or_imprecise_dtypes=True),
               inc=("sG", 1), tag=f"{tag}_iotap")
            wait("vector", f"{tag}_iotap")
            bar("vector", lambda e: e.tensor_copy(aF(0, [1, 128]), statI[:, 0:128]))
            op("vector", lambda e: e.engine_nop(), inc=("sV", 1), tag=f"{tag}_j2")
            wait("scalar", f"{tag}_j2")
            op("scalar", lambda e: e.activation(
                aF(128, [1, 128]), aF(0, [1, 128]),
                AF.Exp, scale=-float(np.log(THETA)) / 64.0),
               inc=("sS", 1), tag=f"{tag}_freqs")
            wait("gpsimd", f"{tag}_j2")
            op("gpsimd", lambda e: e.iota(
                statI[:, 0:1], [[0, 1]], channel_multiplier=1,
                allow_small_or_imprecise_dtypes=True),
               inc=("sG", 1), tag=f"{tag}_iotac")
            op("scalar", lambda e: e.dma_start(
                out=ap(small, 40, [[512, 1], [1, 4]]), in_=slot_h.ap()),
               inc=("sXL", 16), tag=f"{tag}_slot")
            wait("vector", f"{tag}_slot")
            bar("vector", lambda e: e.tensor_copy(
                ap(arenaB, 4300, [[16384, 1], [1, 4]]),
                ap(small, 40, [[512, 1], [1, 4]])))
            op("vector", lambda e: e.engine_nop(), inc=("sV", 1),
               tag=f"{tag}_slotbf")
            wait("tensor", f"{tag}_slotbf")
            wait("tensor", "ones")
            op("tensor", lambda e: e.matmul(
                pb[3][:, 0:4], ap(ones1, 0, [[1024, 1], [1, 128]]),
                ap(arenaB, 4300, [[16384, 1], [1, 4]]),
                start=True, stop=True), inc=("sT", 1), tag=f"{tag}_slotmm")
            wait("vector", f"{tag}_slotmm")
            op("vector", lambda e: e.tensor_copy(
                ap(small, 44, [[512, 128], [1, 4]]), pb[3][:, 0:4]),
               inc=("sV", 1), tag=f"{tag}_slotb")
            wait("vector", f"{tag}_slotb")
            wait("vector", f"{tag}_iotac")
            wait("vector", f"{tag}_freqs")
            bar("vector", lambda e: e.tensor_copy(small[:, 48:49], statI[:, 0:1]))
            op("vector", lambda e: e.scalar_tensor_tensor(
                small[:, 49:50], small[:, 45:46], 2048.0, small[:, 48:49],
                ALU.mult, ALU.add), inc=("sV", 1), tag=f"{tag}_pos")
            wait("gpsimd", f"{tag}_j2")
            wait("gpsimd", f"{tag}_pos")
            op("gpsimd", lambda e: e.iota(
                statI[:, 0:128], [[0, 64], [1, 2]], channel_multiplier=0,
                allow_small_or_imprecise_dtypes=True),
               inc=("sG", 1), tag=f"{tag}_sgni")
            wait("vector", f"{tag}_sgni")
            bar("vector", lambda e: e.tensor_copy(aF(256, [1, 128]), statI[:, 0:128]))
            bar("vector", lambda e: e.tensor_scalar(
                aF(256, [1, 128]), aF(256, [1, 128]), 2.0, -1.0, ALU.mult, ALU.add))
            c1 = float(np.float32(6.28125))
            c2 = float(np.float32(0.001934051513671875))
            c3 = float(2 * np.pi) - c1 - c2
            for t in range(NT):
                bar("vector", lambda e, t=t: e.tensor_scalar(
                    small[:, 50:51], small[:, 49:50], float(128 * t), None, ALU.add))
                bar("vector", lambda e: e.tensor_scalar(
                    aF(384, [1, 128]), aF(128, [1, 128]), small[:, 50:51], None,
                    ALU.mult))
                bar("vector", lambda e: e.tensor_scalar(
                    aF(512, [1, 128]), aF(384, [1, 128]),
                    float(1.0 / (2 * np.pi)), None, ALU.mult))
                bar("vector", lambda e: e.tensor_copy(
                    statI[:, 0:128], aF(512, [1, 128])))
                bar("vector", lambda e: e.tensor_copy(
                    aF(512, [1, 128]), statI[:, 0:128]))
                bar("vector", lambda e: e.scalar_tensor_tensor(
                    aF(640, [1, 128]), aF(512, [1, 128]), -c1,
                    aF(384, [1, 128]), ALU.mult, ALU.add))
                bar("vector", lambda e: e.scalar_tensor_tensor(
                    aF(640, [1, 128]), aF(512, [1, 128]), -c2,
                    aF(640, [1, 128]), ALU.mult, ALU.add))
                bar("vector", lambda e: e.scalar_tensor_tensor(
                    aF(640, [1, 128]), aF(512, [1, 128]), -c3,
                    aF(640, [1, 128]), ALU.mult, ALU.add))
                def _wrap(dst, shift, tg=None):
                    bar("vector", lambda e: e.tensor_scalar(
                        aF(dst, [1, 128]), aF(640, [1, 128]), shift, None, ALU.add))
                    bar("vector", lambda e: e.tensor_scalar(
                        aF(dst + 5120, [1, 128]), aF(dst, [1, 128]),
                        -float(np.pi), None, ALU.is_lt))
                    bar("vector", lambda e: e.tensor_scalar(
                        aF(dst + 5248, [1, 128]), aF(dst, [1, 128]),
                        float(np.pi), None, ALU.is_gt))
                    bar("vector", lambda e: e.tensor_tensor(
                        aF(dst + 5120, [1, 128]), aF(dst + 5120, [1, 128]),
                        aF(dst + 5248, [1, 128]), ALU.subtract))
                    E.op("vector", lambda e: e.scalar_tensor_tensor(
                        aF(dst, [1, 128]), aF(dst + 5120, [1, 128]),
                        float(2 * np.pi), aF(dst, [1, 128]), ALU.mult, ALU.add),
                        inc=("sV", 1), tag=tg)
                _wrap(768, 0.0)
                _wrap(896, float(np.pi / 2), tg=f"{tag}_wr{t}")
                wait("scalar", f"{tag}_wr{t}")
                op("scalar", lambda e: e.activation(
                    aF(1024, [1, 128]), aF(768, [1, 128]), AF.Sin), inc=("sS", 1))
                op("scalar", lambda e, t=t: e.activation(
                    tabc[:, t * 128:(t + 1) * 128], aF(896, [1, 128]), AF.Sin),
                   inc=("sS", 1), tag=f"{tag}_tab{t}")
                wait("vector", f"{tag}_tab{t}")
                op("vector", lambda e, t=t: e.tensor_tensor(
                    tabs[:, t * 128:(t + 1) * 128], aF(1024, [1, 128]),
                    aF(256, [1, 128]), ALU.mult),
                   inc=("sV", 1), tag=f"{tag}_tabs{t}")
            op("gpsimd", lambda e: e.dma_start(
                out=ws(0, [512, ND], [1, 512]),
                in_=ap(wk_h, 0, [[512, 128], [512 * 128, ND], [1, 512]])),
               inc=("sWA", 16), tag=f"{tag}_wk")
            op("gpsimd", lambda e: e.dma_start(
                out=ws(8192, [512, ND], [1, 512]),
                in_=ap(wv_h, 0, [[512, 128], [512 * 128, ND], [1, 512]])),
               inc=("sWA", 16), tag=f"{tag}_wv")
            op("gpsimd", lambda e: e.dma_start(
                out=ap(ckv, 0, [[4096, 128], [128, ND], [1, 128]]),
                in_=ap(wiq_h, 0, [[128, 128], [128 * 128, ND], [1, 128]])),
               inc=("sWA", 16), tag=f"{tag}_wiq")
            op("scalar", lambda e: e.dma_start(
                out=ap(small, 56, [[512, 1], [1, 256]]),
                in_=ap(cw_h, 0, [[256, 1], [1, 256]])),
               inc=("sXL", 16), tag=f"{tag}_cw")
            wait("vector", f"{tag}_cw")
            bar("vector", lambda e: e.tensor_copy(
                ap(arenaB, 4096, [[16384, 1], [1, 256]]),
                ap(small, 56, [[512, 1], [1, 256]])))
            op("vector", lambda e: e.engine_nop(), inc=("sV", 1), tag=f"{tag}_cwb")
            wait("tensor", f"{tag}_cwb")
            wait("tensor", "ones")
            op("tensor", lambda e: e.matmul(
                pb[2][:, 0:128], ap(ones1, 0, [[1024, 1], [1, 128]]),
                ap(arenaB, 4096, [[16384, 1], [1, 128]]),
                start=True, stop=True), inc=("sT", 1))
            op("tensor", lambda e: e.matmul(
                pb[2][:, 128:256], ap(ones1, 0, [[1024, 1], [1, 128]]),
                ap(arenaB, 4096 + 128, [[16384, 1], [1, 128]]),
                start=True, stop=True), inc=("sT", 1), tag=f"{tag}_cwt")
            wait("vector", f"{tag}_cwt")
            op("vector", lambda e: e.tensor_copy(
                aF(5120, [1, 256]), pb[2][:, 0:256]),
               inc=("sV", 1), tag=f"{tag}_cwtile")
            for t in range(NT):
                xoff = 0 if t % 2 == 0 else 2048
                if t >= 2:
                    wait("scalar", f"{tag}_afree{t - 2}")
                if t < 2 and PH["ffn1"]:
                    wait("scalar", "f1_dev15_3")
                wait("scalar", xrdy.format(t=t))
                op("scalar", lambda e, t=t, xoff=xoff: e.dma_start(
                    out=aF(xoff, [1, D]),
                    in_=ap(x_mid, t * 128 * D, [[D, 128], [1, D]])),
                   inc=("sXL", 16), tag=f"{tag}_xl{t}")
                wait("scalar", f"{tag}_xl{t}")
                wait("vector", f"{tag}_xl{t}")
                ln_stats(E, f"{tag}_ln{t}", xoff, save_col=t,
                         prev_user=(f"{tag}_xln{t - 1}" if t >= 1 else None))
                wait("scalar", f"{tag}_ln{t}_ab")
                if t >= 1:
                    wait("scalar", f"{tag}_xtt{t - 1}_{ND - 1}")
                op("scalar", lambda e, xoff=xoff: e.activation(
                    aB(0, [1, D]), aF(xoff, [1, D]),
                    AF.Identity, bias=small[:, 1:2], scale=small[:, 0:1]),
                   inc=("sS", 1), tag=f"{tag}_xln{t}")
                E.settag(f"{tag}_afree{t}", f"{tag}_xln{t}")
                wait("tensor", f"{tag}_xln{t}")
                for ft in range(ND):
                    op("tensor", lambda e, ft=ft, t=t: e.matmul(
                        pb[5][:, ft * 16 + t:ft * 16 + t + 1], aB(ft * 128, [1, 128]),
                        ones1[:, 0:1], start=True, stop=True),
                       inc=(("sT", 1) if (t == NT - 1 and ft == ND - 1) else None),
                       tag=(f"{tag}_csmm" if (t == NT - 1 and ft == ND - 1) else None))
                for ft in range(ND):
                    pdst = pt0 if ft % 2 == 0 else pt1
                    if ft >= 2:
                        wait("tensor", f"{tag}_xte{t}_{ft - 2}")
                    elif t >= 1:
                        wait("tensor", f"{tag}_xte{t - 1}_{ND - 2 + ft}")
                        if ft == 0:
                            wait("tensor", f"{tag}_kde{t - 1}")
                    op("tensor", lambda e, ft=ft, pdst=pdst: e.transpose(
                        pdst[:, 0:128], aB(ft * 128, [1, 128]),
                        ident[:, :]), inc=("sT", 1), tag=f"{tag}_xtt{t}_{ft}")
                    wait("vector", f"{tag}_xtt{t}_{ft}")
                    if ft == 0 and t >= 1:
                        wait("vector", f"{tag}_kmm{t - 1}")
                        wait("vector", f"{tag}_vmm{t - 1}")
                        if t - 1 < 4:
                            wait("vector", f"{tag}_qimm{t - 1}")
                    op("vector", lambda e, ft=ft, pdst=pdst: e.tensor_copy(
                        aB(2048 + ft * 128, [1, 128]), pdst[:, 0:128]),
                       inc=("sV", 1), tag=f"{tag}_xte{t}_{ft}")
                wait("tensor", f"{tag}_xte{t}_{ND - 1}")
                if t == 0:
                    wait("tensor", f"{tag}_wk")
                    wait("tensor", f"{tag}_wv")
                    wait("tensor", f"{tag}_wiq")
                else:
                    wait("tensor", f"{tag}_krope{t - 1}")
                    wait("tensor", f"{tag}_vev{t - 1}")
                for ft in range(ND):
                    op("tensor", lambda e, ft=ft: e.matmul(
                        pb[0][:, :], aB(2048 + ft * 128, [1, 128]),
                        ws(ft * 512, [1, 512]),
                        start=(ft == 0), stop=(ft == ND - 1)),
                       inc=(("sT", 1) if ft == ND - 1 else None),
                       tag=(f"{tag}_kmm{t}" if ft == ND - 1 else None))
                    op("tensor", lambda e, ft=ft: e.matmul(
                        pb[1][:, :], aB(2048 + ft * 128, [1, 128]),
                        ws(8192 + ft * 512, [1, 512]),
                        start=(ft == 0), stop=(ft == ND - 1)),
                       inc=(("sT", 1) if ft == ND - 1 else None),
                       tag=(f"{tag}_vmm{t}" if ft == ND - 1 else None))
                if t < 4:
                    if t >= 1:
                        wait("tensor", f"{tag}_qie{t - 1}")
                    for ft in range(ND):
                        op("tensor", lambda e, ft=ft: e.matmul(
                            pb[4][:, 0:128],
                            ap(ckv, ft * 128, [[4096, 128], [1, 128]]),
                            aB(2048 + ft * 128, [1, 128]),
                            start=(ft == 0), stop=(ft == ND - 1)),
                           inc=(("sT", 1) if ft == ND - 1 else None),
                           tag=(f"{tag}_qimm{t}" if ft == ND - 1 else None))
                    wait("vector", f"{tag}_qimm{t}")
                    op("vector", lambda e, t=t: e.tensor_copy(
                        ap(ckv, 3584 + t * 128, [[4096, 128], [1, 128]]),
                        pb[4][:, 0:128]),
                       inc=("sV", 1), tag=f"{tag}_qie{t}")
                wait("vector", f"{tag}_kmm{t}")
                wait("vector", f"{tag}_tabs{NT - 1}")
                cosrep = ap(tabc, t * 128, [[2048, 128], [0, 4], [1, 128]])
                sinrep4 = ap(tabs, t * 128, [[2048, 128], [0, 4], [2, 64], [1, 2]])
                bar("vector", lambda e, cosrep=cosrep: e.tensor_tensor(
                    aF(4096, [128, 4], [1, 128]),
                    ap(pb[0], 0, [[512, 128], [128, 4], [1, 128]]),
                    cosrep, ALU.mult))
                ksw = ap(pb[0], 1, [[512, 128], [128, 4], [2, 64], [-1, 2]])
                bar("vector", lambda e, ksw=ksw, sinrep4=sinrep4: e.tensor_tensor(
                    aF(4608, [128, 4], [2, 64], [1, 2]), ksw, sinrep4, ALU.mult))
                op("vector", lambda e, t=t: e.tensor_tensor(
                    kv(t * 512, [1, 512]), aF(4096, [1, 512]),
                    aF(4608, [1, 512]), ALU.add),
                   inc=("sV", 1), tag=f"{tag}_krope{t}")
                wait("vector", f"{tag}_vmm{t}")
                op("vector", lambda e, t=t: e.tensor_copy(
                    kv(8192 + t * 512, [1, 512]), pb[1][:, :]),
                   inc=("sV", 1), tag=f"{tag}_vev{t}")
                wait("vector", f"{tag}_cwtile")
                cwa_rep = ap(arenaF, 5120, [[10240, 128], [0, 4], [1, 128]])
                cwb_rep = ap(arenaF, 5248, [[10240, 128], [0, 4], [1, 128]])
                kch3 = kv(t * 512, [128, 4], [1, 128])
                bar("vector", lambda e, kch3=kch3, cwa_rep=cwa_rep: e.tensor_tensor(
                    aF(4096, [128, 4], [1, 128]), kch3, cwa_rep, ALU.mult))
                bar("vector", lambda e: e.tensor_reduce(
                    aF(5376, [1, 4], [0, 1]), aF(4096, [128, 4], [1, 128]),
                    AX.X, ALU.add))
                bar("vector", lambda e, kch3=kch3, cwb_rep=cwb_rep: e.tensor_tensor(
                    aF(4096, [128, 4], [1, 128]), kch3, cwb_rep, ALU.mult))
                bar("vector", lambda e: e.tensor_reduce(
                    aF(5380, [1, 4], [0, 1]), aF(4096, [128, 4], [1, 128]),
                    AX.X, ALU.add))
                op("vector", lambda e: e.tensor_copy(
                    aB(4224, [1, 8]), aF(5376, [1, 8])), inc=("sV", 1),
                   tag=f"{tag}_kd{t}")
                wait("tensor", f"{tag}_kd{t}")
                if t >= 2:
                    wait("tensor", f"{tag}_kde{t - 2}")
                ptr = pt0 if t % 2 == 0 else pt1
                op("tensor", lambda e, ptr=ptr: e.transpose(
                    ptr[:, 0:128], aB(4224, [1, 128]), ident[:, :]),
                   inc=("sT", 1), tag=f"{tag}_kdt{t}")
                wait("vector", f"{tag}_kdt{t}")
                op("vector", lambda e, t=t, ptr=ptr: e.tensor_copy(
                    ap(arenaB, 4352 + t * 128, [[16384, 8], [1, 128]]),
                    ap(ptr, 0, [[1024, 8], [1, 128]])),
                   inc=("sV", 1), tag=f"{tag}_kde{t}")
            kdallB = ap(arenaB, 4352, [[16384, 8], [1, 2048]])
            kdF = ap(arenaF, 4096, [[10240, 8], [1, 2048]])
            kd3 = ap(arenaF, 4096, [[10240, 8], [16, 128], [1, 16]])
            mx3 = ap(arenaF, 6400, [[10240, 8], [1, 128], [0, 1]])
            mxb = ap(arenaF, 6400, [[10240, 8], [1, 128], [0, 16]])
            bar("vector", lambda e: e.tensor_copy(kdF, kdallB))
            bar("vector", lambda e: e.tensor_reduce(mx3, kd3, AX.X, ALU.max))
            bar("vector", lambda e: e.tensor_tensor(kd3, kd3, mxb, ALU.subtract))
            op("vector", lambda e: e.engine_nop(), inc=("sV", 1), tag=f"{tag}_kdsub")
            wait("scalar", f"{tag}_kdsub")
            op("scalar", lambda e: e.activation(kdF, kdF, AF.Exp),
               inc=("sS", 1), tag=f"{tag}_kdexp")
            wait("vector", f"{tag}_kdexp")
            bar("vector", lambda e: e.tensor_reduce(mx3, kd3, AX.X, ALU.add))
            bar("vector", lambda e: e.reciprocal(
                ap(arenaF, 6400, [[10240, 8], [1, 128]]),
                ap(arenaF, 6400, [[10240, 8], [1, 128]])))
            bar("vector", lambda e: e.tensor_tensor(kd3, kd3, mxb, ALU.mult))
            shm = [(i + 4) % 32 for i in range(32)]
            bar("vector", lambda e: e.stream_shuffle(
                ap(arenaF, 6656, [[10240, 32], [1, 2048]]),
                ap(arenaF, 4096, [[10240, 32], [1, 2048]]), shm))
            op("vector", lambda e: e.tensor_tensor(
                ap(arenaF, 4096, [[10240, 4], [1, 2048]]),
                ap(arenaF, 4096, [[10240, 4], [1, 2048]]),
                ap(arenaF, 6656, [[10240, 4], [1, 2048]]), ALU.add),
               inc=("sV", 1), tag=f"{tag}_wrow")
            for c in range(NT):
                bar("vector", lambda e, c=c: e.tensor_copy(
                    ap(arenaB, 6400, [[16384, 4], [1, 128]]),
                    ap(arenaF, 4096 + c * 128, [[10240, 4], [1, 128]])))
                op("vector", lambda e: e.engine_nop(), inc=("sV", 1),
                   tag=f"{tag}_wcb{c}")
                wait("tensor", f"{tag}_wcb{c}")
                if c >= 2:
                    wait("tensor", f"{tag}_wte{c - 2}")
                ptr = pt0 if c % 2 == 0 else pt1
                op("tensor", lambda e, ptr=ptr: e.transpose(
                    ptr[:, 0:128], ap(arenaB, 6400, [[16384, 128], [1, 128]]),
                    ident[:, :]), inc=("sT", 1), tag=f"{tag}_wtt{c}")
                wait("vector", f"{tag}_wtt{c}")
                op("vector", lambda e, c=c, ptr=ptr: e.tensor_copy(
                    ap(ckv, 3072 + c * 4, [[4096, 128], [1, 4]]),
                    ap(ptr, 0, [[1024, 128], [1, 4]])),
                   inc=("sV", 1), tag=f"{tag}_wte{c}")
            wait("vector", "indt")
            for c in range(NT):
                wrep = ap(ckv, 3072 + c * 4, [[4096, 128], [1, 4], [0, 128]])
                bar("vector", lambda e, c=c, wrep=wrep: e.tensor_tensor(
                    aB(6528, [128, 4], [1, 128]), kv(c * 512, [128, 4], [1, 128]),
                    wrep, ALU.mult))
                op("vector", lambda e, c=c, wrep=wrep: e.tensor_tensor(
                    aB(7040, [128, 4], [1, 128]),
                    kv(8192 + c * 512, [128, 4], [1, 128]), wrep, ALU.mult),
                   inc=("sV", 1), tag=f"{tag}_kvw{c}")
                wait("tensor", f"{tag}_kvw{c}")
                if c >= 1:
                    wait("tensor", f"{tag}_cke{c - 1}")
                op("tensor", lambda e: e.matmul(
                    pb[2][0:8, 0:512], indt[:, :], aB(6528, [1, 512]),
                    start=True, stop=True), inc=("sT", 1))
                op("tensor", lambda e: e.matmul(
                    pb[3][0:8, 0:512], indt[:, :], aB(7040, [1, 512]),
                    start=True, stop=True), inc=("sT", 1), tag=f"{tag}_ckmm{c}")
                wait("vector", f"{tag}_ckmm{c}")
                if c >= 2:
                    wait("vector", f"{tag}_ckdma{c - 2}")
                par = (c % 2) * 1024
                op("vector", lambda e, par=par: e.tensor_copy(
                    ap(arenaB, 8192 + par, [[16384, 8], [1, 512]]),
                    ap(pb[2], 0, [[512, 8], [1, 512]])), inc=("sV", 1))
                op("vector", lambda e, par=par: e.tensor_copy(
                    ap(arenaB, 8192 + par + 512, [[16384, 8], [1, 512]]),
                    ap(pb[3], 0, [[512, 8], [1, 512]])),
                   inc=("sV", 1), tag=f"{tag}_cke{c}")
                wait("gpsimd", f"{tag}_cke{c}")
                op("gpsimd", lambda e, c=c, par=par: e.dma_start(
                    out=ap(at_in, ATO_CK + c * 8 * 512, [[512, 8], [1, 512]]),
                    in_=ap(arenaB, 8192 + par, [[16384, 8], [1, 512]])),
                   inc=("sO", 16))
                op("gpsimd", lambda e, c=c, par=par: e.dma_start(
                    out=ap(at_in, ATO_CV + c * 8 * 512, [[512, 8], [1, 512]]),
                    in_=ap(arenaB, 8192 + par + 512, [[16384, 8], [1, 512]])),
                   inc=("sO", 16), tag=f"{tag}_ckdma{c}")
            E.settag(f"{tag}_ckst", f"{tag}_ckdma{NT - 1}")
            E.settag(f"{tag}_cvst", f"{tag}_ckdma{NT - 1}")
            wait("vector", f"{tag}_csmm")
            bar("vector", lambda e: e.tensor_reduce(
                aF(4096, [1, 16]),
                ap(pb[5], 0, [[512, 128], [16, 16], [1, 16]]), AX.X, ALU.add))
            op("vector", lambda e: e.tensor_copy(
                ap(ckv, 3136 + 0, [[4096, 128], [1, 16]]), aF(4096, [1, 16])),
               inc=("sV", 1), tag=f"{tag}_csev")
            wait("gpsimd", f"{tag}_csev")
            wait("gpsimd", f"{tag}_qie3")
            op("gpsimd", lambda e: e.dma_start(
                out=ap(at_in, ATO_QI, [[512, 128], [1, 512]]),
                in_=ap(ckv, 3584, [[4096, 128], [1, 512]])),
               inc=("sO", 16), tag=f"{tag}_qist")
            op("gpsimd", lambda e: e.dma_start(
                out=ap(at_in, ATO_CS, [[16, 128], [1, 16]]),
                in_=ap(ckv, 3136, [[4096, 128], [1, 16]])),
               inc=("sO", 16), tag=f"{tag}_csst")
            for tg in (f"{tag}_ckst", f"{tag}_cvst", f"{tag}_qist", f"{tag}_csst"):
                wait("gpsimd", tg)
            op("gpsimd", lambda e: e.collective_compute(
                "AllGather", ALU.bypass,
                replica_groups=[[0, 1, 2, 3], [4, 5, 6, 7]],
                ins=[at_in.ap().opt()], outs=[at_all.ap().opt()]),
               inc=("sC", 1), tag="ag_at")
            wait("gpsimd", "ag_at")
            for r in range(4):
                for h in range(4):
                    op("gpsimd", lambda e, r=r, h=h: e.dma_start(
                        out=ap(ckv, 2048 + h * 512 + r * 128,
                               [[4096, 128], [1, 128]]),
                        in_=ap(at_all, r * AT_N + ATO_CV + h * 128,
                               [[512, 128], [1, 128]])),
                       inc=("sO", 16), tag=f"{tag}_ucv{r}_{h}")
            for r in range(4):
                for h in range(4):
                    op("gpsimd", lambda e, r=r, h=h: e.dma_start(
                        out=kv((4 * r + h) * 128, [1, 128]),
                        in_=ap(at_all, r * AT_N + ATO_CK + h * 128,
                               [[512, 128], [1, 128]])),
                       inc=("sO", 16), tag=f"{tag}_uck{r}_{h}")
            op("gpsimd", lambda e: e.dma_start(
                out=kv(2048, [1, 512]),
                in_=ap(at_all, 0 * AT_N + ATO_QI, [[512, 128], [1, 512]])),
               inc=("sO", 16), tag=f"{tag}_uqi")
            for r in range(4):
                op("gpsimd", lambda e, r=r: e.dma_start(
                    out=kv(2560 + r * 16, [1, 16]),
                    in_=ap(at_all, r * AT_N + ATO_CS, [[16, 128], [1, 16]])),
                   inc=("sO", 16), tag=f"{tag}_ucs{r}")
            for r in range(4):
                wait("vector", f"{tag}_ucs{r}")
            bar("vector", lambda e: e.tensor_copy(
                aF(4096, [16, 4], [1, 16]), kv(2560, [16, 4], [1, 16])))
            bar("vector", lambda e: e.tensor_tensor(
                aF(4096, [1, 16]), aF(4096, [1, 16]), aF(4096 + 16, [1, 16]), ALU.add))
            bar("vector", lambda e: e.tensor_tensor(
                aF(4096 + 32, [1, 16]), aF(4096 + 32, [1, 16]),
                aF(4096 + 48, [1, 16]), ALU.add))
            bar("vector", lambda e: e.tensor_tensor(
                aF(4096, [1, 16]), aF(4096, [1, 16]), aF(4096 + 32, [1, 16]), ALU.add))
            op("vector", lambda e: e.tensor_copy(
                aB(6400, [1, 16]), aF(4096, [1, 16])), inc=("sV", 1),
               tag=f"{tag}_cst")
            op("gpsimd", lambda e: e.dma_start(
                out=kv(2816, [128, ND], [1, 128]),
                in_=ap(wik_h, 0, [[128, 128], [128 * 128, ND], [1, 128]])),
               inc=("sWA", 16), tag=f"{tag}_wik")
            wait("tensor", f"{tag}_cst")
            wait("tensor", f"{tag}_wik")
            wait("tensor", f"{tag}_uqi")
            for ft in range(ND):
                op("tensor", lambda e, ft=ft: e.matmul(
                    pb[4][0:1, 0:128], aB(6400 + ft, [1, 1]),
                    kv(2816 + ft * 128, [1, 128]),
                    start=(ft == 0), stop=(ft == ND - 1)),
                   inc=(("sT", 1) if ft == ND - 1 else None),
                   tag=(f"{tag}_kimm" if ft == ND - 1 else None))
            wait("vector", f"{tag}_kimm")
            op("vector", lambda e: e.tensor_copy(
                ap(arenaB, 6528, [[16384, 1], [1, 128]]),
                ap(pb[4], 0, [[512, 1], [1, 128]])), inc=("sV", 1),
               tag=f"{tag}_kime")
            wait("tensor", f"{tag}_kime")
            wait("tensor", f"{tag}_kde{NT - 1}")
            op("tensor", lambda e: e.transpose(
                pt0[:, 0:1], ap(arenaB, 6528, [[16384, 1], [1, 128]]),
                ident[0:1, 0:1]), inc=("sT", 1), tag=f"{tag}_kimt")
            wait("vector", f"{tag}_kimt")
            op("vector", lambda e: e.tensor_copy(
                ap(arenaB, 6656, [[16384, 128], [1, 1]]),
                ap(pt0, 0, [[1024, 128], [1, 1]])), inc=("sV", 1),
               tag=f"{tag}_kimc")
            wait("tensor", f"{tag}_kimc")
            op("tensor", lambda e: e.matmul(
                pb[4][0:1, 0:512], ap(arenaB, 6656, [[16384, 128], [1, 1]]),
                kv(2048, [1, 512]), start=True, stop=True),
               inc=("sT", 1), tag=f"{tag}_smm")
            wait("vector", f"{tag}_smm")
            op("vector", lambda e: e.tensor_copy(
                ap(arenaB, 6784, [[16384, 1], [1, 512]]),
                ap(pb[4], 0, [[512, 1], [1, 512]])), inc=("sV", 1),
               tag=f"{tag}_sev")
            wait("tensor", f"{tag}_sev")
            op("tensor", lambda e: e.matmul(
                pb[4][:, 0:512], ap(ones1, 0, [[1024, 1], [1, 128]]),
                ap(arenaB, 6784, [[16384, 1], [1, 512]]),
                start=True, stop=True), inc=("sT", 1), tag=f"{tag}_bmm")
            for c in range(4):
                op("tensor", lambda e, c=c: e.transpose(
                    pt1[:, 2 * c:2 * c + 1],
                    ap(arenaB, 6784 + c * 128, [[16384, 1], [1, 128]]),
                    ident[0:1, 0:1]), inc=(("sT", 1) if c == 3 else None),
                   tag=(f"{tag}_sct" if c == 3 else None))
            wait("vector", f"{tag}_bmm")
            op("vector", lambda e: e.tensor_copy(
                aF(4096, [1, 512]), pb[4][:, 0:512]), inc=("sV", 1))
            wait("vector", f"{tag}_sct")
            op("vector", lambda e: e.tensor_copy(
                aF(4608, [1, 4]), ap(pt1, 0, [[1024, 128], [2, 4]])),
               inc=("sV", 1), tag=f"{tag}_scev")
            wait("vector", f"{tag}_scev")
            for c in range(4):
                bar("vector", lambda e, c=c: e.tensor_scalar(
                    aF(5120, [1, 512]), aF(4096, [1, 512]),
                    ap(arenaF, 4608 + c, [[10240, 128], [1, 1]]), None, ALU.is_gt))
                bar("vector", lambda e, c=c: e.tensor_reduce(
                    ap(arenaF, 5632 + c, [[10240, 128], [1, 1]]),
                    aF(5120, [1, 512]), AX.X, ALU.add))
            bar("vector", lambda e: e.tensor_scalar(
                aF(5120, [1, 4]), aF(5632, [1, 4]), 64.0, 1.0,
                ALU.is_lt, ALU.subtract))
            op("vector", lambda e: e.tensor_scalar(
                aB(6912, [1, 4]), aF(5120, [1, 4]), 30000.0, None, ALU.mult),
               inc=("sV", 1), tag=f"{tag}_maskc")
            wait("tensor", f"{tag}_maskc")
            op("tensor", lambda e: e.transpose(
                pt0[0:4, 0:128], ap(arenaB, 6912, [[16384, 128], [1, 4]]),
                ident[:, :]), inc=("sT", 1), tag=f"{tag}_maskt")
            wait("vector", f"{tag}_maskt")
            op("vector", lambda e: e.tensor_copy(
                ap(arenaB, 7424, [[16384, 4], [1, 128]]),
                ap(pt0, 0, [[1024, 4], [1, 128]])), inc=("sV", 1),
               tag=f"{tag}_maske")
            wait("gpsimd", f"{tag}_maske")
            op("gpsimd", lambda e: e.dma_start(
                out=ap(bounce_b, 0, [[1, 512]]),
                in_=ap(arenaB, 7424, [[16384, 4], [1, 128]])),
               inc=("sO", 16), tag=f"{tag}_maskbo")
            wait("gpsimd", f"{tag}_maskbo")
            op("gpsimd", lambda e: e.dma_start(
                out=ap(ones1, 512, [[1024, 1], [1, 512]]),
                in_=ap(bounce_b, 0, [[1, 512]])),
               inc=("sO", 16), tag=f"{tag}_maskrow")
            for i in range(16):
                r, h = i // 4, i % 4
                wait("tensor", f"{tag}_uck{r}_{h}")
                ptr = pt0 if i % 2 == 0 else pt1
                if i >= 2:
                    wait("tensor", f"{tag}_ckfe{i - 2}")
                elif i == 0:
                    wait("tensor", f"{tag}_maske")
                else:
                    wait("tensor", f"{tag}_scev")
                op("tensor", lambda e, i=i, ptr=ptr: e.transpose(
                    ptr[:, 0:128], kv(i * 128, [1, 128]), ident[:, :]),
                   inc=("sT", 1), tag=f"{tag}_ckft{i}")
                wait("vector", f"{tag}_ckft{i}")
                op("vector", lambda e, i=i, r=r, h=h, ptr=ptr: e.tensor_copy(
                    ap(ckv, h * 512 + r * 128, [[4096, 128], [1, 128]]),
                    ptr[:, 0:128]), inc=("sV", 1), tag=f"{tag}_ckfe{i}")
            op("vector", lambda e: e.engine_nop(), inc=("sV", 1), tag=f"{tag}_adone")

        def bphase(E, tag):
            op, wait, bar = E.op, E.wait, E.bar
            xrdy = "f1_dst{t}" if PH["ffn1"] else "cp1_s{t}"
            for eng in ("scalar", "vector", "tensor", "sync", "gpsimd"):
                wait(eng, f"{tag}_adone")
                wait(eng, f"{tag}_maskrow")
                for r in range(4):
                    wait(eng, f"{tag}_ucv{r}_3")
            for g in range(4):
                for tc in range(4):
                    t = 4 * g + tc
                    xoff = tc * 2048
                    wait("scalar", xrdy.format(t=t))
                    if g >= 1:
                        wait("scalar", f"{tag}_xst{t - 4}")
                    op("scalar", lambda e, t=t, xoff=xoff: e.dma_start(
                        out=aF(xoff, [1, D]),
                        in_=ap(x_mid, t * 128 * D, [[D, 128], [1, D]])),
                       inc=("sXL", 16), tag=f"{tag}_bxl{t}")
                    wait("scalar", f"{tag}_bxl{t}")
                    if t >= 1:
                        wait("scalar", f"{tag}_bxtt{t - 1}")
                    op("scalar", lambda e, t=t, xoff=xoff: e.activation(
                        aB(8192, [1, D]), aF(xoff, [1, D]), AF.Identity,
                        bias=small[:, 32 + t:33 + t], scale=small[:, 16 + t:17 + t]),
                       inc=("sS", 1), tag=f"{tag}_bxln{t}")
                    wait("tensor", f"{tag}_bxln{t}")
                    for ft in range(ND):
                        pdst = pt0 if ft % 2 == 0 else pt1
                        if ft >= 2:
                            wait("tensor", f"{tag}_bxe{t}_{ft - 2}")
                        elif t >= 1:
                            wait("tensor", f"{tag}_bxe{t - 1}_{ND - 2 + ft}")
                        op("tensor", lambda e, ft=ft, pdst=pdst:
                           e.transpose(
                               pdst[:, 0:128], aB(8192 + ft * 128, [1, 128]),
                               ident[:, :]),
                           inc=("sT", 1),
                           tag=(f"{tag}_bxtt{t}" if ft == ND - 1 else f"{tag}_bxt{t}_{ft}"))
                        wait("vector", f"{tag}_bxtt{t}" if ft == ND - 1 else f"{tag}_bxt{t}_{ft}")
                        if ft == 0 and g >= 1:
                            wait("vector", f"{tag}_qmm{g - 1}_1_{tc}")
                        op("vector", lambda e, tc=tc, ft=ft, pdst=pdst:
                           e.tensor_copy(
                               aB(ft * 512 + tc * 128, [1, 128]),
                               pdst[:, 0:128]),
                           inc=("sV", 1), tag=f"{tag}_bxe{t}_{ft}")
                    E.settag(f"{tag}_bxe{t}", f"{tag}_bxe{t}_{ND - 1}")
                for qp in range(2):
                    for half in range(2):
                        qc = 2 * qp + half
                        woff = half * 8192
                        if qp == 1:
                            wait("gpsimd", f"{tag}_qmm{g}_0_3")
                            wait("gpsimd", f"{tag}_qmm{g}_0_3a")
                        elif g >= 1:
                            wait("gpsimd", f"{tag}_smm{g - 1}_{H - 1}_3")
                        op("gpsimd", lambda e, qc=qc, woff=woff: e.dma_start(
                            out=ws(woff, [512, ND], [1, 512]),
                            in_=ap(wq_h, qc * 512,
                                   [[D, 128], [D * 128, ND], [1, 512]])),
                           inc=("sWA", 16), tag=f"{tag}_wq{g}_{qc}")
                    for tc in range(4):
                        t = 4 * g + tc
                        pq0 = pb[0] if tc % 2 == 0 else pb[2]
                        pq1 = pb[1] if tc % 2 == 0 else pb[3]
                        wait("tensor", f"{tag}_wq{g}_{2 * qp}")
                        wait("tensor", f"{tag}_wq{g}_{2 * qp + 1}")
                        wait("tensor", f"{tag}_bxe{t}")
                        if not (g == 0 and qp == 0 and tc < 2):
                            pass
                        idx = (g * 2 + qp) * 4 + tc
                        if idx >= 2:
                            pidx = idx - 2
                            wait("tensor", f"{tag}_rope{pidx}")
                        for ft in range(ND):
                            op("tensor", lambda e, tc=tc, ft=ft, pq0=pq0: e.matmul(
                                pq0[:, :], aB(ft * 512 + tc * 128, [1, 128]),
                                ws(0 * 8192 + ft * 512, [1, 512]),
                                start=(ft == 0), stop=(ft == ND - 1)),
                               inc=(("sT", 1) if ft == ND - 1 else None),
                               tag=(f"{tag}_qmm{g}_{qp}_{tc}a" if ft == ND - 1 else None))
                            op("tensor", lambda e, tc=tc, ft=ft, pq1=pq1: e.matmul(
                                pq1[:, :], aB(ft * 512 + tc * 128, [1, 128]),
                                ws(8192 + ft * 512, [1, 512]),
                                start=(ft == 0), stop=(ft == ND - 1)),
                               inc=(("sT", 1) if ft == ND - 1 else None),
                               tag=(f"{tag}_qmm{g}_{qp}_{tc}" if ft == ND - 1 else None))
                        wait("vector", f"{tag}_qmm{g}_{qp}_{tc}a")
                        wait("vector", f"{tag}_qmm{g}_{qp}_{tc}")
                        for half, pq in ((0, pq0), (1, pq1)):
                            cosrep = ap(tabc, t * 128, [[2048, 128], [0, 4], [1, 128]])
                            sinrep4 = ap(tabs, t * 128,
                                         [[2048, 128], [0, 4], [2, 64], [1, 2]])
                            bar("vector", lambda e, pq=pq, cosrep=cosrep: e.tensor_tensor(
                                aF(8192, [128, 4], [1, 128]),
                                ap(pq, 0, [[512, 128], [128, 4], [1, 128]]),
                                cosrep, ALU.mult))
                            qsw = ap(pq, 1, [[512, 128], [128, 4], [2, 64], [-1, 2]])
                            bar("vector", lambda e, qsw=qsw, sinrep4=sinrep4: e.tensor_tensor(
                                aF(8704, [128, 4], [2, 64], [1, 2]), qsw, sinrep4,
                                ALU.mult))
                            tg = f"{tag}_rope{idx}" if half == 1 else None
                            op("vector", lambda e, tc=tc, qp=qp, half=half: e.tensor_tensor(
                                aB(8192 + tc * 2048 + qp * 1024 + half * 512, [1, 512]),
                                aF(8192, [1, 512]), aF(8704, [1, 512]), ALU.add),
                               inc=("sV", 1), tag=(tg if tg else f"{tag}_ropeh{idx}_{half}"))
                for tc in range(4):
                    t = 4 * g + tc
                    idx3 = (g * 2 + 1) * 4 + tc
                    wait("tensor", f"{tag}_rope{idx3}")
                    for ft in range(ND):
                        pdst = pt0 if ft % 2 == 0 else pt1
                        if ft >= 2:
                            wait("tensor", f"{tag}_qte{t}_{ft - 2}")
                        elif tc >= 1:
                            wait("tensor", f"{tag}_qte{t - 1}_{ND - 2 + ft}")
                        op("tensor", lambda e, tc=tc, ft=ft, pdst=pdst:
                           e.transpose(
                               pdst[:, 0:128],
                               aB(8192 + tc * 2048 + ft * 128, [1, 128]),
                               ident[:, :]), inc=("sT", 1),
                           tag=f"{tag}_qtt{t}_{ft}")
                        wait("vector", f"{tag}_qtt{t}_{ft}")
                        if ft == 0 and tc == 0 and g >= 1:
                            wait("vector", f"{tag}_wo_done{g - 1}")
                        op("vector", lambda e, tc=tc, ft=ft, pdst=pdst:
                           e.tensor_copy(
                               ws(ft * 512 + tc * 128, [1, 128]),
                               pdst[:, 0:128]),
                           inc=("sV", 1), tag=f"{tag}_qte{t}_{ft}")
                for tc in range(4):
                    wait("tensor", f"{tag}_qte{4 * g + tc}_{ND - 1}")
                for h in range(H):
                    hk = h // 4
                    for tc in range(4):
                        psc = pb[tc]
                        if h >= 1:
                            wait("tensor", f"{tag}_att{g}_{h - 1}_{tc}")
                        elif g >= 1:
                            wait("tensor", f"{tag}_att{g - 1}_{H - 1}_{tc}")
                        op("tensor", lambda e, h=h, tc=tc, psc=psc, hk=hk: e.matmul(
                            psc[:, :], ws(h * 512 + tc * 128, [1, 128]),
                            ap(ckv, hk * 512, [[4096, 128], [1, 512]]),
                            start=True, stop=False))
                        op("tensor", lambda e, psc=psc: e.matmul(
                            psc[:, :], ap(ones1, 0, [[1024, 1], [1, 128]]),
                            ap(ones1, 512, [[1024, 1], [1, 512]]),
                            start=False, stop=True),
                           inc=("sT", 1), tag=f"{tag}_smm{g}_{h}_{tc}")
                        wait("vector", f"{tag}_smm{g}_{h}_{tc}")
                        idx2 = (g * H + h) * 4 + tc
                        p2 = idx2 % 2
                        sm0 = 10 + p2 * 2
                        if idx2 >= 2:
                            pp = idx2 - 2
                            wait("vector", "%s_exp%d_%d_%d" % (
                                tag, pp // (4 * H), (pp // 4) % H, pp % 4))
                        bar("vector", lambda e, psc=psc, sm0=sm0: e.tensor_reduce(
                            small[:, sm0:sm0 + 1], ap(psc, 0, [[512, 128], [1, 512]]),
                            AX.X, ALU.max))
                        bar("vector", lambda e, sm0=sm0: e.tensor_scalar(
                            small[:, sm0:sm0 + 1], small[:, sm0:sm0 + 1],
                            -ISC, None, ALU.mult))
                        op("vector", lambda e: e.engine_nop(), inc=("sV", 1),
                           tag=f"{tag}_negm{g}_{h}_{tc}")
                        wait("scalar", f"{tag}_negm{g}_{h}_{tc}")
                        if idx2 >= 2:
                            pp = idx2 - 2
                            wait("scalar", "%s_att%d_%d_%d" % (
                                tag, pp // (4 * H), (pp // 4) % H, pp % 4))
                        op("scalar", lambda e, psc=psc, p2=p2, sm0=sm0: e.activation(
                            aF(8192 + p2 * 512, [1, 512]), psc[:, :], AF.Exp,
                            bias=small[:, sm0:sm0 + 1], scale=ISC),
                           inc=("sS", 1), tag=f"{tag}_exp{g}_{h}_{tc}")
                        wait("vector", f"{tag}_exp{g}_{h}_{tc}")
                        bar("vector", lambda e, p2=p2, sm0=sm0: e.tensor_reduce(
                            small[:, sm0 + 1:sm0 + 2], aF(8192 + p2 * 512, [1, 512]),
                            AX.X, ALU.add))
                        bar("vector", lambda e, sm0=sm0: e.reciprocal(
                            small[:, sm0 + 1:sm0 + 2], small[:, sm0 + 1:sm0 + 2]))
                        op("vector", lambda e, tc=tc, p2=p2, sm0=sm0: e.tensor_scalar(
                            aB(8192 + tc * 512, [1, 512]), aF(8192 + p2 * 512, [1, 512]),
                            small[:, sm0 + 1:sm0 + 2], None, ALU.mult),
                           inc=("sV", 1), tag=f"{tag}_att{g}_{h}_{tc}")
                    for tc in range(4):
                        wait("tensor", f"{tag}_att{g}_{h}_{tc}")
                        ptr = pt0 if tc % 2 == 0 else pt1
                        if tc >= 2:
                            wait("tensor", f"{tag}_atte{g}_{h}_{tc - 2}")
                        elif h >= 1:
                            wait("tensor", f"{tag}_atte{g}_{h - 1}_{tc + 2}")
                        for blt in range(4):
                            op("tensor", lambda e, tc=tc, blt=blt, ptr=ptr: e.transpose(
                                ptr[:, blt * 128:(blt + 1) * 128],
                                aB(8192 + tc * 512 + blt * 128, [1, 128]),
                                ident[:, :]),
                               inc=(("sT", 1) if blt == 3 else None),
                               tag=(f"{tag}_attt{g}_{h}_{tc}" if blt == 3 else None))
                        wait("vector", f"{tag}_attt{g}_{h}_{tc}")
                        op("vector", lambda e, tc=tc, ptr=ptr: e.tensor_copy(
                            ap(arenaB, 10240 + tc * 128,
                               [[16384, 128], [512, 4], [1, 128]]),
                            ap(ptr, 0, [[1024, 128], [128, 4], [1, 128]])),
                           inc=("sV", 1), tag=f"{tag}_atte{g}_{h}_{tc}")
                    for tc in range(4):
                        wait("tensor", f"{tag}_atte{g}_{h}_{tc}")
                    pa = pb[4] if h % 2 == 0 else pb[5]
                    if h >= 2:
                        wait("tensor", f"{tag}_oev{g}_{h - 2}")
                    elif g >= 1:
                        wait("tensor", f"{tag}_oev{g - 1}_{H - 2 + h}")
                    for blt in range(4):
                        op("tensor", lambda e, blt=blt, hk=hk, pa=pa: e.matmul(
                            pa[:, :],
                            ap(ckv, 2048 + hk * 512 + blt * 128,
                               [[4096, 128], [1, 128]]),
                            ap(arenaB, 10240 + blt * 512, [[16384, 128], [1, 512]]),
                            start=(blt == 0), stop=(blt == 3)),
                           inc=(("sT", 1) if blt == 3 else None),
                           tag=(f"{tag}_avmm{g}_{h}" if blt == 3 else None))
                    wait("vector", f"{tag}_avmm{g}_{h}")
                    op("vector", lambda e, h=h, pa=pa: e.tensor_copy(
                        kv(8192 + h * 512, [1, 512]), pa[:, :]),
                       inc=("sV", 1), tag=f"{tag}_oev{g}_{h}")
                for op2 in range(2):
                    for half in range(2):
                        oc = 2 * op2 + half
                        woff = half * 8192
                        wait("gpsimd", f"{tag}_smm{g}_{H - 1}_3")
                        if op2 == 1:
                            wait("gpsimd", f"{tag}_womm{g}_0_3")
                            wait("gpsimd", f"{tag}_womm{g}_0_3a")
                        op("gpsimd", lambda e, oc=oc, woff=woff: e.dma_start(
                            out=ws(woff, [512, ND], [1, 512]),
                            in_=ap(wo_h, oc * 512,
                                   [[D, 128], [D * 128, ND], [1, 512]])),
                           inc=("sWA", 16), tag=f"{tag}_wo{g}_{oc}")
                    for tc in range(4):
                        t = 4 * g + tc
                        po0 = pb[0] if tc % 2 == 0 else pb[2]
                        po1 = pb[1] if tc % 2 == 0 else pb[3]
                        wait("tensor", f"{tag}_wo{g}_{2 * op2}")
                        wait("tensor", f"{tag}_wo{g}_{2 * op2 + 1}")
                        wait("tensor", f"{tag}_oev{g}_{H - 1}")
                        idx = op2 * 4 + tc
                        if idx >= 2:
                            pidx = idx - 2
                            wait("tensor", f"{tag}_woev{g}_{pidx // 4}_{pidx % 4}")
                        for ft in range(ND):
                            op("tensor", lambda e, tc=tc, ft=ft, po0=po0: e.matmul(
                                po0[:, :], kv(8192 + ft * 512 + tc * 128, [1, 128]),
                                ws(0 + ft * 512, [1, 512]),
                                start=(ft == 0), stop=(ft == ND - 1)),
                               inc=(("sT", 1) if ft == ND - 1 else None),
                               tag=(f"{tag}_womm{g}_{op2}_{tc}a" if ft == ND - 1 else None))
                            op("tensor", lambda e, tc=tc, ft=ft, po1=po1: e.matmul(
                                po1[:, :], kv(8192 + ft * 512 + tc * 128, [1, 128]),
                                ws(8192 + ft * 512, [1, 512]),
                                start=(ft == 0), stop=(ft == ND - 1)),
                               inc=(("sT", 1) if ft == ND - 1 else None),
                               tag=(f"{tag}_womm{g}_{op2}_{tc}" if ft == ND - 1 else None))
                        wait("vector", f"{tag}_womm{g}_{op2}_{tc}a")
                        wait("vector", f"{tag}_womm{g}_{op2}_{tc}")
                        xoff = tc * 2048
                        for half, po in ((0, po0), (1, po1)):
                            oc = 2 * op2 + half
                            op("vector", lambda e, po=po, xoff=xoff, oc=oc:
                               e.scalar_tensor_tensor(
                                   aF(xoff + oc * 512, [1, 512]), po[:, :], 1.0,
                                   aF(xoff + oc * 512, [1, 512]), ALU.mult, ALU.add),
                               inc=("sV", 1),
                               tag=f"{tag}_woev{g}_{op2}_{tc}" if half == 1 else None)
                for tc in range(4):
                    t = 4 * g + tc
                    wait("gpsimd", f"{tag}_woev{g}_1_{tc}")
                    op("gpsimd", lambda e, t=t, tc=tc: e.dma_start(
                        out=ap(x_att, t * 128 * D, [[D, 128], [1, D]]),
                        in_=aF(tc * 2048, [1, D])),
                       inc=("sO", 16), tag=f"{tag}_xst{t}")
                op("vector", lambda e: e.engine_nop(), inc=("sV", 1),
                   tag=f"{tag}_wo_done{g}")
            op("vector", lambda e: e.engine_nop(), inc=("sV", 1), tag="at_done")

        def dbg_readback(E):
            op, wait = E.op, E.wait
            end_tag = "f2_dst15" if PH["ffn2"] else "cp3_s15"
            wait("sync", end_tag)
            for ci, (srch, cbase) in enumerate(
                    ((qg_in, 0), (qu_in, 16), (qd_in, 32))):
                for c in range(16):
                    gc = cbase + c
                    par = gc % 2
                    if gc >= 2:
                        wait("sync", f"dbgv{gc - 2}")
                    op("sync", lambda e, c=c, par=par, srch=srch: e.dma_start(
                        out=a8(par * 1024, [1, 1024]),
                        in_=ap(srch, c * 131072, [[1024, 128], [1, 1024]])),
                       inc=("sWL", 16), tag=f"dbgl{gc}")
                    wait("vector", f"dbgl{gc}")
                    if gc >= 2:
                        wait("vector", f"dbgs{gc - 2}")
                    op("vector", lambda e, par=par: e.tensor_copy(
                        aB(par * 1024, [1, 1024]), a8(par * 1024, [1, 1024])),
                       inc=("sV", 1), tag=f"dbgv{gc}")
                    wait("gpsimd", f"dbgv{gc}")
                    op("gpsimd", lambda e, gc=gc, par=par: e.dma_start(
                        out=ap(qdbg, gc * 131072, [[1024, 128], [1, 1024]]),
                        in_=aB(par * 1024, [1, 1024])),
                       inc=("sO", 16), tag=f"dbgs{gc}")
            wait("vector", "dbgs47")
            op("vector", lambda e: e.engine_nop(), inc=("sV", 1), tag="dbg_done")

        env.start("dry")
        program(env)
        with nc.Block() as blk:
            for engname, dec in (("sync", blk.sync), ("vector", blk.vector),
                                 ("scalar", blk.scalar), ("tensor", blk.tensor),
                                 ("gpsimd", blk.gpsimd)):
                def mk(name):
                    def f(e):
                        env.start("emit", name, e)
                        program(env)
                    return f
                dec(mk(engname))

    return nc


_NC_CACHE = {}


def kernel(L0_attn_x, L0_attn_wq, L0_attn_wk, L0_attn_wv, L0_attn_wo,
           L0_attn_wiq, L0_attn_wik, L0_attn_cwa, L0_attn_cwb,
           L0_ffn_w_gate, L0_ffn_w_up, L0_ffn_w_down):
    x = np.ascontiguousarray(np.asarray(L0_attn_x, dtype=np.float32)
                             ).reshape(2 * 8192, D)
    wq = np.ascontiguousarray(np.asarray(L0_attn_wq, dtype=np.float32))
    wk = np.ascontiguousarray(np.asarray(L0_attn_wk, dtype=np.float32))
    wv = np.ascontiguousarray(np.asarray(L0_attn_wv, dtype=np.float32))
    wo = np.ascontiguousarray(np.asarray(L0_attn_wo, dtype=np.float32))
    wiq = np.ascontiguousarray(np.asarray(L0_attn_wiq, dtype=np.float32))
    wik = np.ascontiguousarray(np.asarray(L0_attn_wik, dtype=np.float32))
    cw = np.ascontiguousarray(
        np.stack([np.asarray(L0_attn_cwa, dtype=np.float32),
                  np.asarray(L0_attn_cwb, dtype=np.float32)], axis=0))
    wg = np.ascontiguousarray(np.asarray(L0_ffn_w_gate, dtype=np.float32))
    wu = np.ascontiguousarray(np.asarray(L0_ffn_w_up, dtype=np.float32))
    wd = np.ascontiguousarray(np.asarray(L0_ffn_w_down, dtype=np.float32))

    if "nc" not in _NC_CACHE:
        _NC_CACHE["nc"] = build_program()
    nc = _NC_CACHE["nc"]

    in_maps = []
    for c in range(8):
        slot = c % 4
        in_maps.append({
            "x": np.ascontiguousarray(x[c * TC:(c + 1) * TC]),
            "wq": wq, "wk": wk, "wv": wv, "wo": wo,
            "wiq": wiq, "wik": wik, "cw": cw,
            "wg_shard": np.ascontiguousarray(wg[c * (II // 8):(c + 1) * (II // 8)]),
            "wu_shard": np.ascontiguousarray(wu[c * (II // 8):(c + 1) * (II // 8)]),
            "wd_shard": np.ascontiguousarray(wd[:, c * (II // 8):(c + 1) * (II // 8)]),
            "slot": np.array([[1.0 if slot == 0 else 0.0, float(slot), 0.0, 0.0]],
                             dtype=np.float32),
        })
    res = run_bass_kernel_spmd(nc, in_maps, core_ids=list(range(8)))
    _LAST_RES[0] = res
    out = np.concatenate([np.asarray(r["out"]) for r in res.results], axis=0)
    return out.reshape(2, 8192, D)
